# version 13
# baseline (speedup 1.0000x reference)
import numpy as np
from contextlib import ExitStack

import concourse.bass as bass
import concourse.mybir as mybir
from concourse.bass_utils import run_bass_kernel_spmd

F32 = mybir.dt.float32
BF16 = mybir.dt.bfloat16
AF = mybir.ActivationFunctionType
ALU = mybir.AluOpType
AX = mybir.AxisListType

T = 4096
D = 1024
NT = T // 128
LN_EPS = 1e-5


def AP(t, off, pat):
    return bass.AP(t, off, pat)


class Sem:
    _next = [0]

    def __init__(self, h):
        self.h = h
        self.n = 0
        Sem._next[0] += 1
        self.uid = Sem._next[0]

    def inc(self, k=1):
        self.n += k
        return self.n

    @property
    def tok(self):
        return (self, self.n)


class Prog:
    QMAP = {"t": "tensor", "v": "vector", "s": "scalar", "g": "gpsimd", "y": "sync"}

    def __init__(self, nc, es):
        self.nc = nc
        self.es = es
        self.es0 = es
        self.q = {k: [] for k in self.QMAP}
        self._waited = {k: {} for k in self.QMAP}
        self._sems = []
        self._sempool = {}
        self.last_inc = {k: True for k in self.QMAP}
        self.ninstr = 0
        self._uid = 0

    def un(self, base):
        self._uid += 1
        return f"{base}_{self._uid}"

    def sem(self, name):
        s = self._sempool.get(name)
        if s is None:
            s = Sem(self.es0.enter_context(self.nc.semaphore(name)))
            self._sempool[name] = s
            self._sems.append(s)
        return s

    def all_toks(self):
        return [s.tok for s in self._sems]

    def sb(self, name, shape, dt=F32, es=None):
        return (es or self.es).enter_context(
            self.nc.sbuf_tensor(self.un(name), shape, dt))

    def barrier(self, toks, strict=False):
        if strict:
            bad = [qn for qn in self.QMAP if self.q[qn] and not self.last_inc[qn]]
            if bad:
                raise RuntimeError(f"barrier: queues end without inc: {bad}")
        for qn in self.QMAP:
            self.wait(qn, *toks)

    def ps(self, name, shape, dt=F32):
        return self.es.enter_context(self.nc.psum_tensor(self.un(name), shape, dt))

    def dram(self, name, shape, dt=F32, kind="Internal"):
        return self.nc.dram_tensor(name, shape, dt, kind=kind)

    def wait(self, qn, *tokens):
        for tk in tokens:
            if tk is None:
                continue
            sem, val = tk
            if val <= 0:
                continue
            w = self._waited[qn]
            if w.get(sem.uid, -1) >= val:
                continue
            w[sem.uid] = val
            self.ninstr += 1
            self.q[qn].append(lambda e, h=sem.h, v=val: e.wait_ge(h, v))

    def op(self, qn, fn, inc=None, k=1):
        self.ninstr += 1
        self.last_inc[qn] = inc is not None
        if inc is None:
            self.q[qn].append(fn)
            return None
        inc.inc(k)
        self.q[qn].append(lambda e, f=fn, h=inc.h, kk=k: f(e).then_inc(h, kk))
        return inc.tok

    def dma(self, qn, out, in_, sem, **kw):
        self.ninstr += 1
        self.last_inc[qn] = True
        sem.inc(16)
        self.q[qn].append(
            lambda e, o=out, i=in_, h=sem.h, kw=kw: e.dma_start(o, i, **kw).then_inc(h, 16))
        return sem.tok

    def build(self):
        blk = self.es.enter_context(self.nc.Block())
        for qn, attr in self.QMAP.items():
            ops = self.q[qn]
            if not ops:
                continue

            def make(ops):
                def body(eng):
                    for f in ops:
                        f(eng)
                return body
            getattr(blk, attr)(make(ops))



class Res:

    def __init__(self, p: Prog, resident_x=False, ntiles=NT):
        self.p = p
        self.bnst = p.sb("bnst", [128, 12])
        self.mv = p.sb("mv", [128, 2])
        self.t0 = p.sb("lnt0", [128, 1])
        self.rstd = p.sb("rstd", [128, 4])
        self.lnb = p.sb("lnb", [128, 4])
        self.xln_tm = p.sb("xln_tm", [128, 1024], BF16)
        self.idn = p.sb("idn", [128, 128], BF16)
        self.idf = p.sb("idf", [128, 128])
        self.ones_row = p.sb("ones_row", [1, 128], BF16)
        self.ones_f = p.sb("ones_f", [1, 128])
        self.PS0123 = p.ps("PS0123", [128, 2048])
        self.PS01 = self.PS0123[:, 0:1024]
        self.PS23 = self.PS0123[:, 1024:2048]
        self.PS45 = p.ps("PS45", [128, 1024])
        self.PT = p.ps("PT", [128, 1024], BF16)
        self.PTF = p.ps("PTF", [128, 512])
        self.pt_free = None
        self.ptf_free = None
        self.sv = p.sem("sv")
        self.ss = p.sem("ss")
        self.st = p.sem("st")
        self.sg = p.sem("sg")
        self.dmac = p.sem("dmac")
        self.dmag = p.sem("dmag")
        self.consts_tok = None

    def load_consts(self, p, iden_ap):
        p.dma("y", self.idn[:, :], iden_ap, self.dmac)
        self.consts_tok = self.dmac.tok
        p.wait("s", self.consts_tok)
        p.op("s", lambda e: e.copy(self.idf[:, :], self.idn[:, :]), inc=self.ss)
        p.op("g", lambda e: e.memset(self.ones_row[:, :], 1.0), inc=self.sg)
        p.op("g", lambda e: e.memset(self.ones_f[:, :], 1.0), inc=self.sg)
        self.ones_tok = self.sg.tok
        self.idf_tok = self.ss.tok


def emit_ln(p: Prog, r: Res, x_tile_ap, sub, dst_fm, dst_off, after=(),
            scale_mode=None, war=(), cs=512):
    sv, ss, st = r.sv, r.ss, r.st
    p.wait("v", *[a for a in after if a])
    p.op("v", lambda e: e.bn_stats(r.bnst[:, 0:6], x_tile_ap[:, 0:512]))
    t1 = p.op("v", lambda e: e.bn_stats(r.bnst[:, 6:12], x_tile_ap[:, 512:1024]), inc=sv)
    p.wait("v", t1)
    t2 = p.op("v", lambda e: e.bn_aggr(r.mv[:, 0:2], r.bnst[:, 0:12]), inc=sv)
    p.wait("v", t2)
    t3 = p.op("v", lambda e: e.tensor_scalar(r.t0[:, 0:1], r.mv[:, 1:2], LN_EPS, None,
                                             op0=ALU.add), inc=sv)
    p.wait("s", t3)
    t4 = p.op("s", lambda e: e.activation(r.t0[:, 0:1], r.t0[:, 0:1], AF.Ln), inc=ss)
    p.wait("s", t4)
    t5 = p.op("s", lambda e: e.activation(r.rstd[:, sub:sub + 1], r.t0[:, 0:1], AF.Exp,
                                          scale=-0.5), inc=ss)
    if scale_mode == "rms2":
        EPS32 = float(np.finfo(np.float32).eps)
        p.wait("v", t5)
        t5a = p.op("v", lambda e: e.scalar_tensor_tensor(
            r.t0[:, 0:1], r.mv[:, 1:2], 1.0, r.rstd[:, sub:sub + 1],
            op0=ALU.mult, op1=ALU.mult), inc=sv)
        p.wait("v", t5a)
        t5b = p.op("v", lambda e: e.scalar_tensor_tensor(
            r.t0[:, 0:1], r.t0[:, 0:1], EPS32, r.rstd[:, sub:sub + 1],
            op0=ALU.bypass, op1=ALU.mult) if False else e.tensor_tensor(
            r.t0[:, 0:1], r.t0[:, 0:1], r.rstd[:, sub:sub + 1], op=ALU.mult),
            inc=sv)
        p.wait("v", t5b)
        t5c = p.op("v", lambda e: e.tensor_scalar(r.t0[:, 0:1], r.t0[:, 0:1], EPS32, None,
                                                  op0=ALU.add), inc=sv)
        p.wait("s", t5c)
        t5d = p.op("s", lambda e: e.activation(r.t0[:, 0:1], r.t0[:, 0:1], AF.Ln), inc=ss)
        p.wait("s", t5d)
        t5e = p.op("s", lambda e: e.activation(r.t0[:, 0:1], r.t0[:, 0:1], AF.Exp,
                                               scale=-0.5), inc=ss)
        p.wait("v", t5e)
        t5 = p.op("v", lambda e: e.tensor_tensor(
            r.rstd[:, sub:sub + 1], r.rstd[:, sub:sub + 1], r.t0[:, 0:1],
            op=ALU.mult), inc=sv)
    p.wait("v", t5)
    t6 = p.op("v", lambda e: e.scalar_tensor_tensor(
        r.lnb[:, sub:sub + 1], r.mv[:, 0:1], -1.0, r.rstd[:, sub:sub + 1],
        op0=ALU.mult, op1=ALU.mult), inc=sv)
    p.wait("s", t6)
    t7 = p.op("s", lambda e: e.activation(r.xln_tm[:, :], x_tile_ap[:, 0:1024], AF.Identity,
                                          bias=r.lnb[:, sub:sub + 1],
                                          scale=r.rstd[:, sub:sub + 1]), inc=ss)
    p.wait("t", t7, r.pt_free, r.consts_tok)
    tk = None
    for dc in range(8):
        tk = p.op("t", lambda e, dc=dc: e.transpose(
            r.PT[:, dc * 128:(dc + 1) * 128], r.xln_tm[:, dc * 128:(dc + 1) * 128],
            r.idn[:, :]), inc=(st if dc == 7 else None))
    p.wait("v", tk, *[w for w in war if w])
    dst = AP(dst_fm, dst_off, [[dst_fm.shape[1], 128], [cs, 8], [1, 128]])
    cp = p.op("v", lambda e, d=dst: e.tensor_copy(d, r.PT[:, 0:1024]), inc=sv)
    r.pt_free = cp
    return cp



class XResident:
    def __init__(self, p, r, x_ap=None, y_ap=None, off=0, ntiles=16):
        self.r = r
        self._ready = None
        self.x_ap = x_ap
        self.y_ap = y_ap
        self.off = off
        self.X = p.sb("Xres", [128, ntiles * 1024], F32)
        self.addtok = {}

    def load_all(self, p, r):
        self.lsem = p.sem("xrld")
        nt = self.X.shape[1] // 1024
        for h in range(0, nt, 8):
            n = min(8, nt - h)
            tk = p.dma("y", self.X[:, h * 1024:(h + n) * 1024],
                       AP(self.x_ap, self.off + h * 128 * 1024,
                          [[1024, 128], [128 * 1024, n], [1, 1024]]),
                       self.lsem)
        self._ready = tk

    def store_all(self, p, r, waits=()):
        self.ssem = p.sem("xrst")
        nt = self.X.shape[1] // 1024
        p.wait("y", *waits)
        for h in range(0, nt, 8):
            n = min(8, nt - h)
            tk = p.dma("y", AP(self.y_ap, self.off + h * 128 * 1024,
                               [[1024, 128], [128 * 1024, n], [1, 1024]]),
                       self.X[:, h * 1024:(h + n) * 1024], self.ssem)
        return tk

    def final_toks(self):
        return [self.ssem.tok]

    def prefetch(self, p, r, i):
        pass

    def ready(self, i):
        return self._ready

    def tile(self, p, r, ti):
        return self.X[:, ti * 1024:(ti + 1) * 1024]

    def tile4(self, p, r, i):
        return [self.tile(p, r, 4 * i + s) for s in range(4)]

    def add(self, p, r, ti, psum_ap, wait=(), scale_col=None):
        xt = self.X[:, ti * 1024:(ti + 1) * 1024]
        p.wait("v", *wait, self.addtok.get(ti))
        sc = scale_col if scale_col is not None else 1.0
        tk = p.op("v", lambda e, xt=xt, ps=psum_ap, sc=sc: e.scalar_tensor_tensor(
            xt, ps, sc, xt, op0=ALU.mult, op1=ALU.add), inc=r.sv)
        self.addtok[ti] = tk
        return tk

    def add_half(self, p, r, ti, half, psum_ap, wait=(), scale_col=None):
        xt = self.X[:, ti * 1024 + half * 512:ti * 1024 + (half + 1) * 512]
        p.wait("v", *wait, self.addtok.get((ti, half)))
        sc = scale_col if scale_col is not None else 1.0
        tk = p.op("v", lambda e, xt=xt, ps=psum_ap, sc=sc: e.scalar_tensor_tensor(
            xt, ps, sc, xt, op0=ALU.mult, op1=ALU.add), inc=r.sv)
        self.addtok[(ti, half)] = tk
        return tk

    def flush(self, p, r, i):
        pass

    def loadA(self, p, r, ti):
        return self._ready

    def tileA(self, p, r, ti):
        return self.tile(p, r, ti)

    def consumedA(self, p, r, ti, tok):
        pass

    def new_pass(self):
        pass


class XStream:

    def __init__(self, p, r, x_handle, y_handle, nbuf=2, ngroups=8):
        self.xh = x_handle
        self.yh = y_handle
        self.nbuf = nbuf
        self.ng = ngroups
        self.xb = [p.sb("xs", [128, 4 * 1024]) for i in range(nbuf)]
        self.lsem = [p.sem(f"xld{i}") for i in range(nbuf)]
        self.ssem = [p.sem(f"xst{i}") for i in range(nbuf)]
        self.loaded = {}
        self.stores = {}
        self.addtok = {}

    def _dram_ap(self, h, i):
        return AP(h, (4 * i) * 128 * 1024, [[1024, 128], [128 * 1024, 4], [1, 1024]])

    def prefetch(self, p, r, i):
        if i in self.loaded or i >= self.ng:
            return
        buf = i % self.nbuf
        p.wait("y", self.stores.get(i - self.nbuf))
        self.loaded[i] = p.dma("y", self.xb[buf][:, :], self._dram_ap(self.xh, i),
                               self.lsem[buf])

    def ready(self, i):
        return self.loaded.get(i)

    def tile(self, p, r, ti):
        i, sub = divmod(ti, 4)
        return self.xb[i % self.nbuf][:, sub * 1024:(sub + 1) * 1024]

    def tile4(self, p, r, i):
        return [self.tile(p, r, 4 * i + s) for s in range(4)]

    def add(self, p, r, ti, psum_ap, wait=(), scale_col=None):
        i, sub = divmod(ti, 4)
        dst = self.tile(p, r, ti)
        p.wait("v", *wait, self.addtok.get(ti), self.loaded.get(i))
        sc = scale_col if scale_col is not None else 1.0
        tk = p.op("v", lambda e, d=dst, ps=psum_ap, sc=sc: e.scalar_tensor_tensor(
            d, ps, sc, d, op0=ALU.mult, op1=ALU.add), inc=r.sv)
        self.addtok[ti] = tk
        return tk

    def flush(self, p, r, i):
        buf = i % self.nbuf
        p.wait("y", *[self.addtok.get(i * 4 + s) for s in range(4)])
        self.stores[i] = p.dma("y", self._dram_ap(self.yh, i), self.xb[buf][:, :],
                               self.ssem[buf])

    def final_toks(self):
        return [sm.tok for sm in self.ssem]




class XPhased:

    def __init__(self, p, r, x_handle, y_handle):
        self.xh = x_handle
        self.yh = y_handle
        self.ta = [p.sb("xpa", [128, 1024]) for _ in range(2)]
        self.tasem = [p.sem(f"xpas{i}") for i in range(2)]
        self.tacons = {}
        self.taload = {}
        self.tb = [p.sb("xpb", [128, 4 * 1024]) for _ in range(1)]
        self.tbsem = [p.sem(f"xpbs{i}") for i in range(1)]
        self.tbssem = [p.sem(f"xpbo{i}") for i in range(1)]
        self.tbload = {}
        self.stores = {}
        self.prev_stores = {}
        self.pass_finals = [None]
        self.addtok = {}

    def new_pass(self):
        self.tbload = {}
        self.prev_stores = self.stores
        self.pass_finals = [sm.tok for sm in self.tbssem]
        self.stores = {}
        self.addtok = {}
        self.xh = self.yh

    def loadA(self, p, r, ti):
        if ti in self.taload or ti >= NT:
            return self.taload.get(ti)
        bi = ti % 2
        p.wait("y", self.tacons.get(ti - 2))
        self.taload[ti] = p.dma("y", self.ta[bi][:, :],
                                AP(self.xh, ti * 128 * 1024, [[1024, 128], [1, 1024]]),
                                self.tasem[bi])
        return self.taload[ti]

    def tileA(self, p, r, ti):
        return self.ta[ti % 2]

    def consumedA(self, p, r, ti, tok):
        self.tacons[ti] = tok

    def _loadB(self, p, r, jq):
        if jq in self.tbload:
            return self.tbload[jq]
        bi = 0
        p.wait("y", self.stores.get(jq - 1), self.prev_stores.get(jq),
               self.pass_finals[bi])
        self.tbload[jq] = p.dma(
            "y", self.tb[bi][:, :],
            AP(self.xh, jq * 4 * 128 * 1024, [[1024, 128], [128 * 1024, 4], [1, 1024]]),
            self.tbsem[bi])
        return self.tbload[jq]

    def add(self, p, r, ti, psum_ap, wait=(), scale_col=None):
        jq, s4 = divmod(ti, 4)
        ld = self._loadB(p, r, jq)
        bi = 0
        dst = self.tb[bi][:, s4 * 1024:(s4 + 1) * 1024]
        p.wait("v", *wait, ld, self.addtok.get(ti))
        sc = scale_col if scale_col is not None else 1.0
        tk = p.op("v", lambda e, d=dst, ps=psum_ap, sc=sc: e.scalar_tensor_tensor(
            d, ps, sc, d, op0=ALU.mult, op1=ALU.add), inc=r.sv)
        self.addtok[ti] = tk
        if s4 == 3:
            p.wait("y", tk)
            self.stores[jq] = p.dma(
                "y", AP(self.yh, jq * 4 * 128 * 1024,
                        [[1024, 128], [128 * 1024, 4], [1, 1024]]),
                self.tb[bi][:, :], self.tbssem[bi])
        return tk

    def final_toks(self):
        return [s.tok for s in self.tbssem]



def prep_transpose(p: Prog, r: Res, jobs, waits=(), standalone=True,
                   queues=("y",)):
    es2 = ExitStack() if standalone else None
    nq = len(queues)
    tbuf = [p.sb(f"tts{i}", [128, 2048], BF16, es=es2) for i in range(2 * nq)]
    lsem = [p.sem(f"ttl{i}") for i in range(2 * nq)]
    ssem = p.sem("ttst")
    stored = [None] * (2 * nq)
    for q in queues:
        p.wait(q, *waits)
    i = 0
    last = None
    for (src, R, C, dst) in jobs:
        rchunk = 2048
        while R % rchunk:
            rchunk //= 2
        assert rchunk % 128 == 0
        for c0 in range(0, C, 128):
            for r0 in range(0, R, rchunk):
                bi = i % (2 * nq)
                q = queues[i % nq]
                i += 1
                p.wait(q, stored[bi])
                ld = p.dma(q, tbuf[bi][:, 0:rchunk],
                           src.ap()[r0:r0 + rchunk, c0:c0 + 128], lsem[bi],
                           transpose=True)
                p.wait("g", ld)
                stored[bi] = p.dma("g", dst.ap()[c0:c0 + 128, r0:r0 + rchunk],
                                   tbuf[bi][:, 0:rchunk], ssem)
                last = stored[bi]
    if standalone:
        p.barrier(p.all_toks())
        es2.close()
    return last



def prep_l1ffn_alloc(p: Prog):
    return {"b1sb": p.sb("b1sb", [128, 32]),
            "b2row": p.sb("b2row", [1, 1024], BF16)}


def prep_l1ffn(p: Prog, r: Res, ins, scr, pr=None, standalone=True):
    if pr is None:
        pr = prep_l1ffn_alloc(p)
    b1sb, b2row = pr["b1sb"], pr["b2row"]
    es2 = ExitStack() if standalone else None
    wtmp = [p.sb("pw", [128, 1024], es=es2) for _ in range(2)]
    wsg = p.sb("pwsg", [128, 1024], BF16, es=es2)
    wout = p.sb("pwo", [128, 1024], BF16, es=es2)
    asum = p.sb("pasum", [128, 5], es=es2)
    lsem = [p.sem(f"pld{i}") for i in range(2)]
    stsem = p.sem("pst")
    out_free = None
    sg_free = None
    jobs = [("w1", scr["bw1"], 32, 1), ("w2", scr["bw2"], 8, 4)]
    last_store = None
    buf_users = [None, None]
    for wname, scr_t, nchunk, npiece in jobs:
        w_ap = ins[wname]
        for c in range(nchunk):
            red = []
            for pc in range(npiece):
                buf = wtmp[pc % 2]
                p.wait("y", buf_users[pc % 2])
                ld = p.dma("y", buf[:, :],
                           AP(w_ap, c * 128 * (1024 * npiece) + pc * 1024,
                              [[1024 * npiece, 128], [1, 1024]]), lsem[pc % 2])
                p.wait("v", ld)
                red.append(p.op("v", lambda e, b=buf, pc=pc: e.tensor_reduce(
                    asum[:, pc + 1:pc + 2], b[:, :], axis=AX.X, op=ALU.add,
                    apply_absolute_value=True), inc=r.sv))
                buf_users[pc % 2] = red[-1]
            p.wait("v", red[-1])
            if npiece == 1:
                t = p.op("v", lambda e: e.tensor_copy(asum[:, 0:1], asum[:, 1:2]),
                         inc=r.sv)
            else:
                t = p.op("v", lambda e, np_=npiece: e.tensor_reduce(
                    asum[:, 0:1], asum[:, 1:1 + np_], axis=AX.X, op=ALU.add), inc=r.sv)
            for pc in range(npiece):
                buf = wtmp[pc % 2]
                p.wait("y", buf_users[pc % 2])
                ld = p.dma("y", buf[:, :],
                           AP(w_ap, c * 128 * (1024 * npiece) + pc * 1024,
                              [[1024 * npiece, 128], [1, 1024]]), lsem[pc % 2])
                p.wait("s", ld, sg_free)
                t2 = p.op("s", lambda e, b=buf: e.activation(wsg[:, :], b[:, :], AF.Sign),
                          inc=r.ss)
                buf_users[pc % 2] = t2
                p.wait("v", t2, t, out_free)
                t3 = p.op("v", lambda e, w=1024 * npiece: e.tensor_scalar(
                    wout[:, :], wsg[:, :], asum[:, 0:1], 1.0 / w,
                    op0=ALU.mult, op1=ALU.mult), inc=r.sv)
                sg_free = t3
                p.wait("y", t3)
                last_store = p.dma("y", AP(scr_t, c * 128 * (1024 * npiece) + pc * 1024,
                                           [[1024 * npiece, 128], [1, 1024]]),
                                   wout[:, :], stsem)
                out_free = last_store
    b1t = p.sb("pb1t", [32, 128], es=es2)
    bsem = p.sem("pb1")
    ldb = p.dma("y", b1t[:, :], AP(ins["b1"], 0, [[128, 32], [1, 128]]), bsem)
    p.wait("t", ldb, r.idf_tok, r.ptf_free)
    tk = p.op("t", lambda e: e.transpose(r.PTF[:, 0:32], b1t[:, :], r.idf[0:32, 0:32]),
              inc=r.st)
    p.wait("v", tk)
    tk2 = p.op("v", lambda e: e.tensor_copy(b1sb[:, 0:32], r.PTF[:, 0:32]), inc=r.sv)
    r.ptf_free = tk2
    b2tok = p.dma("g", b2row[:, :], AP(ins["b2"], 0, [[1024, 1], [1, 1024]]), r.dmag)
    if standalone:
        p.barrier(p.all_toks())
        es2.close()
    pr.update({"wdone": last_store, "consts": b2tok, "b1tok": tk2})
    return pr


def mk_l1ffn_bufs(p: Prog):
    return {
        "w1s": [p.sb("w1s", [128, 8 * 512], BF16) for _ in range(2)],
        "w2h": p.sb("w2h", [128, 16 * 1024], BF16),
        "abuf": p.sb("abuf", [128, 32 * 512], BF16),
        "xlnf": p.sb("xlnf", [128, 8 * 512], BF16),
    }


def emit_l1ffn(p: Prog, r: Res, bufs, scr, pr, xio):
    sv, ss, st = r.sv, r.ss, r.st
    w1s, w2h, abuf, xlnf = bufs["w1s"], bufs["w2h"], bufs["abuf"], bufs["xlnf"]
    dma1 = [p.sem(f"w1d{i}") for i in range(2)]
    dma2 = p.sem("w2d")
    w1_loaded = [None, None]
    w1_used = [None, None]
    w2_loaded = None
    w2_used = None

    def load_w1_slice(sl):
        nonlocal w1_loaded
        buf = sl % 2
        p.wait("y", w1_used[buf], pr["wdone"])
        tk = None
        for dc in range(8):
            tk = p.dma("y", w1s[buf][:, dc * 512:(dc + 1) * 512],
                       scr["bw1T"].ap()[dc * 128:(dc + 1) * 128,
                                        sl * 512:(sl + 1) * 512], dma1[buf])
        w1_loaded[buf] = tk

    def load_w2_half(h):
        nonlocal w2_loaded
        p.wait("y", w2_used, pr["wdone"])
        tk = None
        for fc in range(16):
            j = h * 16 + fc
            tk = p.dma("y", w2h[:, fc * 1024:(fc + 1) * 1024],
                       scr["bw2T"].ap()[j * 128:(j + 1) * 128, :], dma2)
        w2_loaded = tk

    PH = r.PS45
    PY = [r.PS01, r.PS23]
    ph_free = [None, None]
    py_free = [None, None]

    load_w1_slice(0)
    xio.prefetch(p, r, 0)
    abuf_free = None
    xlnf_free = None
    for i in range(8):
        xio.prefetch(p, r, i + 1)
        xt = xio.tile4(p, r, i)
        lntok = None
        for sub in range(4):
            lntok = emit_ln(p, r, xt[sub], sub, xlnf, sub * 128,
                            after=(xio.ready(i),), war=(xlnf_free,))
        for sl in range(8):
            buf = sl % 2
            if sl < 7 or i < 7:
                load_w1_slice((sl + 1) % 8)
            for j4 in range(4):
                j = sl * 4 + j4
                par = j % 2
                ph = PH[:, par * 512:(par + 1) * 512]
                p.wait("t", w1_loaded[buf], lntok, ph_free[par])
                tk = None
                for dc in range(8):
                    tk = p.op("t", lambda e, b=buf, dc=dc, j4=j4, ph=ph: e.matmul(
                        ph, w1s[b][:, dc * 512 + j4 * 128: dc * 512 + (j4 + 1) * 128],
                        xlnf[:, dc * 512:(dc + 1) * 512],
                        start=(dc == 0), stop=(dc == 7)),
                        inc=(st if dc == 7 else None))
                p.wait("s", tk, abuf_free)
                gk = p.op("s", lambda e, j=j, ph=ph: e.activation(
                    abuf[:, j * 512:(j + 1) * 512], ph, AF.Gelu,
                    bias=pr["b1sb"][:, j:j + 1]), inc=ss)
                ph_free[par] = gk
            w1_used[buf] = st.tok
        xlnf_free = st.tok
        for h in range(2):
            load_w2_half(h)
            for sub in range(4):
                par = sub % 2
                py = PY[par]
                p.wait("t", w2_loaded, ss.tok, py_free[par])
                started = [False, False]
                if h == 0:
                    p.wait("t", pr["consts"], r.ones_tok)
                    for nh in range(2):
                        p.op("t", lambda e, nh=nh, py=py: e.matmul(
                            py[:, nh * 512:(nh + 1) * 512], r.ones_row[0:1, :],
                            pr["b2row"][0:1, nh * 512:(nh + 1) * 512],
                            start=True, stop=False))
                    started = [True, True]
                tk = None
                for fc in range(16):
                    j = h * 16 + fc
                    for nh in range(2):
                        tk = p.op("t", lambda e, fc=fc, j=j, sub=sub, nh=nh, py=py,
                                  st_=(not started[nh] and fc == 0): e.matmul(
                            py[:, nh * 512:(nh + 1) * 512],
                            abuf[:, j * 512 + sub * 128: j * 512 + (sub + 1) * 128],
                            w2h[:, fc * 1024 + nh * 512: fc * 1024 + (nh + 1) * 512],
                            start=st_, stop=(fc == 15)),
                            inc=(st if (fc == 15 and nh == 1) else None))
                py_free[par] = xio.add(p, r, i * 4 + sub, py[:, 0:1024], wait=(tk,))
            w2_used = st.tok
        abuf_free = st.tok
        xio.flush(p, r, i)
    return



def np_consts():
    import ml_dtypes
    return {"c_iden": np.eye(128, dtype=ml_dtypes.bfloat16)}


def build_l1ffn_program():
    nc = bass.Bass("TRN2", target_bir_lowering=False)
    es = ExitStack()
    with es:
        p = Prog(nc, es)
        ins = {
            "x": p.dram("x", [T, D], kind="ExternalInput"),
            "w1": p.dram("w1", [4096, 1024], kind="ExternalInput"),
            "b1": p.dram("b1", [4096], kind="ExternalInput"),
            "w2": p.dram("w2", [1024, 4096], kind="ExternalInput"),
            "b2": p.dram("b2", [1024], kind="ExternalInput"),
            "c_iden": p.dram("c_iden", [128, 128], BF16, kind="ExternalInput"),
        }
        y = p.dram("y", [T, D], kind="ExternalOutput")
        scr = {
            "bw1": p.dram("bw1", [4096, 1024], BF16),
            "bw2": p.dram("bw2", [1024, 4096], BF16),
            "bw1T": p.dram("bw1T", [1024, 4096], BF16),
            "bw2T": p.dram("bw2T", [4096, 1024], BF16),
        }
        r = Res(p, resident_x=False)
        r.load_consts(p, ins["c_iden"].ap())
        pr = prep_l1ffn(p, r, ins, scr)
        prep_transpose(p, r, [(scr["bw1"], 4096, 1024, scr["bw1T"]),
                              (scr["bw2"], 1024, 4096, scr["bw2T"])])
        bufs = mk_l1ffn_bufs(p)
        xio = XStream(p, r, ins["x"], y)
        emit_l1ffn(p, r, bufs, scr, pr, xio)
        p.wait("y", *xio.final_toks())
        p.build()
    return nc, p.ninstr



def prep_moe_alloc(p: Prog):
    return {"multT": p.sb("multT", [128, 15 * 8 + 16]),
            "fc1b": p.sb("fc1b", [128, 15 * 16 + 32]),
            "gwT": p.sb("gwT", [128, 8 * 15], BF16),
            "bmat": p.sb("bmat", [16, 1024], BF16)}


def prep_moe(p: Prog, r: Res, ins, scr, pr=None, standalone=True):
    lsem = [p.sem(f"mld{i}") for i in range(2)]
    stsem = p.sem("mst")
    csem = p.sem("mcs")
    if pr is None:
        pr = prep_moe_alloc(p)
    multT, fc1b, gwT, bmat = pr["multT"], pr["fc1b"], pr["gwT"], pr["bmat"]
    es2 = ExitStack() if standalone else None
    wtmp = [p.sb("mpw", [128, 1024], es=es2) for _ in range(2)]
    wob = p.sb("mpo", [128, 1024], BF16, es=es2)
    mrow = p.sb("mrow", [16, 128], es=es2)
    grow = p.sb("grow", [15, 1024], es=es2)
    buf_users = [None, None]
    out_free = None
    last_store = None
    tposed = None
    for e in range(16):
        if e < 15:
            src = AP(ins["geglu_mult"], e * 1024, [[128, 8], [1, 128]])
            ncol, dst0 = 8, e * 8
        else:
            src = AP(ins["s_geglu_mult"], 0, [[128, 16], [1, 128]])
            ncol, dst0 = 16, 15 * 8
        p.wait("y", tposed)
        ld = p.dma("y", mrow[0:ncol, :], src, csem)
        p.wait("t", ld, r.idf_tok, r.ptf_free)
        tk = p.op("t", lambda e2, n=ncol: e2.transpose(
            r.PTF[:, 0:n], mrow[0:n, :], r.idf[0:n, 0:n]), inc=r.st)
        p.wait("v", tk)
        tposed = p.op("v", lambda e2, d0=dst0, n=ncol: e2.tensor_copy(
            multT[:, d0:d0 + n], r.PTF[:, 0:n]), inc=r.sv)
        r.ptf_free = tposed

    for e in range(15):
        for j in range(8):
            bi = j % 2
            p.wait("y", buf_users[bi])
            ld = p.dma("y", wtmp[bi][:, :],
                       AP(ins["fc1_w"], (e * 2048 + j * 128) * 1024,
                          [[1024, 128], [1, 1024]]), lsem[bi])
            p.wait("v", ld, tposed, out_free)
            t3 = p.op("v", lambda e2, b=bi, e_=e, j_=j: e2.tensor_scalar(
                wob[:, :], wtmp[b][:, :], multT[:, e_ * 8 + j_:e_ * 8 + j_ + 1], None,
                op0=ALU.mult), inc=r.sv)
            buf_users[bi] = t3
            p.wait("y", t3)
            last_store = p.dma("y", AP(scr["fc1"], (e * 2048 + j * 128) * 1024,
                                       [[1024, 128], [1, 1024]]), wob[:, :], stsem)
            out_free = last_store
        p.dma("g", AP(scr["fc1"], (e * 2048 + 1024) * 1024, [[1024, 1024], [1, 1024]]),
              AP(ins["fc1_w"], (e * 2048 + 1024) * 1024, [[1024, 1024], [1, 1024]]),
              r.dmag)
    for j in range(16):
        bi = j % 2
        p.wait("y", buf_users[bi])
        ld = p.dma("y", wtmp[bi][:, :],
                   AP(ins["s_fc1_w"], (j * 128) * 1024, [[1024, 128], [1, 1024]]),
                   lsem[bi])
        p.wait("v", ld, tposed, out_free)
        t3 = p.op("v", lambda e2, b=bi, j_=j: e2.tensor_scalar(
            wob[:, :], wtmp[b][:, :], multT[:, 120 + j_:120 + j_ + 1], None,
            op0=ALU.mult), inc=r.sv)
        buf_users[bi] = t3
        p.wait("y", t3)
        last_store = p.dma("y", AP(scr["s_fc1"], (j * 128) * 1024,
                                   [[1024, 128], [1, 1024]]), wob[:, :], stsem)
        out_free = last_store
    p.dma("g", AP(scr["s_fc1"], 2048 * 1024, [[1024, 2048], [1, 1024]]),
          AP(ins["s_fc1_w"], 2048 * 1024, [[1024, 2048], [1, 1024]]), r.dmag)
    p.dma("g", AP(scr["fc2"], 0, [[1024, 15 * 1024], [1, 1024]]),
          AP(ins["fc2_w"], 0, [[1024, 15 * 1024], [1, 1024]]), r.dmag)
    p.dma("g", AP(scr["s_fc2"], 0, [[2048, 1024], [1, 2048]]),
          AP(ins["s_fc2_w"], 0, [[2048, 1024], [1, 2048]]), r.dmag)

    tp2 = tposed

    def thin_cols(src_ap, dst0, n):
        nonlocal tp2
        p.wait("y", tp2)
        ld = p.dma("y", mrow[0:n, :], src_ap, csem)
        p.wait("t", ld, r.ptf_free)
        tk = p.op("t", lambda e2, n=n: e2.transpose(r.PTF[:, 0:n], mrow[0:n, :],
                                                    r.idf[0:n, 0:n]), inc=r.st)
        p.wait("v", tk)
        tp2 = p.op("v", lambda e2, d0=dst0, n=n: e2.tensor_copy(
            fc1b[:, d0:d0 + n], r.PTF[:, 0:n]), inc=r.sv)
        r.ptf_free = tp2
        return tp2

    for e in range(15):
        thin_cols(AP(ins["fc1_b"], e * 2048, [[128, 16], [1, 128]]), e * 16, 16)
    thin_cols(AP(ins["s_fc1_b"], 0, [[128, 16], [1, 128]]), 240, 16)
    thin_cols(AP(ins["s_fc1_b"], 2048, [[128, 16], [1, 128]]), 256, 16)
    bs = tp2
    for e in range(15):
        p.wait("v", bs)
        bs = p.op("v", lambda e2, e_=e: e2.tensor_tensor(
            fc1b[:, e_ * 16:e_ * 16 + 8], fc1b[:, e_ * 16:e_ * 16 + 8],
            multT[:, e_ * 8:e_ * 8 + 8], op=ALU.mult), inc=r.sv)
    p.wait("v", bs)
    bs = p.op("v", lambda e2: e2.tensor_tensor(
        fc1b[:, 240:256], fc1b[:, 240:256], multT[:, 120:136], op=ALU.mult), inc=r.sv)

    p.wait("y", bs, tp2)
    ldg = p.dma("y", grow[:, :], AP(ins["gate_w"], 0, [[1024, 15], [1, 1024]]), csem)
    gt = None
    for dc in range(8):
        p.wait("t", ldg, r.ptf_free)
        tk = p.op("t", lambda e2, dc=dc: e2.transpose(
            r.PTF[:, 0:15], grow[0:15, dc * 128:(dc + 1) * 128], r.idf[0:15, 0:15]),
            inc=r.st)
        p.wait("v", tk, gt)
        gt = p.op("v", lambda e2, dc=dc: e2.tensor_copy(
            gwT[:, dc * 15:(dc + 1) * 15], r.PTF[:, 0:15]), inc=r.sv)
        r.ptf_free = gt

    p.dma("g", bmat[0:15, :], AP(ins["fc2_b"], 0, [[1024, 15], [1, 1024]]), r.dmag)
    p.dma("g", bmat[15:16, :], AP(ins["s_fc2_b"], 0, [[1024, 1], [1, 1024]]), r.dmag)

    if standalone:
        es2.close()
        p.barrier([r.sv.tok, r.ss.tok, r.st.tok, r.sg.tok, lsem[0].tok,
                   lsem[1].tok, stsem.tok, csem.tok, r.dmag.tok, r.dmac.tok])
    pr.update({"wdone": last_store, "castdone": r.dmag.tok, "biasdone": bs,
               "gatedone": gt})
    return pr


def prep_moe_pool(p: Prog, r: Res, ins, scr, pr, pacer=()):
    multT, fc1b, gwT, bmat = pr["multT"], pr["fc1b"], pr["gwT"], pr["bmat"]
    gsem = p.sem("mpg")
    ldsem = p.sem("mpl")
    stsem = p.sem("mps")
    csem = p.sem("mpc")
    wtmp = [p.sb(f"mpw{i}", [128, 1024]) for i in range(2)]
    wob = [p.sb(f"mpo{i}", [128, 1024], BF16) for i in range(2)]
    TOTAL = 648 + 90 + 136 * 8
    state = {"step": 0, "next": 0}

    def pace(w):
        state["step"] += w
        if pacer and state["step"] >= state["next"]:
            j = min(len(pacer) - 1, state["step"] * len(pacer) // TOTAL)
            p.wait("g", pacer[j])
            state["next"] = (j + 1) * TOTAL // len(pacer)
    for e in range(15):
        pace(5)
        p.dma("g", AP(scr["fc1"], (e * 2048 + 1024) * 1024,
                      [[1024, 1024], [1, 1024]]),
              AP(ins["fc1_w"], (e * 2048 + 1024) * 1024,
                 [[1024, 1024], [1, 1024]]), r.dmag)
    pace(5)
    p.dma("g", AP(scr["s_fc1"], 2048 * 1024, [[1024, 2048], [1, 1024]]),
          AP(ins["s_fc1_w"], 2048 * 1024, [[1024, 2048], [1, 1024]]), r.dmag)
    pace(5)
    p.dma("g", AP(scr["fc2"], 0, [[1024, 15 * 1024], [1, 1024]]),
          AP(ins["fc2_w"], 0, [[1024, 15 * 1024], [1, 1024]]), r.dmag)
    pace(5)
    p.dma("g", AP(scr["s_fc2"], 0, [[2048, 1024], [1, 2048]]),
          AP(ins["s_fc2_w"], 0, [[2048, 1024], [1, 2048]]), r.dmag)
    tk = None
    for e in range(15):
        for j in range(8):
            pace(1)
            tk = p.dma("g", multT[:, e * 8 + j:e * 8 + j + 1],
                       AP(ins["geglu_mult"], e * 1024 + j * 128,
                          [[1, 128], [1, 1]]), csem)
    for j in range(16):
        pace(1)
        tk = p.dma("g", multT[:, 120 + j:121 + j],
                   AP(ins["s_geglu_mult"], j * 128, [[1, 128], [1, 1]]), csem)
    multT_tok = tk
    for e in range(15):
        for j in range(16):
            pace(1)
            f0 = j * 128 if j < 8 else 1024 + (j - 8) * 128
            tk = p.dma("g", fc1b[:, e * 16 + j:e * 16 + j + 1],
                       AP(ins["fc1_b"], e * 2048 + f0, [[1, 128], [1, 1]]), csem)
    for j in range(16):
        pace(1)
        tk = p.dma("g", fc1b[:, 240 + j:241 + j],
                   AP(ins["s_fc1_b"], j * 128, [[1, 128], [1, 1]]), csem)
    for j in range(16):
        pace(1)
        tk = p.dma("g", fc1b[:, 256 + j:257 + j],
                   AP(ins["s_fc1_b"], 2048 + j * 128, [[1, 128], [1, 1]]), csem)
    p.wait("g", tk, multT_tok)
    bs = None
    for e in range(15):
        bs = p.op("g", lambda e2, e_=e: e2.tensor_tensor(
            fc1b[:, e_ * 16:e_ * 16 + 8], fc1b[:, e_ * 16:e_ * 16 + 8],
            multT[:, e_ * 8:e_ * 8 + 8], op=ALU.mult), inc=gsem)
    bs = p.op("g", lambda e2: e2.tensor_tensor(
        fc1b[:, 240:256], fc1b[:, 240:256], multT[:, 120:136], op=ALU.mult),
        inc=gsem)
    gt = None
    for dc in range(8):
        for e in range(15):
            pace(1)
            gt = p.dma("g", gwT[:, dc * 15 + e:dc * 15 + e + 1],
                       AP(ins["gate_w"], e * 1024 + dc * 128, [[1, 128], [1, 1]]),
                       csem)
    p.dma("g", bmat[0:15, :], AP(ins["fc2_b"], 0, [[1024, 15], [1, 1024]]), r.dmag)
    p.dma("g", bmat[15:16, :], AP(ins["s_fc2_b"], 0, [[1024, 1], [1, 1024]]),
          r.dmag)
    chunks = [(ins["fc1_w"], (e * 2048 + j * 128) * 1024, e * 8 + j,
               scr["fc1"], (e * 2048 + j * 128) * 1024)
              for e in range(15) for j in range(8)]
    chunks += [(ins["s_fc1_w"], (j * 128) * 1024, 120 + j,
                scr["s_fc1"], (j * 128) * 1024) for j in range(16)]
    loads = {}

    def post_load(k):
        if k >= len(chunks):
            return
        src_t, src_off = chunks[k][0], chunks[k][1]
        loads[k] = p.dma("g", wtmp[k % 2][:, :],
                         AP(src_t, src_off, [[1024, 128], [1, 1024]]), ldsem)
    post_load(0)
    last_store = None
    wob_store = [None, None]
    for k in range(len(chunks)):
        pace(8)
        post_load(k + 1)
        bi = k % 2
        _, _, mcol, dst_t, dst_off = chunks[k]
        p.wait("g", loads[k], multT_tok, wob_store[bi])
        p.op("g", lambda e2, b=bi, c=mcol: e2.tensor_scalar(
            wob[b][:, :], wtmp[b][:, :], multT[:, c:c + 1], None,
            op0=ALU.mult), inc=gsem)
        last_store = p.dma("g", AP(dst_t, dst_off, [[1024, 128], [1, 1024]]),
                           wob[bi][:, :], stsem)
        wob_store[bi] = last_store
    pr.update({"wdone": last_store, "castdone": r.dmag.tok, "biasdone": bs,
               "gatedone": gt})
    return pr


def emit_moe(p: Prog, r: Res, scr, pr, xio, tiles):
    sv, ss, st = r.sv, r.ss, r.st
    nsub = len(tiles)
    ng = nsub // 4
    xscr = scr["xln"]
    carr = p.sb("carr", [128, nsub * 16])
    cT = p.sb("cT", [16, nsub * 128], BF16)
    work = p.sb("mwork", [128, 15])
    mtmp = p.sb("mmask", [128, 15])
    cbuf = p.sb("cbuf", [128, 15])
    ctmp2 = p.sb("ctmp2", [128, 16])
    cns = p.sb("cns", [128, 4])
    xlnf = p.sb("xlnf_moe", [128, 8 * 128], BF16)
    xsem = p.sem("xlw")
    mk = p.op("g", lambda e: e.memset(ctmp2[:, 15:16], 1.0), inc=r.sg)

    lgp = r.PS45
    lg_free = None
    ct_free = mk
    xst_prev = None
    for s_i, ti in enumerate(tiles):
        lntok = emit_ln(p, r, xio.tile(p, r, ti), s_i % 4, xlnf, 0,
                        after=(xio.ready(ti // 4), pr["wdone"], pr["castdone"],
                               pr["biasdone"], pr["gatedone"]),
                        war=(xst_prev, lg_free), cs=128)
        p.wait("y", lntok)
        xst_prev = p.dma("y", AP(xscr, s_i * 128,
                                 [[nsub * 128, 128], [128 * nsub * 128, 8], [1, 128]]),
                         xlnf[:, :], xsem)
        p.wait("t", lntok, lg_free)
        tk = None
        for dc in range(8):
            tk = p.op("t", lambda e, dc=dc: e.matmul(
                lgp[:, 0:15], xlnf[:, dc * 128:(dc + 1) * 128],
                pr["gwT"][:, dc * 15:(dc + 1) * 15],
                start=(dc == 0), stop=(dc == 7)), inc=(st if dc == 7 else None))
        p.wait("v", tk)
        q = [None]

        def vstep(fn):
            p.wait("v", q[0])
            q[0] = p.op("v", fn, inc=sv)
            return q[0]

        vstep(lambda e: e.reduce_max(cns[:, 0:1], lgp[:, 0:15], axis=AX.X))
        vstep(lambda e: e.tensor_scalar(cns[:, 1:2], cns[:, 0:1], -1.0, None,
                                        op0=ALU.mult))
        p.wait("s", q[0])
        t3 = p.op("s", lambda e: e.activation(cbuf[:, 0:15], lgp[:, 0:15], AF.Exp,
                                              bias=cns[:, 1:2]), inc=ss)
        lg_free = t3
        p.wait("v", t3)
        q[0] = t3
        vstep(lambda e: e.tensor_copy(work[:, :], cbuf[:, 0:15]))
        vstep(lambda e: e.reduce_max(cns[:, 0:1], work[:, :], axis=AX.X))
        vstep(lambda e: e.tensor_scalar(mtmp[:, :], work[:, :], cns[:, 0:1], None,
                                        op0=ALU.is_ge))
        vstep(lambda e: e.scalar_tensor_tensor(work[:, :], mtmp[:, :], -2.0,
                                               work[:, :], op0=ALU.mult, op1=ALU.add))
        vstep(lambda e: e.reduce_max(cns[:, 0:1], work[:, :], axis=AX.X))
        vstep(lambda e: e.tensor_scalar(mtmp[:, :], work[:, :], cns[:, 0:1], None,
                                        op0=ALU.is_ge))
        vstep(lambda e: e.scalar_tensor_tensor(work[:, :], mtmp[:, :], -2.0,
                                               work[:, :], op0=ALU.mult, op1=ALU.add))
        vstep(lambda e: e.reduce_max(cns[:, 2:3], work[:, :], axis=AX.X))
        vstep(lambda e: e.tensor_scalar(mtmp[:, :], cbuf[:, 0:15], cns[:, 2:3], None,
                                        op0=ALU.is_ge))
        vstep(lambda e: e.tensor_tensor(cbuf[:, 0:15], cbuf[:, 0:15], mtmp[:, :],
                                        op=ALU.mult))
        vstep(lambda e: e.reduce_sum(cns[:, 3:4], cbuf[:, 0:15], axis=AX.X))
        vstep(lambda e: e.reciprocal(cns[:, 3:4], cns[:, 3:4]))
        vstep(lambda e, s_=s_i: e.tensor_scalar(carr[:, s_ * 16:s_ * 16 + 15],
                                                cbuf[:, 0:15], cns[:, 3:4], None,
                                                op0=ALU.mult))
        p.wait("v", ct_free)
        t16 = vstep(lambda e, s_=s_i: e.tensor_copy(
            ctmp2[:, 0:15], carr[:, s_ * 16:s_ * 16 + 15]))
        p.wait("t", t16, r.ptf_free)
        t17 = p.op("t", lambda e: e.transpose(r.PTF[0:16, 0:128], ctmp2[:, 0:16],
                                              r.idf[:, :]), inc=st)
        p.wait("v", t17)
        ct_free = p.op("v", lambda e, s_=s_i: e.tensor_copy(
            cT[:, s_ * 128:(s_ + 1) * 128], r.PTF[0:16, 0:128]), inc=sv)
        r.ptf_free = ct_free

    xlt = [p.sb("xlt", [128, 8 * 512], BF16) for _ in range(2)]
    xlsem = [p.sem(f"xl2{i}") for i in range(2)]
    xl_used = [None, None]
    xl_loaded = {}

    def load_xln(tg):
        if tg in xl_loaded or tg >= ng:
            return
        bi = tg % 2
        p.wait("y", xl_used[bi], xst_prev)
        xl_loaded[tg] = p.dma(
            "y", xlt[bi][:, :],
            AP(xscr, tg * 512, [[nsub * 128, 128], [128 * nsub * 128, 8], [1, 512]]),
            xlsem[bi])

    gel = [p.sb(f"gel{i}", [128, 512], BF16) for i in range(2)]
    PHs = [r.PS45, r.PS23]
    PY = r.PS01
    ph_free = [(lg_free,), ()]
    prev_ak = [None, None]
    py_free = [None, None]

    def fc1_chunk(wbuf, ws, col_xh, col_g, bias_xh, bias_g, xbi, dst_a, waitw,
                  a_war=None, slot=0):
        PH = PHs[slot]
        p.wait("t", waitw, xl_loaded.get(xbi[0]), *ph_free[slot])
        tkx = tkg = None
        for dc in range(8):
            tkx = p.op("t", lambda e, dc=dc, w=wbuf, ws=ws, c=col_xh, b=xbi[1],
                       PH=PH: e.matmul(
                PH[:, 0:512], w[:, dc * ws + c:dc * ws + c + 128],
                xlt[b][:, dc * 512:(dc + 1) * 512],
                start=(dc == 0), stop=(dc == 7)), inc=(st if dc == 7 else None))
        for dc in range(8):
            tkg = p.op("t", lambda e, dc=dc, w=wbuf, ws=ws, c=col_g, b=xbi[1],
                       PH=PH: e.matmul(
                PH[:, 512:1024], w[:, dc * ws + c:dc * ws + c + 128],
                xlt[b][:, dc * 512:(dc + 1) * 512],
                start=(dc == 0), stop=(dc == 7)), inc=(st if dc == 7 else None))
        p.wait("s", tkg, prev_ak[slot])
        gk = p.op("s", lambda e, PH=PH, slot=slot: e.activation(
            gel[slot][:, :], PH[:, 512:1024], AF.Gelu, bias=bias_g), inc=ss)
        p.wait("v", tkx, gk, a_war)
        ak = p.op("v", lambda e, d=dst_a, PH=PH, slot=slot: e.scalar_tensor_tensor(
            d, PH[:, 0:512], bias_xh, gel[slot][:, :],
            op0=ALU.add, op1=ALU.mult), inc=sv)
        ph_free[slot] = (gk, ak)
        prev_ak[slot] = ak
        return ak

    def fc2_sub(tg, sub, nj, lhs_a, w2buf, scale_col, bias_lhs=None, bias_rhs=None):
        ti = tiles[tg * 4 + sub]
        py = PY
        for nh in range(2):
            p.wait("t", py_free[nh], pr["castdone"])
            first = True
            if bias_lhs is not None:
                p.op("t", lambda e, nh=nh, py=py: e.matmul(
                    py[:, nh * 512:(nh + 1) * 512], bias_lhs,
                    bias_rhs[:, nh * 512:(nh + 1) * 512], start=True, stop=False))
                first = False
            tk = None
            for j in range(nj):
                tk = p.op("t", lambda e, j=j, nh=nh, py=py, fs=(first and j == 0):
                          e.matmul(
                    py[:, nh * 512:(nh + 1) * 512],
                    lhs_a(j, sub),
                    w2buf[:, j * 1024 + nh * 512: j * 1024 + (nh + 1) * 512],
                    start=fs, stop=(j == nj - 1)),
                    inc=(st if j == nj - 1 else None))
            py_free[nh] = xio.add_half(p, r, ti, nh, py[:, nh * 512:(nh + 1) * 512],
                                       wait=(tk,), scale_col=scale_col)
        return py_free[1]

    asb = p.sb("asb", [128, 16 * 512], BF16)
    es_sh = ExitStack()
    s1b = [p.sb("s1b", [128, 8 * 1024], BF16, es=es_sh) for _ in range(2)]
    s2b = p.sb("s2b", [128, 8 * 1024], BF16, es=es_sh)
    s1sem = [p.sem(f"s1d{i}") for i in range(2)]
    s2sem = p.sem("s2d")
    s1_used = [None, None]
    s2_used = None
    s1_loaded = [None, None]
    s2_loaded = None

    def load_s1(k):
        nonlocal s1_loaded
        bi = k % 2
        p.wait("y", s1_used[bi], pr["wdone"], pr["castdone"])
        tk = None
        for dc in range(8):
            p.dma("y", s1b[bi][:, dc * 1024:dc * 1024 + 512],
                  scr["s_fc1T"].ap()[dc * 128:(dc + 1) * 128,
                                     k * 512:(k + 1) * 512], s1sem[bi])
            tk = p.dma("y", s1b[bi][:, dc * 1024 + 512:dc * 1024 + 1024],
                       scr["s_fc1T"].ap()[dc * 128:(dc + 1) * 128,
                                          2048 + k * 512:2048 + (k + 1) * 512],
                       s1sem[bi])
        s1_loaded[bi] = tk

    def load_s2(h):
        nonlocal s2_loaded
        p.wait("y", s2_used, pr["castdone"])
        tk = None
        for fc in range(8):
            tk = p.dma("y", s2b[:, fc * 1024:(fc + 1) * 1024],
                       scr["s_fc2T"].ap()[(h * 8 + fc) * 128:(h * 8 + fc + 1) * 128, :],
                       s2sem)
        s2_loaded = tk

    load_s1(0)
    load_xln(0)
    a_war = None
    for tg in range(ng):
        load_xln(tg + 1)
        for k in range(4):
            if k < 3:
                load_s1(k + 1)
            elif tg < ng - 1:
                load_s1(0)
            bi = k % 2
            for j4 in range(4):
                j = k * 4 + j4
                fc1_chunk(s1b[bi], 1024, j4 * 128, 512 + j4 * 128,
                          pr["fc1b"][:, 240 + j:241 + j], pr["fc1b"][:, 256 + j:257 + j],
                          (tg, tg % 2), asb[:, j * 512:(j + 1) * 512],
                          s1_loaded[bi], a_war=a_war, slot=j % 2)
            s1_used[bi] = st.tok
        for h in range(2):
            load_s2(h)
            p.wait("t", s2_loaded, sv.tok)
            for sub in range(4):
                bl = cT[0:16, (tg * 4 + sub) * 128:(tg * 4 + sub + 1) * 128] \
                    if h == 0 else None
                fc2_sub(tg, sub, 8,
                        lambda j, sub_=sub, h_=h: asb[:, (h_ * 8 + j) * 512 + sub_ * 128:
                                                      (h_ * 8 + j) * 512 + (sub_ + 1) * 128],
                        s2b, None, bias_lhs=bl, bias_rhs=pr["bmat"])
            s2_used = st.tok
        xl_used[tg % 2] = st.tok
        a_war = st.tok
    xl_loaded.clear()
    shared_done = st.tok
    p.barrier(p.all_toks())
    es_sh.close()

    f1b = p.sb("ef1", [128, 8 * 2048], BF16)
    f2b = p.sb("ef2", [128, 8 * 1024], BF16)
    abuf = asb
    f1sem = p.sem("f1d")
    f2sem = p.sem("f2d")
    f1_used = None
    f2_used = None

    for e in range(15):
        p.wait("y", f1_used, pr["wdone"], pr["castdone"])
        tk1 = None
        for dc in range(8):
            tk1 = p.dma("y", f1b[:, dc * 2048:(dc + 1) * 2048],
                        scr["fc1T"].ap()[dc * 128:(dc + 1) * 128,
                                         e * 2048:(e + 1) * 2048], f1sem)
        p.wait("y", f2_used)
        tk2 = None
        for fc in range(8):
            tk2 = p.dma("y", f2b[:, fc * 1024:(fc + 1) * 1024],
                        scr["fc2T"].ap()[fc * 128:(fc + 1) * 128,
                                         e * 1024:(e + 1) * 1024], f2sem)
        load_xln(0)
        for tg in range(ng):
            load_xln(tg + 1)
            for j in range(8):
                fc1_chunk(f1b, 2048, j * 128, 1024 + j * 128,
                          pr["fc1b"][:, e * 16 + j:e * 16 + j + 1],
                          pr["fc1b"][:, e * 16 + 8 + j:e * 16 + 8 + j + 1],
                          (tg, tg % 2), abuf[:, j * 512:(j + 1) * 512], tk1,
                          a_war=a_war, slot=j % 2)
            p.wait("t", tk2, sv.tok)
            for sub in range(4):
                fc2_sub(tg, sub, 8,
                        lambda j, sub_=sub: abuf[:, j * 512 + sub_ * 128:
                                                 j * 512 + (sub_ + 1) * 128],
                        f2b, carr[:, (tg * 4 + sub) * 16 + e:
                                  (tg * 4 + sub) * 16 + e + 1])
            xl_used[tg % 2] = st.tok
            a_war = st.tok
        xl_loaded.clear()
        f1_used = st.tok
        f2_used = st.tok


def build_moe_program(half):
    nc = bass.Bass("TRN2", target_bir_lowering=False)
    es = ExitStack()
    names = ["gate_w", "fc1_w", "fc1_b", "geglu_mult", "fc2_w", "fc2_b",
             "s_fc1_w", "s_fc1_b", "s_geglu_mult", "s_fc2_w", "s_fc2_b"]
    shapes = {"gate_w": [15, 1024], "fc1_w": [15 * 2048, 1024], "fc1_b": [15 * 2048],
              "geglu_mult": [15 * 1024], "fc2_w": [15 * 1024, 1024],
              "fc2_b": [15 * 1024], "s_fc1_w": [4096, 1024], "s_fc1_b": [4096],
              "s_geglu_mult": [2048], "s_fc2_w": [1024, 2048], "s_fc2_b": [1024]}
    with es:
        p = Prog(nc, es)
        ins = {n: p.dram(n, shapes[n], kind="ExternalInput") for n in names}
        ins["x"] = p.dram("x", [2048, D], kind="ExternalInput")
        ins["c_iden"] = p.dram("c_iden", [128, 128], BF16, kind="ExternalInput")
        y = p.dram("y", [2048, D], kind="ExternalOutput")
        scr = {
            "fc1": p.dram("s_fc1s", [15 * 2048, 1024], BF16),
            "fc2": p.dram("s_fc2s", [15 * 1024, 1024], BF16),
            "s_fc1": p.dram("s_sfc1s", [4096, 1024], BF16),
            "s_fc2": p.dram("s_sfc2s", [1024, 2048], BF16),
            "fc1T": p.dram("s_fc1Ts", [1024, 15 * 2048], BF16),
            "fc2T": p.dram("s_fc2Ts", [1024, 15 * 1024], BF16),
            "s_fc1T": p.dram("s_sfc1Ts", [1024, 4096], BF16),
            "s_fc2T": p.dram("s_sfc2Ts", [2048, 1024], BF16),
            "xln": p.dram("s_xln", [8 * 128 * 2048], BF16),
        }
        r = Res(p, resident_x=True, ntiles=16)
        r.load_consts(p, ins["c_iden"].ap())
        pr = prep_moe(p, r, ins, scr)
        prep_transpose(p, r, [
            (scr["fc1"], 15 * 2048, 1024, scr["fc1T"]),
            (scr["fc2"], 15 * 1024, 1024, scr["fc2T"]),
            (scr["s_fc1"], 4096, 1024, scr["s_fc1T"]),
            (scr["s_fc2"], 1024, 2048, scr["s_fc2T"]),
        ])
        xio = XResident(p, r, x_ap=ins["x"], y_ap=y, off=0, ntiles=16)
        xio.load_all(p, r)
        emit_moe(p, r, scr, pr, xio, list(range(16)))
        xio.store_all(p, r, waits=(r.sv.tok, r.ss.tok, r.st.tok))
        p.wait("y", *xio.final_toks())
        p.build()
    return nc, p.ninstr



def prep_attn_w(p: Prog, r: Res, ins, scr, has_bias):
    p.dma("g", AP(scr["wqkv"], 0, [[3072, 1024], [1, 3072]]),
          AP(ins["wqkv"], 0, [[3072, 1024], [1, 3072]]), r.dmag)
    cd = p.dma("g", AP(scr["wout"], 0, [[1024, 1024], [1, 1024]]),
               AP(ins["wout"], 0, [[1024, 1024], [1, 1024]]), r.dmag)
    pr = {"castdone": cd}
    if has_bias:
        brow = p.sb("brow", [1, 1024], BF16)
        pr["bout"] = brow
        pr["btok"] = p.dma("g", brow[:, :], AP(ins["bout"], 0, [[1024, 1], [1, 1024]]),
                           r.dmag)
        pr["castdone"] = pr["btok"]
    return pr


def emit_attn_window(p: Prog, r: Res, scr, pr, xio, kind, consts):
    sv, ss, st = r.sv, r.ss, r.st
    rope = (kind == "l0aa")
    lag = 1 if kind == "l0aa" else 0
    wqkv = p.sb("wqkv", [128, 8 * 3072], BF16)
    wout = p.sb("wout", [128, 8 * 1024], BF16)
    qsem = p.sem("aw_wd")
    wtok = None
    p.wait("y", pr["castdone"])
    for dc in range(8):
        p.dma("y", wqkv[:, dc * 3072:(dc + 1) * 3072],
              scr["wqkv"].ap()[dc * 128:(dc + 1) * 128, :], qsem)
        wtok = p.dma("y", wout[:, dc * 1024:(dc + 1) * 1024],
                     scr["wout"].ap()[dc * 128:(dc + 1) * 128, :], qsem)
    masks = p.sb("masks", [128, 3 * 384], BF16)
    msem = p.sem("aw_mk")
    mtok = None
    for mv in range(3):
        mtok = p.dma("y", masks[:, mv * 384:(mv + 1) * 384],
                     AP(consts["masks"], mv * 128 * 384, [[384, 128], [1, 384]]), msem)
    kring = p.sb("kring", [128, 4 * 1024], BF16)
    vring = p.sb("vring", [128, 4 * 1040], BF16)
    qfm = [p.sb("qfm", [128, 8 * 128], BF16) for _ in range(2)]
    qtm = p.sb("qtm", [128, 1024], BF16)
    ktm = p.sb("ktm", [128, 1024], BF16)
    aotm = p.sb("aotm", [128, 1024], BF16)
    aofm = p.sb("aofm", [128, 8 * 128], BF16)
    Pm = p.sb("Pm", [128, 384], BF16)
    rl = p.sb("rl", [128, 16])
    xlnf = p.sb("xlnf_aw", [128, 8 * 128], BF16)
    vmem = p.op("g", lambda e: e.memset(vring[:, :], 1.0), inc=r.sg)
    kmem = p.op("g", lambda e: e.memset(kring[:, :], 0.0), inc=r.sg)
    if rope:
        crep = [p.sb("crep", [128, 512], BF16) for _ in range(2)]
        srep = [p.sb("srep", [128, 512], BF16) for _ in range(2)]
        rsem = [p.sem(f"aw_cs{i}") for i in range(2)]
        rtmp = p.sb("rtmp", [128, 512])
        rtmp2 = p.sb("rtmp2", [128, 512])
        cs_used = [None, None]
        cs_tok = [None, None]

        def load_cs(i):
            if i >= NT:
                return
            bi = i % 2
            p.wait("y", cs_used[bi])
            p.dma("y", crep[bi][:, :],
                  AP(consts["crep"], i * 128 * 512, [[512, 128], [1, 512]]), rsem[bi])
            cs_tok[bi] = p.dma("y", srep[bi][:, :],
                               AP(consts["srep"], i * 128 * 512, [[512, 128], [1, 512]]),
                               rsem[bi])
        load_cs(0)

    PQ, PK, PV = r.PS01, r.PS23, r.PS45
    def av_slot(h):
        if h < 7:
            return PK[:, h * 65:h * 65 + 65]
        if h < 14:
            return PK[:, 512 + (h - 7) * 65:512 + (h - 7) * 65 + 65]
        return PV[:, (h - 14) * 65:(h - 14) * 65 + 65]

    X1 = [[64, 16], [1, 32]]
    RAW = {id(PQ): (r.PS0123, 0, 2048), id(PK): (r.PS0123, 1024, 2048)}

    def strided(tsr, base, pat):
        if id(tsr) in RAW:
            t, off, w = RAW[id(tsr)]
            return AP(t, off + base, [[w, 128]] + pat)
        return AP(tsr, base, [[tsr.shape[1] if hasattr(tsr, "shape") else 1024, 128]]
                  + pat)

    prev_attn = None
    prev_evac = None
    prev_add = None
    ln_tok = None

    def proj(i):
        nonlocal ln_tok, prev_evac
        ti = i
        lnt = emit_ln(p, r, xio.tile(p, r, ti), ti % 4, xlnf, 0,
                      after=(xio.ready(ti // 4),), war=(prev_attn,), cs=128)
        ln_tok = lnt
        if rope:
            load_cs(i + 1)
        p.wait("t", lnt, wtok, prev_evac, prev_attn, prev_add)
        toks = {}
        for which, ps, c0 in [("q", PQ, 0), ("k", PK, 1024), ("v", PV, 2048)]:
            for nh in range(2):
                for dc in range(8):
                    tk = p.op("t", lambda e, dc=dc, ps=ps, c0=c0, nh=nh: e.matmul(
                        ps[:, nh * 512:(nh + 1) * 512],
                        xlnf[:, dc * 128:(dc + 1) * 128],
                        wqkv[:, dc * 3072 + c0 + nh * 512:dc * 3072 + c0 + (nh + 1) * 512],
                        start=(dc == 0), stop=(dc == 7)),
                        inc=(st if dc == 7 else None))
            toks[which] = tk
        slot = i % 4
        p.wait("v", toks["v"], vmem, prev_attn)
        vdst = AP(vring, slot * 1040, [[4 * 1040, 128], [65, 16], [1, 64]])
        ev = p.op("v", lambda e, d=vdst: e.tensor_copy(d, PV[:, 0:1024]), inc=sv)
        if rope:
            bi = i % 2
            for src_ps, dst in [(PQ, qtm), (PK, ktm)]:
                p.wait("v", toks["q" if src_ps is PQ else "k"], cs_tok[bi], ev)
                q = [None]

                def vv(fn):
                    p.wait("v", q[0])
                    q[0] = p.op("v", fn, inc=sv)
                    return q[0]
                x1 = strided(src_ps, 0, X1)
                x2 = strided(src_ps, 32, X1)
                cc = AP(crep[bi], 0, [[512, 128], [32, 16], [1, 32]])
                ss2 = AP(srep[bi], 0, [[512, 128], [32, 16], [1, 32]])
                o1 = strided(dst, 0, X1)
                o2 = strided(dst, 32, X1)
                vv(lambda e, x1=x1, cc=cc: e.tensor_tensor(rtmp[:, :], x1, cc,
                                                           op=ALU.mult))
                vv(lambda e, x2=x2, ss_=ss2: e.tensor_tensor(rtmp2[:, :], x2, ss_,
                                                            op=ALU.mult))
                vv(lambda e, o1=o1: e.tensor_tensor(o1, rtmp[:, :], rtmp2[:, :],
                                                    op=ALU.subtract))
                vv(lambda e, x1=x1, ss_=ss2: e.tensor_tensor(rtmp[:, :], x1, ss_,
                                                            op=ALU.mult))
                vv(lambda e, x2=x2, cc=cc: e.tensor_tensor(rtmp2[:, :], x2, cc,
                                                           op=ALU.mult))
                ev = vv(lambda e, o2=o2: e.tensor_tensor(o2, rtmp[:, :], rtmp2[:, :],
                                                         op=ALU.add))
            cs_used[bi] = ev
        else:
            p.wait("s", toks["q"], prev_attn)
            e1 = p.op("s", lambda e: e.copy(qtm[:, :], PQ[:, 0:1024]), inc=ss)
            p.wait("s", toks["k"])
            ev2 = p.op("s", lambda e: e.copy(ktm[:, :], PK[:, 0:1024]), inc=ss)
            ev = ev2
        p.wait("t", ev, (ss.tok if not rope else None), r.pt_free)
        tk = None
        for dc in range(8):
            tk = p.op("t", lambda e, dc=dc: e.transpose(
                r.PT[:, dc * 128:(dc + 1) * 128], qtm[:, dc * 128:(dc + 1) * 128],
                r.idn[:, :]), inc=(st if dc == 7 else None))
        p.wait("v", tk, prev_attn)
        cp1 = p.op("v", lambda e, b=i % 2: e.tensor_copy(qfm[b][:, :], r.PT[:, 0:1024]),
                   inc=sv)
        r.pt_free = cp1
        p.wait("t", cp1)
        for dc in range(8):
            tk = p.op("t", lambda e, dc=dc: e.transpose(
                r.PT[:, dc * 128:(dc + 1) * 128], ktm[:, dc * 128:(dc + 1) * 128],
                r.idn[:, :]), inc=(st if dc == 7 else None))
        p.wait("v", tk, kmem, prev_attn)
        cp2 = p.op("v", lambda e, slot=slot: e.tensor_copy(
            kring[:, slot * 1024:(slot + 1) * 1024], r.PT[:, 0:1024]), inc=sv)
        r.pt_free = cp2
        prev_evac = cp2
        return cp1, cp2

    def attn(j, qready, kready):
        nonlocal prev_attn, prev_add
        qb = qfm[j % 2]
        if kind == "l0aa":
            chunks = [j - 1, j, j + 1]
            mv = 0 if j == 0 else (2 if j == NT - 1 else 1)
        else:
            chunks = [j - 2, j - 1, j]
            mv = 0 if j == 0 else (1 if j == 1 else 2)
        sc = PQ
        deltap = r.PS01
        outs = {}
        atok = None
        for h in range(16):
            hp, lo = divmod(h, 2)
            rbase = 64 * lo
            p.wait("t", qready, kready, mtok, kmem, vmem, prev_add)
            tk = None
            for c, ch in enumerate(chunks):
                slot = ch % 4
                tk = p.op("t", lambda e, c=c, ch=ch, slot=slot, hp=hp, rb=rbase:
                          e.matmul(
                    sc[:, c * 128:(c + 1) * 128],
                    kring[rb:rb + 64, slot * 1024 + hp * 128:slot * 1024 + (hp + 1) * 128],
                    qb[rb:rb + 64, hp * 128:(hp + 1) * 128],
                    start=True, stop=True), inc=(st if c == 2 else None))
            p.wait("s", tk)
            e1 = p.op("s", lambda e: e.activation(Pm[:, :], sc[:, 0:384], AF.Exp,
                                                  scale=0.125), inc=ss)
            p.wait("v", e1)
            e2 = p.op("v", lambda e, mv=mv: e.tensor_tensor(
                Pm[:, :], Pm[:, :], masks[:, mv * 384:(mv + 1) * 384], op=ALU.mult),
                inc=sv)
            p.wait("t", e2)
            for c, ch in enumerate(chunks):
                slot = ch % 4
                atok = p.op("t", lambda e, c=c, slot=slot, h=h: e.matmul(
                    av_slot(h),
                    Pm[:, c * 128:(c + 1) * 128],
                    vring[:, slot * 1040 + h * 65:slot * 1040 + (h + 1) * 65],
                    start=(c == 0), stop=(c == 2)), inc=(st if c == 2 else None))
            outs[h] = atok
        p.wait("v", atok)
        q = [None]
        for h in range(16):
            av = av_slot(h)
            p.wait("v", q[0])
            q[0] = p.op("v", lambda e, av=av, h=h: e.reciprocal(rl[:, h:h + 1],
                                                               av[:, 64:65]), inc=sv)
            p.wait("v", q[0])
            q[0] = p.op("v", lambda e, av=av, h=h: e.tensor_scalar(
                aotm[:, h * 64:(h + 1) * 64], av[:, 0:64], rl[:, h:h + 1], None,
                op0=ALU.mult), inc=sv)
        p.wait("t", q[0], r.pt_free)
        tk = None
        for dc in range(8):
            tk = p.op("t", lambda e, dc=dc: e.transpose(
                r.PT[:, dc * 128:(dc + 1) * 128], aotm[:, dc * 128:(dc + 1) * 128],
                r.idn[:, :]), inc=(st if dc == 7 else None))
        p.wait("v", tk)
        cp = p.op("v", lambda e: e.tensor_copy(aofm[:, :], r.PT[:, 0:1024]), inc=sv)
        r.pt_free = cp
        p.wait("t", cp, wtok)
        first = True
        tk = None
        if "bout" in pr:
            p.wait("t", pr["btok"], r.ones_tok)
            for nh in range(2):
                p.op("t", lambda e, nh=nh: e.matmul(
                    deltap[:, nh * 512:(nh + 1) * 512], r.ones_row[0:1, :],
                    pr["bout"][0:1, nh * 512:(nh + 1) * 512], start=True, stop=False))
            first = False
        for dc in range(8):
            for nh in range(2):
                tk = p.op("t", lambda e, dc=dc, nh=nh, fs=(first and dc == 0): e.matmul(
                    deltap[:, nh * 512:(nh + 1) * 512],
                    aofm[:, dc * 128:(dc + 1) * 128],
                    wout[:, dc * 1024 + nh * 512:dc * 1024 + (nh + 1) * 512],
                    start=fs, stop=(dc == 7)), inc=(st if (dc == 7 and nh == 1) else None))
        prev_add = xio.add(p, r, j, deltap[:, 0:1024], wait=(tk,))
        prev_attn = st.tok

    prev_add = None
    qtoks = {}
    xio.prefetch(p, r, 0)
    for i in range(NT):
        xio.prefetch(p, r, i // 4 + 1)
        cp1, cp2 = proj(i)
        qtoks[i % 2] = cp1
        j = i - lag
        if 0 <= j:
            attn(j, qtoks[j % 2], cp2)
            if j % 4 == 3:
                xio.flush(p, r, j // 4)
    if lag == 1:
        attn(NT - 1, qtoks[(NT - 1) % 2], cp2)
        xio.flush(p, r, NT // 4 - 1)
    return


def build_attnw_program(kind):
    nc = bass.Bass("TRN2", target_bir_lowering=False)
    es = ExitStack()
    rope = (kind == "l0aa")
    with es:
        p = Prog(nc, es)
        ins = {
            "x": p.dram("x", [T, D], kind="ExternalInput"),
            "wqkv": p.dram("wqkv", [1024, 3072], kind="ExternalInput"),
            "wout": p.dram("wout", [1024, 1024], kind="ExternalInput"),
            "c_iden": p.dram("c_iden", [128, 128], BF16, kind="ExternalInput"),
            "c_masks": p.dram("c_masks", [3 * 128, 384], BF16, kind="ExternalInput"),
        }
        if rope:
            ins["bout"] = p.dram("bout", [1024], kind="ExternalInput")
            ins["c_crep"] = p.dram("c_crep", [4096, 512], BF16, kind="ExternalInput")
            ins["c_srep"] = p.dram("c_srep", [4096, 512], BF16, kind="ExternalInput")
        y = p.dram("y", [T, D], kind="ExternalOutput")
        scr = {
            "wqkv": p.dram("s_wqkv", [1024, 3072], BF16),
            "wout": p.dram("s_wout", [1024, 1024], BF16),
        }
        r = Res(p, resident_x=False)
        r.load_consts(p, ins["c_iden"].ap())
        pr = prep_attn_w(p, r, ins, scr, has_bias=rope)
        xio = XStream(p, r, ins["x"], y, nbuf=3)
        consts = {"masks": ins["c_masks"]}
        if rope:
            consts["crep"] = ins["c_crep"]
            consts["srep"] = ins["c_srep"]
        emit_attn_window(p, r, scr, pr, xio, kind, consts)
        p.wait("y", *xio.final_toks())
        p.build()
    return nc, p.ninstr


def np_attn_consts(kind):
    import ml_dtypes
    pvec = np.arange(128)[:, None]
    qvec = np.arange(128)[None, :]
    masks = np.zeros((3, 128, 384), np.float32)
    for v in range(3):
        for c in range(3):
            if kind == "l0aa":
                diff = (c - 1) * 128 + pvec - qvec
                ok = (np.abs(diff) <= 128)
                if v == 0:
                    ok = ok & (c >= 1)
                if v == 2:
                    ok = ok & (c <= 1)
            else:
                diff = (c - 2) * 128 + pvec - qvec
                ok = (diff <= 0) & (diff >= -255)
                if v == 0:
                    ok = ok & (c >= 2)
                if v == 1:
                    ok = ok & (c >= 1)
            masks[v, :, c * 128:(c + 1) * 128] = ok
    out = {"c_masks": masks.reshape(3 * 128, 384).astype(ml_dtypes.bfloat16)}
    if kind == "l0aa":
        inv = 1.0 / 10000.0 ** (np.arange(0, 64, 2, dtype=np.float64) / 64)
        t = np.arange(4096, dtype=np.float64)
        fr = np.outer(t, inv)
        out["c_crep"] = np.tile(np.cos(fr).astype(np.float32), (1, 16)).astype(
            ml_dtypes.bfloat16)
        out["c_srep"] = np.tile(np.sin(fr).astype(np.float32), (1, 16)).astype(
            ml_dtypes.bfloat16)
    return out



def prep_attn_full_alloc(p: Prog, kind):
    if kind == "l0ab":
        return {"mrow": p.sb("mrowf", [1, 1024], BF16)}
    pr = {"negc": p.sb("negc", [128, 16])}
    for bn in ["bq", "bk", "bv", "bo"]:
        pr[bn] = p.sb("b_" + bn, [1, 1024], BF16)
    return pr


def prep_attn_full(p: Prog, r: Res, ins, scr, kind, pr=None, standalone=True):
    if pr is None:
        pr = prep_attn_full_alloc(p, kind)
    if kind == "l0ab":
        nw = p.sb("nwrow", [8, 128])
        nsem = p.sem("nws")
        ld = p.dma("y", nw[:, :], AP(ins["norm_w"], 0, [[128, 8], [1, 128]]), nsem)
        nwc = p.sb("nwc", [128, 8])
        p.wait("t", ld, r.idf_tok, r.ptf_free)
        tk = p.op("t", lambda e: e.transpose(r.PTF[:, 0:8], nw[0:8, :],
                                             r.idf[0:8, 0:8]), inc=r.st)
        p.wait("v", tk)
        tw = p.op("v", lambda e: e.tensor_copy(nwc[:, 0:8], r.PTF[:, 0:8]), inc=r.sv)
        r.ptf_free = tw
        lsem = [p.sem(f"fald{i}") for i in range(2)]
        stsem = p.sem("fast")
        es2 = ExitStack() if standalone else None
        wtmp = [p.sb("fpw", [128, 2048], es=es2) for _ in range(2)]
        wob = p.sb("fpo", [128, 2048], BF16, es=es2)
        busers = [None, None]
        out_free = None
        last = None
        for wname, scrn, width in [("Wq", "wq", 1024), ("Wkv", "wkv", 2048)]:
            for c in range(8):
                bi = c % 2
                p.wait("y", busers[bi])
                ld = p.dma("y", wtmp[bi][:, 0:width],
                           AP(ins[wname], c * 128 * width, [[width, 128], [1, width]]),
                           lsem[bi])
                p.wait("v", ld, tw, out_free)
                t3 = p.op("v", lambda e, b=bi, c=c, w=width: e.tensor_scalar(
                    wob[:, 0:w], wtmp[b][:, 0:w], nwc[:, c:c + 1], None,
                    op0=ALU.mult), inc=r.sv)
                busers[bi] = t3
                p.wait("y", t3)
                last = p.dma("y", AP(scr[scrn], c * 128 * width,
                                     [[width, 128], [1, width]]), wob[:, 0:w]
                             if False else wob[:, 0:width], stsem)
                out_free = last
        if standalone:
            es2.close()
            p.barrier([r.sv.tok, r.st.tok, lsem[0].tok, lsem[1].tok,
                       stsem.tok, nsem.tok])
        cd = p.dma("g", AP(scr["wo"], 0, [[1024, 1024], [1, 1024]]),
                   AP(ins["Wo"], 0, [[1024, 1024], [1, 1024]]), r.dmag)
        pr["wdone"] = last
        pr["castdone"] = cd
        mrow = pr["mrow"]
        es3 = ExitStack() if standalone else None
        gq = p.sb("gq", [1, 1024], es=es3)
        gk = p.sb("gk", [1, 1024], es=es3)
        gsem = p.sem("gsm")
        l1 = p.dma("y", gq[:, :], AP(ins["q_gamma"], 0, [[1024, 1], [1, 1024]]), gsem)
        l2 = p.dma("y", gk[:, :], AP(ins["k_gamma"], 0, [[1024, 1], [1, 1024]]), gsem)
        p.wait("v", l2)
        t1 = p.op("v", lambda e: e.tensor_scalar(gq[:, :], gq[:, :], 1.0, None,
                                                 op0=ALU.add), inc=r.sv)
        p.wait("v", t1)
        t2 = p.op("v", lambda e: e.tensor_scalar(gk[:, :], gk[:, :], 1.0, None,
                                                 op0=ALU.add), inc=r.sv)
        p.wait("v", t2)
        t3 = p.op("v", lambda e: e.tensor_tensor(gq[:, :], gq[:, :], gk[:, :],
                                                 op=ALU.mult), inc=r.sv)
        p.wait("v", t3)
        t4 = p.op("v", lambda e: e.tensor_scalar(mrow[:, :], gq[:, :], 64.0, None,
                                                 op0=ALU.mult), inc=r.sv)
        pr["mtok"] = t4
        if standalone:
            p.barrier(p.all_toks())
            es3.close()
    else:
        csem = p.sem("fa_cast")
        bsem = p.sem("fa_brow")
        for wname, scrn in [("Wq", "wq"), ("Wk", "wk"), ("Wv", "wv"), ("Wo", "wo")]:
            cd = p.dma("g", AP(scr[scrn], 0, [[1024, 1024], [1, 1024]]),
                       AP(ins[wname], 0, [[1024, 1024], [1, 1024]]), csem)
        pr["castdone"] = cd
        pr["wdone"] = cd
        for bn in ["bq", "bk", "bv", "bo"]:
            cd = p.dma("g", pr[bn][:, :], AP(ins[bn], 0, [[1024, 1], [1, 1024]]),
                       bsem)
        pr["btok"] = cd
        negc = pr["negc"]
        es3 = ExitStack() if standalone else None
        crow = p.sb("crow", [1, 16], BF16, es=es3)
        bqf = p.sb("bqf", [1, 1024], es=es3)
        bkf = p.sb("bkf", [1, 1024], es=es3)
        gsem = p.sem("gsm2")
        l1 = p.dma("y", bqf[:, :], AP(ins["bq"], 0, [[1024, 1], [1, 1024]]), gsem)
        l2 = p.dma("y", bkf[:, :], AP(ins["bk"], 0, [[1024, 1], [1, 1024]]), gsem)
        p.wait("v", l2)
        t1 = p.op("v", lambda e: e.tensor_tensor(bqf[:, :], bqf[:, :], bkf[:, :],
                                                 op=ALU.mult), inc=r.sv)
        p.wait("v", t1)
        t2 = p.op("v", lambda e: e.reduce_sum(
            bkf[:, 0:16], AP(bqf, 0, [[1024, 1], [64, 16], [1, 64]]), axis=AX.X),
            inc=r.sv)
        p.wait("v", t2)
        t3 = p.op("v", lambda e: e.tensor_scalar(crow[:, 0:16], bkf[:, 0:16],
                                                 -1.0 / 64.0, None, op0=ALU.mult),
                  inc=r.sv)
        p.wait("t", t3, r.ones_tok)
        tk = p.op("t", lambda e: e.matmul(r.PTF[:, 0:16], r.ones_row[0:1, :],
                                          crow[0:1, 0:16], start=True, stop=True),
                  inc=r.st)
        p.wait("v", tk, r.ptf_free)
        t5 = p.op("v", lambda e: e.tensor_copy(negc[:, 0:16], r.PTF[:, 0:16]), inc=r.sv)
        r.ptf_free = t5
        pr["negctok"] = t5
        if standalone:
            p.barrier(p.all_toks())
            es3.close()
    return pr


def emit_attn_full(p: Prog, r: Res, scr, pr, xio, kind):
    sv, ss, st = r.sv, r.ss, r.st
    l0 = (kind == "l0ab")
    xscr = scr["xln"]
    Kfm = p.sb("Kfm", [128, 4 * 4096], BF16)
    vaug = p.sb("vaug", [128, 32 * 520], BF16)
    Qfm = p.sb("Qfm", [128, 4 * 512], BF16)
    wq = p.sb("wq_s", [128, 8 * 512], BF16)
    wk = p.sb("wk_s", [128, 8 * 512], BF16)
    wv = p.sb("wv_s", [128, 8 * 512], BF16)
    wo = p.sb("wo_s", [128, 4 * 1024], BF16)
    xlnf = p.sb("xlnf_fa", [128, 8 * 128], BF16)
    rtmp = p.sb("rtmp_fa", [128, 512])
    rq8 = p.sb("rq8", [128, 8])
    Pbuf2 = p.sb("Pb2", [128, 2048], BF16)
    Pbuf = [Pbuf2[:, 0:1024], Pbuf2[:, 1024:2048]]
    tbuf2 = p.sb("tb2", [128, 2048]) if l0 else None
    kcast = p.sb("kcast", [128, 512], BF16)
    rlrow = p.sb("rlrow", [1, 512])
    fbc = p.sb("fbc", [64, 512])
    aofm = p.sb("aofm", [128, 4 * 512], BF16)
    mbc = p.sb("mbc", [128, 512], BF16) if l0 else None
    normsT = None if l0 else p.sb("normsT", [8, 4096])
    maskT = None if l0 else p.sb("maskT", [8, 4096], BF16)
    bis = None if l0 else p.sb("bis", [8, 8])
    wsem = p.sem("fa_w")
    xlsem = [p.sem(f"fa_xl{i}") for i in range(2)]
    xasem = [p.sem(f"fa_xas{i}") for i in range(2)]
    xbsem = [p.sem(f"fa_xbs{i}") for i in range(2)]
    xa = [p.sb("fa_xa", [128, 1024], BF16) for _ in range(2)]
    xb2 = [p.sb("fa_xb", [128, 8 * 512], BF16) for _ in range(2)]
    vmem_tok = None
    xscr_done = None

    def load_weights(sw):
        tk = None
        p.wait("y", pr["wdone"], pr["castdone"], st.tok)
        if l0:
            for dc in range(8):
                p.dma("y", wq[:, dc * 512:(dc + 1) * 512],
                      AP(scr["wq"], dc * 128 * 1024 + sw * 512,
                         [[1024, 128], [1, 512]]), wsem)
                p.dma("y", wk[:, dc * 512:(dc + 1) * 512],
                      AP(scr["wkv"], dc * 128 * 2048 + sw * 512,
                         [[2048, 128], [1, 512]]), wsem)
                tk = p.dma("y", wv[:, dc * 512:(dc + 1) * 512],
                           AP(scr["wkv"], dc * 128 * 2048 + 1024 + sw * 512,
                              [[2048, 128], [1, 512]]), wsem)
            for pc in range(4):
                tk = p.dma("y", wo[:, pc * 1024:(pc + 1) * 1024],
                           AP(scr["wo"], (sw * 512 + pc * 128) * 1024,
                              [[1024, 128], [1, 1024]]), wsem)
        else:
            for dc in range(8):
                p.dma("y", wq[:, dc * 512:(dc + 1) * 512],
                      scr["wqT"].ap()[dc * 128:(dc + 1) * 128,
                                      sw * 512:(sw + 1) * 512], wsem)
                p.dma("y", wk[:, dc * 512:(dc + 1) * 512],
                      scr["wkT"].ap()[dc * 128:(dc + 1) * 128,
                                      sw * 512:(sw + 1) * 512], wsem)
                tk = p.dma("y", wv[:, dc * 512:(dc + 1) * 512],
                           scr["wvT"].ap()[dc * 128:(dc + 1) * 128,
                                           sw * 512:(sw + 1) * 512], wsem)
            for pc in range(4):
                tk = p.dma("y", wo[:, pc * 1024:(pc + 1) * 1024],
                           scr["woT"].ap()[sw * 512 + pc * 128:
                                           sw * 512 + (pc + 1) * 128, :], wsem)
        return tk

    def proj_tm(ps, wbuf, xslice, xtok, bias_row, extra=()):
        p.wait("t", xtok, (wsem, wsem.n), *extra)
        first = True
        if bias_row is not None:
            p.wait("t", pr["btok"], r.ones_tok)
            p.op("t", lambda e, b=bias_row, ps=ps: e.matmul(
                ps[:, 0:512], r.ones_row[0:1, :], b, start=True, stop=False))
            first = False
        tk = None
        for dc in range(8):
            tk = p.op("t", lambda e, dc=dc, w=wbuf, xs=xslice(dc), ps=ps,
                      fs=(first and dc == 0): e.matmul(
                ps[:, 0:512], xs, w[:, dc * 512:(dc + 1) * 512],
                start=fs, stop=(dc == 7)), inc=(st if dc == 7 else None))
        return tk

    def rms_to_rq8(ps, war):
        p.wait("s", st.tok, war)
        t1 = p.op("s", lambda e, ps=ps: e.activation(rtmp[:, :], ps[:, 0:512],
                                                     AF.Square), inc=ss)
        p.wait("v", t1, war)
        t2 = p.op("v", lambda e: e.reduce_sum(
            rq8[:, 0:8], AP(rtmp, 0, [[512, 128], [64, 8], [1, 64]]), axis=AX.X),
            inc=sv)
        p.wait("v", t2)
        t3 = p.op("v", lambda e: e.tensor_scalar(rq8[:, 0:8], rq8[:, 0:8], 1e-24, None,
                                                 op0=ALU.max), inc=sv)
        p.wait("s", t3)
        t4 = p.op("s", lambda e: e.activation(rq8[:, 0:8], rq8[:, 0:8], AF.Ln), inc=ss)
        p.wait("s", t4)
        t5 = p.op("s", lambda e: e.activation(rq8[:, 0:8], rq8[:, 0:8], AF.Exp,
                                              scale=-0.5), inc=ss)
        return t5

    RQ8B = AP(rq8, 0, [[8, 128], [1, 8], [0, 64]])
    PQ_, PK_, PV_ = r.PS01, r.PS23, r.PS45

    def phase_a(sw):
        nonlocal vmem_tok, xscr_done
        wtok = load_weights(sw)
        p.wait("g", st.tok)
        vm = p.op("g", lambda e: e.memset(vaug[:, :], 1.0), inc=r.sg)
        vmem_tok = vm
        xa_used = [None, None]
        prev_kv = st.tok
        prev_sq = sv.tok
        nrm_prev = sv.tok
        prev_st = None
        for ti in range(NT):
            bi = ti % 2
            if sw == 0:
                xrd = xio.loadA(p, r, ti)
                lnt = emit_ln(p, r, xio.tileA(p, r, ti), ti % 4, xlnf, 0,
                              after=(xrd,), war=(prev_kv, prev_st), cs=128,
                              scale_mode=("rms2" if l0 else None))
                xio.consumedA(p, r, ti, lnt)
                xtok = lnt
                p.wait("y", lnt)
                prev_st = p.dma("y", AP(xscr, ti * 128,
                                        [[4096, 128], [128 * 4096, 8], [1, 128]]),
                                xlnf[:, 0:1024], xlsem[bi])
                xsl = lambda dc: xlnf[:, dc * 128:(dc + 1) * 128]
            else:
                p.wait("y", xa_used[bi], *xscr_done)
                ld = p.dma("y", xa[bi][:, :],
                           AP(xscr, ti * 128, [[4096, 128], [128 * 4096, 8], [1, 128]]),
                           xasem[bi])
                xtok = ld
                xsl = lambda dc, b=bi: xa[b][:, dc * 128:(dc + 1) * 128]
            ktk = proj_tm(PK_, wk, xsl, xtok,
                          None if l0 else pr["bk"][0:1, sw * 512:(sw + 1) * 512],
                          extra=(prev_kv,))
            vtk = proj_tm(PV_, wv, xsl, xtok,
                          None if l0 else pr["bv"][0:1, sw * 512:(sw + 1) * 512])
            if not l0:
                qtk = proj_tm(PQ_, wq, xsl, xtok,
                              pr["bq"][0:1, sw * 512:(sw + 1) * 512])
            if sw == 1:
                xa_used[bi] = vtk if l0 else qtk
            p.wait("v", vtk, vm, prev_kv)
            vdst = AP(vaug, ti * 520, [[32 * 520, 128], [65, 8], [1, 64]])
            ev = p.op("v", lambda e, d=vdst: e.tensor_copy(d, PV_[:, 0:512]), inc=sv)
            if l0:
                t5k = rms_to_rq8(PK_, nrm_prev)
                p.wait("v", t5k, ktk, nrm_prev, ev)
                kn = p.op("v", lambda e: e.tensor_tensor(
                    kcast[:, 0:512], PK_[:, 0:512], RQ8B, op=ALU.mult), inc=sv)
                nrm_prev = kn
                ksrctok = kn
            else:
                t1 = None
                p.wait("s", qtk, prev_sq, st.tok)
                t1 = p.op("s", lambda e: e.activation(rtmp[:, :], PQ_[:, 0:512],
                                                      AF.Square), inc=ss)
                p.wait("v", t1, prev_sq)
                t2 = p.op("v", lambda e: e.reduce_sum(
                    rq8[:, 0:8], AP(rtmp, 0, [[512, 128], [64, 8], [1, 64]]),
                    axis=AX.X), inc=sv)
                p.wait("t", t2, r.ptf_free)
                t3 = p.op("t", lambda e: e.transpose(r.PTF[0:8, 0:128], rq8[:, 0:8],
                                                     r.idf[:, :]), inc=st)
                p.wait("v", t3)
                t4 = p.op("v", lambda e, ti=ti: e.tensor_copy(
                    normsT[0:8, ti * 128:(ti + 1) * 128],
                    r.PTF[0:8, 0:128]), inc=sv)
                r.ptf_free = t4
                prev_sq = t4
                p.wait("v", ktk, nrm_prev, ev)
                kn = p.op("v", lambda e: e.tensor_copy(kcast[:, 0:512],
                                                       PK_[:, 0:512]), inc=sv)
                nrm_prev = kn
                ksrctok = kn
            p.wait("t", ksrctok, r.pt_free)
            tk = None
            for pc in range(4):
                tk = p.op("t", lambda e, pc=pc: e.transpose(
                    r.PT[:, pc * 128:(pc + 1) * 128],
                    kcast[:, pc * 128:(pc + 1) * 128], r.idn[:, :]),
                    inc=(st if pc == 3 else None))
            p.wait("v", tk)
            kdst = AP(Kfm, ti * 128, [[4 * 4096, 128], [4096, 4], [1, 128]])
            cpk = p.op("v", lambda e, d=kdst: e.tensor_copy(d, r.PT[:, 0:512]), inc=sv)
            r.pt_free = cpk
            prev_kv = cpk
        if sw == 0:
            xscr_done = (xlsem[0].tok, xlsem[1].tok)
        return prev_kv

    def bisect(sw):
        q = [None]

        def vstep(fn):
            p.wait("v", q[0])
            q[0] = p.op("v", fn, inc=sv)
            return q[0]
        rows = normsT[0:8, :]
        brow = bis[0:8, :]
        mrows = maskT[0:8, :]
        q[0] = sv.tok
        vstep(lambda e: e.reduce_max(brow[:, 1:2], rows[:, :], axis=AX.X))
        vstep(lambda e: e.tensor_scalar(brow[:, 1:2], brow[:, 1:2], 1.0, None,
                                        op0=ALU.add))
        vstep(lambda e: e.memset(brow[:, 0:1], 0.0))
        for it in range(40):
            vstep(lambda e: e.tensor_tensor(brow[:, 2:3], brow[:, 0:1], brow[:, 1:2],
                                            op=ALU.add))
            vstep(lambda e: e.tensor_scalar(brow[:, 2:3], brow[:, 2:3], 0.5, None,
                                            op0=ALU.mult))
            vstep(lambda e: e.tensor_scalar(mrows[:, :], rows[:, :], brow[:, 2:3],
                                            0.0, op0=ALU.is_ge, op1=ALU.add,
                                            accum_out=brow[:, 3:4]))
            vstep(lambda e: e.tensor_scalar(brow[:, 4:5], brow[:, 3:4], 819.0, None,
                                            op0=ALU.is_ge))
            vstep(lambda e: e.tensor_tensor(brow[:, 5:6], brow[:, 2:3], brow[:, 0:1],
                                            op=ALU.subtract))
            vstep(lambda e: e.tensor_tensor(brow[:, 5:6], brow[:, 5:6], brow[:, 4:5],
                                            op=ALU.mult))
            vstep(lambda e: e.tensor_tensor(brow[:, 0:1], brow[:, 0:1], brow[:, 5:6],
                                            op=ALU.add))
            vstep(lambda e: e.tensor_tensor(brow[:, 5:6], brow[:, 1:2], brow[:, 2:3],
                                            op=ALU.subtract))
            vstep(lambda e: e.tensor_tensor(brow[:, 5:6], brow[:, 5:6], brow[:, 4:5],
                                            op=ALU.mult))
            vstep(lambda e: e.tensor_tensor(brow[:, 1:2], brow[:, 2:3], brow[:, 5:6],
                                            op=ALU.add))
        return vstep(lambda e: e.tensor_scalar(mrows[:, :], rows[:, :], brow[:, 0:1],
                                               None, op0=ALU.is_ge))

    def phase_b(sw, kv_tok, mtok):
        SCe, SCo = r.PS01, r.PS23
        xb_used = [None, None]
        prev_add = None
        prev_dn = sv.tok
        for jq in range(8):
            bi = jq % 2
            p.wait("y", xb_used[bi], *xscr_done)
            ld = p.dma("y", xb2[bi][:, :],
                       AP(xscr, jq * 512, [[4096, 128], [128 * 4096, 8], [1, 512]]),
                       xbsem[bi])
            qtoks = []
            for s4 in range(4):
                qtk = proj_tm(SCe, wq,
                              lambda dc, b=bi, s4=s4: xb2[b][:, dc * 512 + s4 * 128:
                                                             dc * 512 + (s4 + 1) * 128],
                              ld,
                              None if l0 else pr["bq"][0:1, sw * 512:(sw + 1) * 512],
                              extra=(prev_add, prev_dn))
                if l0:
                    t5 = rms_to_rq8(SCe, prev_dn)
                    p.wait("v", t5, qtk, prev_dn, pr["mbctok"])
                    qn = p.op("v", lambda e: e.tensor_tensor(
                        rtmp[:, :], SCe[:, 0:512], RQ8B, op=ALU.mult), inc=sv)
                    p.wait("v", qn)
                    qn2 = p.op("v", lambda e: e.tensor_tensor(
                        Pbuf[1][:, 0:512], rtmp[:, :], mbc[:, :], op=ALU.mult), inc=sv)
                    qrdy = qn2
                else:
                    p.wait("v", qtk, prev_dn)
                    qrdy = p.op("v", lambda e: e.tensor_copy(Pbuf[1][:, 0:512],
                                                             SCe[:, 0:512]), inc=sv)
                p.wait("t", qrdy, r.pt_free)
                tk = None
                for pc in range(4):
                    tk = p.op("t", lambda e, pc=pc: e.transpose(
                        r.PT[:, pc * 128:(pc + 1) * 128],
                        Pbuf[1][:, pc * 128:(pc + 1) * 128], r.idn[:, :]),
                        inc=(st if pc == 3 else None))
                p.wait("v", tk)
                qdst = AP(Qfm, s4 * 128, [[4 * 512, 128], [512, 4], [1, 128]])
                cpq = p.op("v", lambda e, d=qdst: e.tensor_copy(d, r.PT[:, 0:512]),
                           inc=sv)
                r.pt_free = cpq
                prev_dn = cpq
                qtoks.append(cpq)
            qready = qtoks[-1]
            last_av = None
            ptok = [None, None]
            PARS = [(0, None, SCe, 0), (1, None, SCo, 64)]
            for hp in range(4):
                for cg in range(16):
                    sctok = [None, None]
                    a2s = [None, None]
                    for par, _, SC, rb in PARS:
                        p.wait("t", qready, kv_tok, mtok, prev_add, ptok[par])
                        tk = None
                        for cc in range(2):
                            c = cg * 2 + cc
                            tk = p.op("t", lambda e, c=c, cc=cc, SC=SC, rb=rb, hp=hp:
                                      e.matmul(
                                SC[:, cc * 512:(cc + 1) * 512],
                                Kfm[rb:rb + 64, hp * 4096 + c * 128:
                                    hp * 4096 + (c + 1) * 128],
                                Qfm[rb:rb + 64, hp * 512:(hp + 1) * 512],
                                start=True, stop=True), inc=(st if cc == 1 else None))
                        sctok[par] = tk
                    if l0:
                        a1s = [None, None]
                        for par, _, SC, rb in PARS:
                            p.wait("s", sctok[par], ptok[par])
                            a1s[par] = p.op("s", lambda e, SC=SC, par=par:
                                            e.activation(
                                tbuf2[:, par * 1024:(par + 1) * 1024],
                                SC[:, 0:1024], AF.Tanh, scale=0.02), inc=ss)
                        for par, _, SC, rb in PARS:
                            p.wait("s", a1s[par])
                            a2s[par] = p.op("s", lambda e, par=par: e.activation(
                                Pbuf[par][:, 0:1024],
                                tbuf2[:, par * 1024:(par + 1) * 1024], AF.Exp,
                                scale=6.25), inc=ss)
                    else:
                        for par, _, SC, rb in PARS:
                            h = 2 * hp + par
                            p.wait("s", sctok[par], ptok[par])
                            a2s[par] = p.op("s", lambda e, SC=SC, h=h, sw=sw, par=par:
                                            e.activation(
                                Pbuf[par][:, 0:1024], SC[:, 0:1024], AF.Exp,
                                scale=0.125,
                                bias=pr["negc"][:, sw * 8 + h:sw * 8 + h + 1]),
                                inc=ss)
                    for par, _, SC, rb in PARS:
                        h = 2 * hp + par
                        p.wait("t", a2s[par], vmem_tok)
                        cb = par * 512
                        for cc in range(2):
                            c = cg * 2 + cc
                            tk2 = p.op("t", lambda e, c=c, cc=cc, cb=cb, h=h, par=par:
                                       e.matmul(
                                r.PS45[0:65, cb:cb + 512],
                                vaug[:, c * 520 + h * 65:c * 520 + (h + 1) * 65],
                                Pbuf[par][:, cc * 512:(cc + 1) * 512],
                                start=(c == 0), stop=(c == 31)),
                                inc=(st if (c == 31) else None))
                        ptok[par] = tk2
                for par, h in [(0, 2 * hp), (1, 2 * hp + 1)]:
                    cb = par * 512
                    tk2 = ptok[par]
                    p.wait("v", tk2)
                    q1 = p.op("v", lambda e, cb=cb: e.reciprocal(
                        rlrow[0:1, :], r.PS45[64:65, cb:cb + 512]), inc=sv)
                    if not l0:
                        p.wait("t", mtok, r.ptf_free, r.consts_tok)
                        ms = p.op("t", lambda e, h=h, jq=jq: e.matmul(
                            r.PTF[0:1, 0:512], r.idn[0:8, h:h + 1],
                            maskT[0:8, jq * 512:(jq + 1) * 512],
                            start=True, stop=True), inc=st)
                        p.wait("v", q1, ms)
                        q1 = p.op("v", lambda e: e.tensor_tensor(
                            rlrow[0:1, :], rlrow[0:1, :], r.PTF[0:1, 0:512],
                            op=ALU.mult), inc=sv)
                        r.ptf_free = q1
                    p.wait("t", q1, r.ptf_free, r.ones_tok)
                    q2 = p.op("t", lambda e: e.matmul(r.PTF[0:64, 0:512],
                                                      r.ones_f[0:1, 0:64],
                                                      rlrow[0:1, 0:512],
                                                      start=True, stop=True), inc=st)
                    p.wait("v", q2)
                    q3 = p.op("v", lambda e: e.tensor_copy(fbc[:, :],
                                                           r.PTF[0:64, 0:512]),
                              inc=sv)
                    r.ptf_free = q3
                    p.wait("v", q3)
                    q4 = p.op("v", lambda e, cb=cb, hp=hp, par=par: e.tensor_tensor(
                        aofm[64 * par:64 * par + 64, hp * 512:(hp + 1) * 512],
                        r.PS45[0:64, cb:cb + 512], fbc[:, :], op=ALU.mult), inc=sv)
                    ptok[par] = q4
                    last_av = q4
            for s4 in range(4):
                deltap = SCe
                p.wait("t", last_av, (wsem, wsem.n), prev_add)
                first = True
                tk = None
                if (not l0) and sw == 0:
                    p.wait("t", pr["btok"], r.ones_tok)
                    for nh in range(2):
                        p.op("t", lambda e, nh=nh, deltap=deltap: e.matmul(
                            deltap[:, nh * 512:(nh + 1) * 512], r.ones_row[0:1, :],
                            pr["bo"][0:1, nh * 512:(nh + 1) * 512],
                            start=True, stop=False))
                    first = False
                for pc in range(4):
                    for nh in range(2):
                        tk = p.op("t", lambda e, pc=pc, nh=nh, s4=s4, deltap=deltap,
                                  fs=(first and pc == 0): e.matmul(
                            deltap[:, nh * 512:(nh + 1) * 512],
                            aofm[:, pc * 512 + s4 * 128:pc * 512 + (s4 + 1) * 128],
                            wo[:, pc * 1024 + nh * 512:pc * 1024 + (nh + 1) * 512],
                            start=fs, stop=(pc == 3)),
                            inc=(st if (pc == 3 and nh == 1) else None))
                prev_add = xio.add(p, r, jq * 4 + s4, deltap[:, 0:1024], wait=(tk,))
            xb_used[bi] = st.tok
        return prev_add

    mb2 = None
    for sw in range(2):
        if l0:
            p.wait("t", pr["mtok"], r.ptf_free, r.ones_tok)
            if mb2 is not None:
                p.wait("t", mb2)
            mb = p.op("t", lambda e, sw=sw: e.matmul(
                r.PTF[:, 0:512], r.ones_row[0:1, :],
                pr["mrow"][0:1, sw * 512:(sw + 1) * 512], start=True, stop=True),
                inc=st)
            p.wait("v", mb, sv.tok)
            mb2 = p.op("v", lambda e: e.tensor_copy(mbc[:, :], r.PTF[:, 0:512]),
                       inc=sv)
            r.ptf_free = mb2
            pr["mbctok"] = mb2
        kv = phase_a(sw)
        mtok = None
        if not l0:
            mtok = bisect(sw)
        phase_b(sw, kv, mtok)
        xio.new_pass()


def build_attnf_program(kind):
    nc = bass.Bass("TRN2", target_bir_lowering=False)
    es = ExitStack()
    l0 = (kind == "l0ab")
    with es:
        p = Prog(nc, es)
        ins = {"x": p.dram("x", [T, D], kind="ExternalInput"),
               "c_iden": p.dram("c_iden", [128, 128], BF16, kind="ExternalInput")}
        if l0:
            ins["norm_w"] = p.dram("norm_w", [1024], kind="ExternalInput")
            ins["Wq"] = p.dram("Wq", [1024, 1024], kind="ExternalInput")
            ins["Wkv"] = p.dram("Wkv", [1024, 2048], kind="ExternalInput")
            ins["Wo"] = p.dram("Wo", [1024, 1024], kind="ExternalInput")
            ins["q_gamma"] = p.dram("q_gamma", [1024], kind="ExternalInput")
            ins["k_gamma"] = p.dram("k_gamma", [1024], kind="ExternalInput")
            scr = {"wq": p.dram("s_wq", [1024, 1024], BF16),
                   "wkv": p.dram("s_wkv", [1024, 2048], BF16),
                   "wo": p.dram("s_wo", [1024, 1024], BF16)}
        else:
            for n in ["Wq", "Wk", "Wv", "Wo"]:
                ins[n] = p.dram(n, [1024, 1024], kind="ExternalInput")
            for n in ["bq", "bk", "bv", "bo"]:
                ins[n] = p.dram(n, [1024], kind="ExternalInput")
            scr = {w: p.dram("s_" + w, [1024, 1024], BF16)
                   for w in ["wq", "wk", "wv", "wo"]}
            for w in ["wq", "wk", "wv", "wo"]:
                scr[w + "T"] = p.dram("s_" + w + "T", [1024, 1024], BF16)
        scr["xln"] = p.dram("s_xlnf", [8 * 128 * 4096], BF16)
        y = p.dram("y", [T, D], kind="ExternalOutput")
        r = Res(p, resident_x=False)
        r.load_consts(p, ins["c_iden"].ap())
        pr = prep_attn_full(p, r, ins, scr, kind)
        if not l0:
            prep_transpose(p, r, [(scr[w], 1024, 1024, scr[w + "T"])
                                  for w in ["wq", "wk", "wv", "wo"]])
        xio = XPhased(p, r, ins["x"], y)
        emit_attn_full(p, r, scr, pr, xio, kind)
        p.wait("y", *xio.final_toks())
        p.build()
    return nc, p.ninstr



def build_fused_program():
    nc = bass.Bass("TRN2", target_bir_lowering=False)
    es = ExitStack()
    with es:
        p = Prog(nc, es)

        def EI(name, shape, dt=F32):
            return p.dram(name, shape, dt, kind="ExternalInput")

        I = {
            "x": EI("x", [T, D]),
            "aa_wqkv": EI("aa_wqkv", [1024, 3072]),
            "aa_wout": EI("aa_wout", [1024, 1024]),
            "aa_bout": EI("aa_bout", [1024]),
            "m_gate_w": EI("m_gate_w", [15, 1024]),
            "m_fc1_w": EI("m_fc1_w", [15 * 2048, 1024]),
            "m_fc1_b": EI("m_fc1_b", [15 * 2048]),
            "m_geglu": EI("m_geglu", [15 * 1024]),
            "m_fc2_w": EI("m_fc2_w", [15 * 1024, 1024]),
            "m_fc2_b": EI("m_fc2_b", [15 * 1024]),
            "m_sfc1_w": EI("m_sfc1_w", [4096, 1024]),
            "m_sfc1_b": EI("m_sfc1_b", [4096]),
            "m_sgeglu": EI("m_sgeglu", [2048]),
            "m_sfc2_w": EI("m_sfc2_w", [1024, 2048]),
            "m_sfc2_b": EI("m_sfc2_b", [1024]),
            "ab_norm_w": EI("ab_norm_w", [1024]),
            "ab_Wq": EI("ab_Wq", [1024, 1024]),
            "ab_Wkv": EI("ab_Wkv", [1024, 2048]),
            "ab_Wo": EI("ab_Wo", [1024, 1024]),
            "ab_qg": EI("ab_qg", [1024]),
            "ab_kg": EI("ab_kg", [1024]),
            "ba_Wq": EI("ba_Wq", [1024, 1024]),
            "ba_bq": EI("ba_bq", [1024]),
            "ba_Wk": EI("ba_Wk", [1024, 1024]),
            "ba_bk": EI("ba_bk", [1024]),
            "ba_Wv": EI("ba_Wv", [1024, 1024]),
            "ba_bv": EI("ba_bv", [1024]),
            "ba_Wo": EI("ba_Wo", [1024, 1024]),
            "ba_bo": EI("ba_bo", [1024]),
            "bb_wqkv": EI("bb_wqkv", [1024, 3072]),
            "bb_wout": EI("bb_wout", [1024, 1024]),
            "f_w1": EI("f_w1", [4096, 1024]),
            "f_b1": EI("f_b1", [4096]),
            "f_w2": EI("f_w2", [1024, 4096]),
            "f_b2": EI("f_b2", [1024]),
            "c_iden": EI("c_iden", [128, 128], BF16),
            "c_masks_aa": EI("c_masks_aa", [3 * 128, 384], BF16),
            "c_crep": EI("c_crep", [4096, 512], BF16),
            "c_srep": EI("c_srep", [4096, 512], BF16),
            "c_masks_ab": EI("c_masks_ab", [3 * 128, 384], BF16),
        }
        y = p.dram("y", [T, D], kind="ExternalOutput")
        xbuf = p.dram("xbuf", [T, D])
        scr_aa = {"wqkv": p.dram("s_aa_wqkv", [1024, 3072], BF16),
                  "wout": p.dram("s_aa_wout", [1024, 1024], BF16)}
        scr_moe = {"fc1": p.dram("s_m_fc1", [15 * 2048, 1024], BF16),
                   "fc2": p.dram("s_m_fc2", [15 * 1024, 1024], BF16),
                   "s_fc1": p.dram("s_m_sfc1", [4096, 1024], BF16),
                   "s_fc2": p.dram("s_m_sfc2", [1024, 2048], BF16),
                   "xln": p.dram("s_m_xln", [8 * 128 * 2048], BF16)}
        scr_ab = {"wq": p.dram("s_ab_wq", [1024, 1024], BF16),
                  "wkv": p.dram("s_ab_wkv", [1024, 2048], BF16),
                  "wo": p.dram("s_ab_wo", [1024, 1024], BF16),
                  "xln": p.dram("s_ab_xln", [8 * 128 * 4096], BF16)}
        scr_ba = {w: p.dram("s_ba_" + w, [1024, 1024], BF16)
                  for w in ["wq", "wk", "wv", "wo"]}
        scr_ba["xln"] = p.dram("s_ba_xln", [8 * 128 * 4096], BF16)
        scr_bb = {"wqkv": p.dram("s_bb_wqkv", [1024, 3072], BF16),
                  "wout": p.dram("s_bb_wout", [1024, 1024], BF16)}
        scr_f = {"bw1": p.dram("s_f_bw1", [4096, 1024], BF16),
                 "bw2": p.dram("s_f_bw2", [1024, 4096], BF16)}

        r = Res(p)
        r.load_consts(p, I["c_iden"].ap())

        def SCOPE(fn):
            es2 = ExitStack()
            old = p.es
            p.es = es2
            fn()
            p.barrier(p.all_toks(), strict=True)
            p.es = old
            es2.close()

        pr_moe = prep_moe_alloc(p)
        pr_ab = prep_attn_full_alloc(p, "l0ab")
        pr_ba = prep_attn_full_alloc(p, "l1aa")
        pr_f = prep_l1ffn_alloc(p)
        scr_moe["fc1T"] = p.dram("s_m_fc1T", [1024, 15 * 2048], BF16)
        scr_moe["fc2T"] = p.dram("s_m_fc2T", [1024, 15 * 1024], BF16)
        scr_moe["s_fc1T"] = p.dram("s_m_sfc1T", [1024, 4096], BF16)
        scr_moe["s_fc2T"] = p.dram("s_m_sfc2T", [2048, 1024], BF16)
        for w in ["wq", "wk", "wv", "wo"]:
            scr_ba[w + "T"] = p.dram("s_ba_" + w + "T", [1024, 1024], BF16)
        scr_f["bw1T"] = p.dram("s_f_bw1T", [1024, 4096], BF16)
        scr_f["bw2T"] = p.dram("s_f_bw2T", [4096, 1024], BF16)
        moe_ins = {"gate_w": I["m_gate_w"], "fc1_w": I["m_fc1_w"],
                   "fc1_b": I["m_fc1_b"], "geglu_mult": I["m_geglu"],
                   "fc2_w": I["m_fc2_w"], "fc2_b": I["m_fc2_b"],
                   "s_fc1_w": I["m_sfc1_w"], "s_fc1_b": I["m_sfc1_b"],
                   "s_geglu_mult": I["m_sgeglu"], "s_fc2_w": I["m_sfc2_w"],
                   "s_fc2_b": I["m_sfc2_b"]}

        pr_aa = prep_attn_w(p, r, {"wqkv": I["aa_wqkv"], "wout": I["aa_wout"],
                                   "bout": I["aa_bout"]}, scr_aa, has_bias=True)

        def s_l0aa():
            xio = XStream(p, r, I["x"], xbuf, nbuf=3)
            emit_attn_window(p, r, scr_aa, pr_aa, xio, "l0aa",
                             {"masks": I["c_masks_aa"], "crep": I["c_crep"],
                              "srep": I["c_srep"]})
        SCOPE(s_l0aa)
        prep_moe(p, r, moe_ins, scr_moe, pr=pr_moe)
        prep_transpose(p, r, [
            (scr_moe["fc1"], 15 * 2048, 1024, scr_moe["fc1T"]),
            (scr_moe["fc2"], 15 * 1024, 1024, scr_moe["fc2T"]),
            (scr_moe["s_fc1"], 4096, 1024, scr_moe["s_fc1T"]),
            (scr_moe["s_fc2"], 1024, 2048, scr_moe["s_fc2T"]),
        ])

        def s_moe(half):
            def fn():
                xio = XResident(p, r, x_ap=xbuf, y_ap=xbuf,
                                off=half * 2048 * 1024, ntiles=16)
                xio.load_all(p, r)
                emit_moe(p, r, scr_moe, pr_moe, xio, list(range(16)))
                xio.store_all(p, r, waits=(r.sv.tok, r.ss.tok, r.st.tok))
            return fn

        SCOPE(s_moe(0))
        SCOPE(s_moe(1))

        prep_attn_full(p, r, {"norm_w": I["ab_norm_w"], "Wq": I["ab_Wq"],
                              "Wkv": I["ab_Wkv"], "Wo": I["ab_Wo"],
                              "q_gamma": I["ab_qg"], "k_gamma": I["ab_kg"]},
                       scr_ab, "l0ab", pr=pr_ab)

        def s_l0ab():
            xio = XPhased(p, r, xbuf, xbuf)
            emit_attn_full(p, r, scr_ab, pr_ab, xio, "l0ab")
        SCOPE(s_l0ab)

        SCOPE(s_moe(0))
        SCOPE(s_moe(1))

        prep_attn_full(p, r, {"Wq": I["ba_Wq"], "bq": I["ba_bq"],
                              "Wk": I["ba_Wk"], "bk": I["ba_bk"],
                              "Wv": I["ba_Wv"], "bv": I["ba_bv"],
                              "Wo": I["ba_Wo"], "bo": I["ba_bo"]},
                       scr_ba, "l1aa", pr=pr_ba)
        prep_transpose(p, r, [(scr_ba[w], 1024, 1024, scr_ba[w + "T"])
                              for w in ["wq", "wk", "wv", "wo"]])

        def s_l1aa():
            xio = XPhased(p, r, xbuf, xbuf)
            emit_attn_full(p, r, scr_ba, pr_ba, xio, "l1aa")
        SCOPE(s_l1aa)

        prep_l1ffn(p, r, {"w1": I["f_w1"], "b1": I["f_b1"],
                          "w2": I["f_w2"], "b2": I["f_b2"]}, scr_f, pr=pr_f)
        prep_transpose(p, r, [(scr_f["bw1"], 4096, 1024, scr_f["bw1T"]),
                              (scr_f["bw2"], 1024, 4096, scr_f["bw2T"])])

        def s_ffn(dst):
            def fn():
                xio = XStream(p, r, xbuf, dst, nbuf=2)
                bufs = mk_l1ffn_bufs(p)
                emit_l1ffn(p, r, bufs, scr_f, pr_f, xio)
            return fn
        SCOPE(s_ffn(xbuf))

        pr_bb = prep_attn_w(p, r, {"wqkv": I["bb_wqkv"], "wout": I["bb_wout"]},
                            scr_bb, has_bias=False)

        def s_l1ab():
            xio = XStream(p, r, xbuf, xbuf, nbuf=3)
            emit_attn_window(p, r, scr_bb, pr_bb, xio, "l1ab",
                             {"masks": I["c_masks_ab"]})
        SCOPE(s_l1ab)

        SCOPE(s_ffn(y))
        p.build()
    return nc, p.ninstr, len(p._sems)



_PROGS = {}


def _prog(kind):
    if kind not in _PROGS:
        if kind == "moe":
            _PROGS[kind] = build_moe_program(0)[0]
        elif kind in ("l0aa", "l1ab"):
            _PROGS[kind] = build_attnw_program(kind)[0]
        elif kind in ("l0ab", "l1aa"):
            _PROGS[kind] = build_attnf_program(kind)[0]
        elif kind == "l1ffn":
            _PROGS[kind] = build_l1ffn_program()[0]
    return _PROGS[kind]


def kernel(**ins):
    x = np.ascontiguousarray(ins["L0_attn_a_x"].astype(np.float32))
    B = x.shape[0]
    consts = np_consts()
    cw_aa = np_attn_consts("l0aa")
    cw_ab = np_attn_consts("l1ab")
    if "fused" not in _PROGS:
        _PROGS["fused"] = build_fused_program()[0]
    nc = _PROGS["fused"]
    base = {
        "aa_wqkv": ins["L0_attn_a_Wqkv"], "aa_wout": ins["L0_attn_a_Wout"],
        "aa_bout": ins["L0_attn_a_bout"],
        "m_gate_w": ins["L0_ffn_gate_w"],
        "m_fc1_w": np.ascontiguousarray(ins["L0_ffn_fc1_w"].reshape(-1, 1024)),
        "m_fc1_b": np.ascontiguousarray(ins["L0_ffn_fc1_b"].reshape(-1)),
        "m_geglu": np.ascontiguousarray(ins["L0_ffn_geglu_mult"].reshape(-1)),
        "m_fc2_w": np.ascontiguousarray(ins["L0_ffn_fc2_w"].reshape(-1, 1024)),
        "m_fc2_b": np.ascontiguousarray(ins["L0_ffn_fc2_b"].reshape(-1)),
        "m_sfc1_w": ins["L0_ffn_s_fc1_w"], "m_sfc1_b": ins["L0_ffn_s_fc1_b"],
        "m_sgeglu": ins["L0_ffn_s_geglu_mult"],
        "m_sfc2_w": ins["L0_ffn_s_fc2_w"], "m_sfc2_b": ins["L0_ffn_s_fc2_b"],
        "ab_norm_w": ins["L0_attn_b_norm_w"], "ab_Wq": ins["L0_attn_b_Wq"],
        "ab_Wkv": ins["L0_attn_b_Wkv"], "ab_Wo": ins["L0_attn_b_Wo"],
        "ab_qg": np.ascontiguousarray(ins["L0_attn_b_q_gamma"].reshape(-1)),
        "ab_kg": np.ascontiguousarray(ins["L0_attn_b_k_gamma"].reshape(-1)),
        "ba_Wq": ins["L1_attn_a_Wq"], "ba_bq": ins["L1_attn_a_bq"],
        "ba_Wk": ins["L1_attn_a_Wk"], "ba_bk": ins["L1_attn_a_bk"],
        "ba_Wv": ins["L1_attn_a_Wv"], "ba_bv": ins["L1_attn_a_bv"],
        "ba_Wo": ins["L1_attn_a_Wo"], "ba_bo": ins["L1_attn_a_bo"],
        "bb_wqkv": ins["L1_attn_b_w_qkv"], "bb_wout": ins["L1_attn_b_w_out"],
        "f_w1": ins["L1_ffn_w1"], "f_b1": ins["L1_ffn_b1"],
        "f_w2": ins["L1_ffn_w2"], "f_b2": ins["L1_ffn_b2"],
        "c_iden": consts["c_iden"],
        "c_masks_aa": cw_aa["c_masks"], "c_crep": cw_aa["c_crep"],
        "c_srep": cw_aa["c_srep"],
        "c_masks_ab": cw_ab["c_masks"],
    }
    maps = []
    for b in range(B):
        m = dict(base)
        m["x"] = np.ascontiguousarray(x[b])
        maps.append(m)
    res = run_bass_kernel_spmd(nc, maps, core_ids=list(range(B)))
    return np.stack([res.results[b]["y"] for b in range(B)], 0)


# version 14
# speedup vs baseline: 1.0003x; 1.0003x over previous
import numpy as np
from contextlib import ExitStack

import concourse.bass as bass
import concourse.mybir as mybir
from concourse.bass_utils import run_bass_kernel_spmd

F32 = mybir.dt.float32
BF16 = mybir.dt.bfloat16
AF = mybir.ActivationFunctionType
ALU = mybir.AluOpType
AX = mybir.AxisListType

T = 4096
D = 1024
NT = T // 128
LN_EPS = 1e-5


def AP(t, off, pat):
    return bass.AP(t, off, pat)


class Sem:
    _next = [0]

    def __init__(self, h):
        self.h = h
        self.n = 0
        Sem._next[0] += 1
        self.uid = Sem._next[0]

    def inc(self, k=1):
        self.n += k
        return self.n

    @property
    def tok(self):
        return (self, self.n)


class Prog:
    QMAP = {"t": "tensor", "v": "vector", "s": "scalar", "g": "gpsimd", "y": "sync"}

    def __init__(self, nc, es):
        self.nc = nc
        self.es = es
        self.es0 = es
        self.q = {k: [] for k in self.QMAP}
        self._waited = {k: {} for k in self.QMAP}
        self._sems = []
        self._sempool = {}
        self.last_inc = {k: True for k in self.QMAP}
        self.ninstr = 0
        self._uid = 0

    def un(self, base):
        self._uid += 1
        return f"{base}_{self._uid}"

    def sem(self, name):
        s = self._sempool.get(name)
        if s is None:
            s = Sem(self.es0.enter_context(self.nc.semaphore(name)))
            self._sempool[name] = s
            self._sems.append(s)
        return s

    def all_toks(self):
        return [s.tok for s in self._sems]

    def sb(self, name, shape, dt=F32, es=None):
        return (es or self.es).enter_context(
            self.nc.sbuf_tensor(self.un(name), shape, dt))

    def barrier(self, toks, strict=False):
        if strict:
            bad = [qn for qn in self.QMAP if self.q[qn] and not self.last_inc[qn]]
            if bad:
                raise RuntimeError(f"barrier: queues end without inc: {bad}")
        for qn in self.QMAP:
            self.wait(qn, *toks)

    def ps(self, name, shape, dt=F32):
        return self.es.enter_context(self.nc.psum_tensor(self.un(name), shape, dt))

    def dram(self, name, shape, dt=F32, kind="Internal"):
        return self.nc.dram_tensor(name, shape, dt, kind=kind)

    def wait(self, qn, *tokens):
        for tk in tokens:
            if tk is None:
                continue
            sem, val = tk
            if val <= 0:
                continue
            w = self._waited[qn]
            if w.get(sem.uid, -1) >= val:
                continue
            w[sem.uid] = val
            self.ninstr += 1
            self.q[qn].append(lambda e, h=sem.h, v=val: e.wait_ge(h, v))

    def op(self, qn, fn, inc=None, k=1):
        self.ninstr += 1
        self.last_inc[qn] = inc is not None
        if inc is None:
            self.q[qn].append(fn)
            return None
        inc.inc(k)
        self.q[qn].append(lambda e, f=fn, h=inc.h, kk=k: f(e).then_inc(h, kk))
        return inc.tok

    def dma(self, qn, out, in_, sem, **kw):
        self.ninstr += 1
        self.last_inc[qn] = True
        sem.inc(16)
        self.q[qn].append(
            lambda e, o=out, i=in_, h=sem.h, kw=kw: e.dma_start(o, i, **kw).then_inc(h, 16))
        return sem.tok

    def build(self):
        blk = self.es.enter_context(self.nc.Block())
        for qn, attr in self.QMAP.items():
            ops = self.q[qn]
            if not ops:
                continue

            def make(ops):
                def body(eng):
                    for f in ops:
                        f(eng)
                return body
            getattr(blk, attr)(make(ops))



class Res:

    def __init__(self, p: Prog, resident_x=False, ntiles=NT):
        self.p = p
        self.bnst = p.sb("bnst", [128, 12])
        self.mv = p.sb("mv", [128, 2])
        self.t0 = p.sb("lnt0", [128, 1])
        self.rstd = p.sb("rstd", [128, 4])
        self.lnb = p.sb("lnb", [128, 4])
        self.xln_tm = p.sb("xln_tm", [128, 1024], BF16)
        self.idn = p.sb("idn", [128, 128], BF16)
        self.idf = p.sb("idf", [128, 128])
        self.ones_row = p.sb("ones_row", [1, 128], BF16)
        self.ones_f = p.sb("ones_f", [1, 128])
        self.PS0123 = p.ps("PS0123", [128, 2048])
        self.PS01 = self.PS0123[:, 0:1024]
        self.PS23 = self.PS0123[:, 1024:2048]
        self.PS45 = p.ps("PS45", [128, 1024])
        self.PT = p.ps("PT", [128, 1024], BF16)
        self.PTF = p.ps("PTF", [128, 512])
        self.pt_free = None
        self.ptf_free = None
        self.sv = p.sem("sv")
        self.ss = p.sem("ss")
        self.st = p.sem("st")
        self.sg = p.sem("sg")
        self.dmac = p.sem("dmac")
        self.dmag = p.sem("dmag")
        self.consts_tok = None

    def load_consts(self, p, iden_ap):
        p.dma("y", self.idn[:, :], iden_ap, self.dmac)
        self.consts_tok = self.dmac.tok
        p.wait("s", self.consts_tok)
        p.op("s", lambda e: e.copy(self.idf[:, :], self.idn[:, :]), inc=self.ss)
        p.op("g", lambda e: e.memset(self.ones_row[:, :], 1.0), inc=self.sg)
        p.op("g", lambda e: e.memset(self.ones_f[:, :], 1.0), inc=self.sg)
        self.ones_tok = self.sg.tok
        self.idf_tok = self.ss.tok


def emit_ln(p: Prog, r: Res, x_tile_ap, sub, dst_fm, dst_off, after=(),
            scale_mode=None, war=(), cs=512):
    sv, ss, st = r.sv, r.ss, r.st
    p.wait("v", *[a for a in after if a])
    p.op("v", lambda e: e.bn_stats(r.bnst[:, 0:6], x_tile_ap[:, 0:512]))
    t1 = p.op("v", lambda e: e.bn_stats(r.bnst[:, 6:12], x_tile_ap[:, 512:1024]), inc=sv)
    p.wait("v", t1)
    t2 = p.op("v", lambda e: e.bn_aggr(r.mv[:, 0:2], r.bnst[:, 0:12]), inc=sv)
    p.wait("v", t2)
    t3 = p.op("v", lambda e: e.tensor_scalar(r.t0[:, 0:1], r.mv[:, 1:2], LN_EPS, None,
                                             op0=ALU.add), inc=sv)
    p.wait("s", t3)
    t4 = p.op("s", lambda e: e.activation(r.t0[:, 0:1], r.t0[:, 0:1], AF.Ln), inc=ss)
    p.wait("s", t4)
    t5 = p.op("s", lambda e: e.activation(r.rstd[:, sub:sub + 1], r.t0[:, 0:1], AF.Exp,
                                          scale=-0.5), inc=ss)
    if scale_mode == "rms2":
        EPS32 = float(np.finfo(np.float32).eps)
        p.wait("v", t5)
        t5a = p.op("v", lambda e: e.scalar_tensor_tensor(
            r.t0[:, 0:1], r.mv[:, 1:2], 1.0, r.rstd[:, sub:sub + 1],
            op0=ALU.mult, op1=ALU.mult), inc=sv)
        p.wait("v", t5a)
        t5b = p.op("v", lambda e: e.scalar_tensor_tensor(
            r.t0[:, 0:1], r.t0[:, 0:1], EPS32, r.rstd[:, sub:sub + 1],
            op0=ALU.bypass, op1=ALU.mult) if False else e.tensor_tensor(
            r.t0[:, 0:1], r.t0[:, 0:1], r.rstd[:, sub:sub + 1], op=ALU.mult),
            inc=sv)
        p.wait("v", t5b)
        t5c = p.op("v", lambda e: e.tensor_scalar(r.t0[:, 0:1], r.t0[:, 0:1], EPS32, None,
                                                  op0=ALU.add), inc=sv)
        p.wait("s", t5c)
        t5d = p.op("s", lambda e: e.activation(r.t0[:, 0:1], r.t0[:, 0:1], AF.Ln), inc=ss)
        p.wait("s", t5d)
        t5e = p.op("s", lambda e: e.activation(r.t0[:, 0:1], r.t0[:, 0:1], AF.Exp,
                                               scale=-0.5), inc=ss)
        p.wait("v", t5e)
        t5 = p.op("v", lambda e: e.tensor_tensor(
            r.rstd[:, sub:sub + 1], r.rstd[:, sub:sub + 1], r.t0[:, 0:1],
            op=ALU.mult), inc=sv)
    p.wait("v", t5)
    t6 = p.op("v", lambda e: e.scalar_tensor_tensor(
        r.lnb[:, sub:sub + 1], r.mv[:, 0:1], -1.0, r.rstd[:, sub:sub + 1],
        op0=ALU.mult, op1=ALU.mult), inc=sv)
    p.wait("s", t6)
    t7 = p.op("s", lambda e: e.activation(r.xln_tm[:, :], x_tile_ap[:, 0:1024], AF.Identity,
                                          bias=r.lnb[:, sub:sub + 1],
                                          scale=r.rstd[:, sub:sub + 1]), inc=ss)
    p.wait("t", t7, r.pt_free, r.consts_tok)
    tk = None
    for dc in range(8):
        tk = p.op("t", lambda e, dc=dc: e.transpose(
            r.PT[:, dc * 128:(dc + 1) * 128], r.xln_tm[:, dc * 128:(dc + 1) * 128],
            r.idn[:, :]), inc=(st if dc == 7 else None))
    p.wait("v", tk, *[w for w in war if w])
    dst = AP(dst_fm, dst_off, [[dst_fm.shape[1], 128], [cs, 8], [1, 128]])
    cp = p.op("v", lambda e, d=dst: e.tensor_copy(d, r.PT[:, 0:1024]), inc=sv)
    r.pt_free = cp
    return cp



class XResident:
    def __init__(self, p, r, x_ap=None, y_ap=None, off=0, ntiles=16):
        self.r = r
        self._ready = None
        self.x_ap = x_ap
        self.y_ap = y_ap
        self.off = off
        self.X = p.sb("Xres", [128, ntiles * 1024], F32)
        self.addtok = {}

    def load_all(self, p, r):
        self.lsem = p.sem("xrld")
        nt = self.X.shape[1] // 1024
        for h in range(0, nt, 8):
            n = min(8, nt - h)
            tk = p.dma("y", self.X[:, h * 1024:(h + n) * 1024],
                       AP(self.x_ap, self.off + h * 128 * 1024,
                          [[1024, 128], [128 * 1024, n], [1, 1024]]),
                       self.lsem)
        self._ready = tk

    def store_all(self, p, r, waits=()):
        self.ssem = p.sem("xrst")
        nt = self.X.shape[1] // 1024
        p.wait("y", *waits)
        for h in range(0, nt, 8):
            n = min(8, nt - h)
            tk = p.dma("y", AP(self.y_ap, self.off + h * 128 * 1024,
                               [[1024, 128], [128 * 1024, n], [1, 1024]]),
                       self.X[:, h * 1024:(h + n) * 1024], self.ssem)
        return tk

    def final_toks(self):
        return [self.ssem.tok]

    def prefetch(self, p, r, i):
        pass

    def ready(self, i):
        return self._ready

    def tile(self, p, r, ti):
        return self.X[:, ti * 1024:(ti + 1) * 1024]

    def tile4(self, p, r, i):
        return [self.tile(p, r, 4 * i + s) for s in range(4)]

    def add(self, p, r, ti, psum_ap, wait=(), scale_col=None):
        xt = self.X[:, ti * 1024:(ti + 1) * 1024]
        p.wait("v", *wait, self.addtok.get(ti))
        sc = scale_col if scale_col is not None else 1.0
        tk = p.op("v", lambda e, xt=xt, ps=psum_ap, sc=sc: e.scalar_tensor_tensor(
            xt, ps, sc, xt, op0=ALU.mult, op1=ALU.add), inc=r.sv)
        self.addtok[ti] = tk
        return tk

    def add_half(self, p, r, ti, half, psum_ap, wait=(), scale_col=None):
        xt = self.X[:, ti * 1024 + half * 512:ti * 1024 + (half + 1) * 512]
        p.wait("v", *wait, self.addtok.get((ti, half)))
        sc = scale_col if scale_col is not None else 1.0
        tk = p.op("v", lambda e, xt=xt, ps=psum_ap, sc=sc: e.scalar_tensor_tensor(
            xt, ps, sc, xt, op0=ALU.mult, op1=ALU.add), inc=r.sv)
        self.addtok[(ti, half)] = tk
        return tk

    def flush(self, p, r, i):
        pass

    def loadA(self, p, r, ti):
        return self._ready

    def tileA(self, p, r, ti):
        return self.tile(p, r, ti)

    def consumedA(self, p, r, ti, tok):
        pass

    def new_pass(self):
        pass


class XStream:

    def __init__(self, p, r, x_handle, y_handle, nbuf=2, ngroups=8):
        self.xh = x_handle
        self.yh = y_handle
        self.nbuf = nbuf
        self.ng = ngroups
        self.xb = [p.sb("xs", [128, 4 * 1024]) for i in range(nbuf)]
        self.lsem = [p.sem(f"xld{i}") for i in range(nbuf)]
        self.ssem = [p.sem(f"xst{i}") for i in range(nbuf)]
        self.loaded = {}
        self.stores = {}
        self.addtok = {}

    def _dram_ap(self, h, i):
        return AP(h, (4 * i) * 128 * 1024, [[1024, 128], [128 * 1024, 4], [1, 1024]])

    def prefetch(self, p, r, i):
        if i in self.loaded or i >= self.ng:
            return
        buf = i % self.nbuf
        p.wait("y", self.stores.get(i - self.nbuf))
        self.loaded[i] = p.dma("y", self.xb[buf][:, :], self._dram_ap(self.xh, i),
                               self.lsem[buf])

    def ready(self, i):
        return self.loaded.get(i)

    def tile(self, p, r, ti):
        i, sub = divmod(ti, 4)
        return self.xb[i % self.nbuf][:, sub * 1024:(sub + 1) * 1024]

    def tile4(self, p, r, i):
        return [self.tile(p, r, 4 * i + s) for s in range(4)]

    def add(self, p, r, ti, psum_ap, wait=(), scale_col=None):
        i, sub = divmod(ti, 4)
        dst = self.tile(p, r, ti)
        p.wait("v", *wait, self.addtok.get(ti), self.loaded.get(i))
        sc = scale_col if scale_col is not None else 1.0
        tk = p.op("v", lambda e, d=dst, ps=psum_ap, sc=sc: e.scalar_tensor_tensor(
            d, ps, sc, d, op0=ALU.mult, op1=ALU.add), inc=r.sv)
        self.addtok[ti] = tk
        return tk

    def flush(self, p, r, i):
        buf = i % self.nbuf
        p.wait("y", *[self.addtok.get(i * 4 + s) for s in range(4)])
        self.stores[i] = p.dma("y", self._dram_ap(self.yh, i), self.xb[buf][:, :],
                               self.ssem[buf])

    def final_toks(self):
        return [sm.tok for sm in self.ssem]




class XPhased:

    def __init__(self, p, r, x_handle, y_handle):
        self.xh = x_handle
        self.yh = y_handle
        self.ta = [p.sb("xpa", [128, 1024]) for _ in range(2)]
        self.tasem = [p.sem(f"xpas{i}") for i in range(2)]
        self.tacons = {}
        self.taload = {}
        self.tb = [p.sb("xpb", [128, 4 * 1024]) for _ in range(1)]
        self.tbsem = [p.sem(f"xpbs{i}") for i in range(1)]
        self.tbssem = [p.sem(f"xpbo{i}") for i in range(1)]
        self.tbload = {}
        self.stores = {}
        self.prev_stores = {}
        self.pass_finals = [None]
        self.addtok = {}

    def new_pass(self):
        self.tbload = {}
        self.prev_stores = self.stores
        self.pass_finals = [sm.tok for sm in self.tbssem]
        self.stores = {}
        self.addtok = {}
        self.xh = self.yh

    def loadA(self, p, r, ti):
        if ti in self.taload or ti >= NT:
            return self.taload.get(ti)
        bi = ti % 2
        p.wait("y", self.tacons.get(ti - 2))
        self.taload[ti] = p.dma("y", self.ta[bi][:, :],
                                AP(self.xh, ti * 128 * 1024, [[1024, 128], [1, 1024]]),
                                self.tasem[bi])
        return self.taload[ti]

    def tileA(self, p, r, ti):
        return self.ta[ti % 2]

    def consumedA(self, p, r, ti, tok):
        self.tacons[ti] = tok

    def _loadB(self, p, r, jq):
        if jq in self.tbload:
            return self.tbload[jq]
        bi = 0
        p.wait("y", self.stores.get(jq - 1), self.prev_stores.get(jq),
               self.pass_finals[bi])
        self.tbload[jq] = p.dma(
            "y", self.tb[bi][:, :],
            AP(self.xh, jq * 4 * 128 * 1024, [[1024, 128], [128 * 1024, 4], [1, 1024]]),
            self.tbsem[bi])
        return self.tbload[jq]

    def add(self, p, r, ti, psum_ap, wait=(), scale_col=None):
        jq, s4 = divmod(ti, 4)
        ld = self._loadB(p, r, jq)
        bi = 0
        dst = self.tb[bi][:, s4 * 1024:(s4 + 1) * 1024]
        p.wait("v", *wait, ld, self.addtok.get(ti))
        sc = scale_col if scale_col is not None else 1.0
        tk = p.op("v", lambda e, d=dst, ps=psum_ap, sc=sc: e.scalar_tensor_tensor(
            d, ps, sc, d, op0=ALU.mult, op1=ALU.add), inc=r.sv)
        self.addtok[ti] = tk
        if s4 == 3:
            p.wait("y", tk)
            self.stores[jq] = p.dma(
                "y", AP(self.yh, jq * 4 * 128 * 1024,
                        [[1024, 128], [128 * 1024, 4], [1, 1024]]),
                self.tb[bi][:, :], self.tbssem[bi])
        return tk

    def final_toks(self):
        return [s.tok for s in self.tbssem]



def prep_transpose(p: Prog, r: Res, jobs, waits=(), standalone=True,
                   queues=("y",)):
    es2 = ExitStack() if standalone else None
    nq = len(queues)
    tbuf = [p.sb(f"tts{i}", [128, 2048], BF16, es=es2) for i in range(2 * nq)]
    lsem = [p.sem(f"ttl{i}") for i in range(2 * nq)]
    ssem = p.sem("ttst")
    stored = [None] * (2 * nq)
    for q in queues:
        p.wait(q, *waits)
    i = 0
    last = None
    for job in jobs:
        (src, R, C, dst) = job[:4]
        if len(job) > 4:
            for q in queues:
                p.wait(q, *job[4])
        rchunk = 2048
        while R % rchunk:
            rchunk //= 2
        assert rchunk % 128 == 0
        for c0 in range(0, C, 128):
            for r0 in range(0, R, rchunk):
                bi = i % (2 * nq)
                q = queues[i % nq]
                i += 1
                p.wait(q, stored[bi])
                ld = p.dma(q, tbuf[bi][:, 0:rchunk],
                           src.ap()[r0:r0 + rchunk, c0:c0 + 128], lsem[bi],
                           transpose=True)
                p.wait("g", ld)
                stored[bi] = p.dma("g", dst.ap()[c0:c0 + 128, r0:r0 + rchunk],
                                   tbuf[bi][:, 0:rchunk], ssem)
                last = stored[bi]
    if standalone:
        p.barrier(p.all_toks())
        es2.close()
    return last



def prep_l1ffn_alloc(p: Prog):
    return {"b1sb": p.sb("b1sb", [128, 32]),
            "b2row": p.sb("b2row", [1, 1024], BF16)}


def prep_l1ffn(p: Prog, r: Res, ins, scr, pr=None, standalone=True):
    if pr is None:
        pr = prep_l1ffn_alloc(p)
    b1sb, b2row = pr["b1sb"], pr["b2row"]
    es2 = ExitStack() if standalone else None
    wtmp = [p.sb("pw", [128, 1024], es=es2) for _ in range(2)]
    wsg = p.sb("pwsg", [128, 1024], BF16, es=es2)
    wout = p.sb("pwo", [128, 1024], BF16, es=es2)
    asum = p.sb("pasum", [128, 5], es=es2)
    lsem = [p.sem(f"pld{i}") for i in range(2)]
    stsem = p.sem("pst")
    out_free = None
    sg_free = None
    jobs = [("w1", scr["bw1"], 32, 1), ("w2", scr["bw2"], 8, 4)]
    last_store = None
    buf_users = [None, None]
    for wname, scr_t, nchunk, npiece in jobs:
        w_ap = ins[wname]
        for c in range(nchunk):
            red = []
            for pc in range(npiece):
                buf = wtmp[pc % 2]
                p.wait("y", buf_users[pc % 2])
                ld = p.dma("y", buf[:, :],
                           AP(w_ap, c * 128 * (1024 * npiece) + pc * 1024,
                              [[1024 * npiece, 128], [1, 1024]]), lsem[pc % 2])
                p.wait("v", ld)
                red.append(p.op("v", lambda e, b=buf, pc=pc: e.tensor_reduce(
                    asum[:, pc + 1:pc + 2], b[:, :], axis=AX.X, op=ALU.add,
                    apply_absolute_value=True), inc=r.sv))
                buf_users[pc % 2] = red[-1]
            p.wait("v", red[-1])
            if npiece == 1:
                t = p.op("v", lambda e: e.tensor_copy(asum[:, 0:1], asum[:, 1:2]),
                         inc=r.sv)
            else:
                t = p.op("v", lambda e, np_=npiece: e.tensor_reduce(
                    asum[:, 0:1], asum[:, 1:1 + np_], axis=AX.X, op=ALU.add), inc=r.sv)
            for pc in range(npiece):
                buf = wtmp[pc % 2]
                p.wait("y", buf_users[pc % 2])
                ld = p.dma("y", buf[:, :],
                           AP(w_ap, c * 128 * (1024 * npiece) + pc * 1024,
                              [[1024 * npiece, 128], [1, 1024]]), lsem[pc % 2])
                p.wait("s", ld, sg_free)
                t2 = p.op("s", lambda e, b=buf: e.activation(wsg[:, :], b[:, :], AF.Sign),
                          inc=r.ss)
                buf_users[pc % 2] = t2
                p.wait("v", t2, t, out_free)
                t3 = p.op("v", lambda e, w=1024 * npiece: e.tensor_scalar(
                    wout[:, :], wsg[:, :], asum[:, 0:1], 1.0 / w,
                    op0=ALU.mult, op1=ALU.mult), inc=r.sv)
                sg_free = t3
                p.wait("y", t3)
                last_store = p.dma("y", AP(scr_t, c * 128 * (1024 * npiece) + pc * 1024,
                                           [[1024 * npiece, 128], [1, 1024]]),
                                   wout[:, :], stsem)
                out_free = last_store
    b1t = p.sb("pb1t", [32, 128], es=es2)
    bsem = p.sem("pb1")
    ldb = p.dma("y", b1t[:, :], AP(ins["b1"], 0, [[128, 32], [1, 128]]), bsem)
    p.wait("t", ldb, r.idf_tok, r.ptf_free)
    tk = p.op("t", lambda e: e.transpose(r.PTF[:, 0:32], b1t[:, :], r.idf[0:32, 0:32]),
              inc=r.st)
    p.wait("v", tk)
    tk2 = p.op("v", lambda e: e.tensor_copy(b1sb[:, 0:32], r.PTF[:, 0:32]), inc=r.sv)
    r.ptf_free = tk2
    b2tok = p.dma("g", b2row[:, :], AP(ins["b2"], 0, [[1024, 1], [1, 1024]]), r.dmag)
    if standalone:
        p.barrier(p.all_toks())
        es2.close()
    pr.update({"wdone": last_store, "consts": b2tok, "b1tok": tk2})
    return pr


def mk_l1ffn_bufs(p: Prog):
    return {
        "w1s": [p.sb("w1s", [128, 8 * 512], BF16) for _ in range(2)],
        "w2h": p.sb("w2h", [128, 16 * 1024], BF16),
        "abuf": p.sb("abuf", [128, 32 * 512], BF16),
        "xlnf": p.sb("xlnf", [128, 8 * 512], BF16),
    }


def emit_l1ffn(p: Prog, r: Res, bufs, scr, pr, xio):
    sv, ss, st = r.sv, r.ss, r.st
    w1s, w2h, abuf, xlnf = bufs["w1s"], bufs["w2h"], bufs["abuf"], bufs["xlnf"]
    dma1 = [p.sem(f"w1d{i}") for i in range(2)]
    dma2 = p.sem("w2d")
    w1_loaded = [None, None]
    w1_used = [None, None]
    w2_loaded = None
    w2_used = None

    def load_w1_slice(sl):
        nonlocal w1_loaded
        buf = sl % 2
        p.wait("y", w1_used[buf], pr["wdone"])
        tk = None
        for dc in range(8):
            tk = p.dma("y", w1s[buf][:, dc * 512:(dc + 1) * 512],
                       scr["bw1T"].ap()[dc * 128:(dc + 1) * 128,
                                        sl * 512:(sl + 1) * 512], dma1[buf])
        w1_loaded[buf] = tk

    def load_w2_half(h):
        nonlocal w2_loaded
        p.wait("y", w2_used, pr["wdone"])
        tk = None
        for fc in range(16):
            j = h * 16 + fc
            tk = p.dma("y", w2h[:, fc * 1024:(fc + 1) * 1024],
                       scr["bw2T"].ap()[j * 128:(j + 1) * 128, :], dma2)
        w2_loaded = tk

    PH = r.PS45
    PY = [r.PS01, r.PS23]
    ph_free = [None, None]
    py_free = [None, None]

    load_w1_slice(0)
    xio.prefetch(p, r, 0)
    abuf_free = None
    xlnf_free = None
    for i in range(8):
        xio.prefetch(p, r, i + 1)
        xt = xio.tile4(p, r, i)
        lntok = None
        for sub in range(4):
            lntok = emit_ln(p, r, xt[sub], sub, xlnf, sub * 128,
                            after=(xio.ready(i),), war=(xlnf_free,))
        for sl in range(8):
            buf = sl % 2
            if sl < 7 or i < 7:
                load_w1_slice((sl + 1) % 8)
            for j4 in range(4):
                j = sl * 4 + j4
                par = j % 2
                ph = PH[:, par * 512:(par + 1) * 512]
                p.wait("t", w1_loaded[buf], lntok, ph_free[par])
                tk = None
                for dc in range(8):
                    tk = p.op("t", lambda e, b=buf, dc=dc, j4=j4, ph=ph: e.matmul(
                        ph, w1s[b][:, dc * 512 + j4 * 128: dc * 512 + (j4 + 1) * 128],
                        xlnf[:, dc * 512:(dc + 1) * 512],
                        start=(dc == 0), stop=(dc == 7)),
                        inc=(st if dc == 7 else None))
                p.wait("s", tk, abuf_free)
                gk = p.op("s", lambda e, j=j, ph=ph: e.activation(
                    abuf[:, j * 512:(j + 1) * 512], ph, AF.Gelu,
                    bias=pr["b1sb"][:, j:j + 1]), inc=ss)
                ph_free[par] = gk
            w1_used[buf] = st.tok
        xlnf_free = st.tok
        for h in range(2):
            load_w2_half(h)
            for sub in range(4):
                par = sub % 2
                py = PY[par]
                p.wait("t", w2_loaded, ss.tok, py_free[par])
                started = [False, False]
                if h == 0:
                    p.wait("t", pr["consts"], r.ones_tok)
                    for nh in range(2):
                        p.op("t", lambda e, nh=nh, py=py: e.matmul(
                            py[:, nh * 512:(nh + 1) * 512], r.ones_row[0:1, :],
                            pr["b2row"][0:1, nh * 512:(nh + 1) * 512],
                            start=True, stop=False))
                    started = [True, True]
                tk = None
                for fc in range(16):
                    j = h * 16 + fc
                    for nh in range(2):
                        tk = p.op("t", lambda e, fc=fc, j=j, sub=sub, nh=nh, py=py,
                                  st_=(not started[nh] and fc == 0): e.matmul(
                            py[:, nh * 512:(nh + 1) * 512],
                            abuf[:, j * 512 + sub * 128: j * 512 + (sub + 1) * 128],
                            w2h[:, fc * 1024 + nh * 512: fc * 1024 + (nh + 1) * 512],
                            start=st_, stop=(fc == 15)),
                            inc=(st if (fc == 15 and nh == 1) else None))
                py_free[par] = xio.add(p, r, i * 4 + sub, py[:, 0:1024], wait=(tk,))
            w2_used = st.tok
        abuf_free = st.tok
        xio.flush(p, r, i)
    return



def np_consts():
    import ml_dtypes
    return {"c_iden": np.eye(128, dtype=ml_dtypes.bfloat16)}


def build_l1ffn_program():
    nc = bass.Bass("TRN2", target_bir_lowering=False)
    es = ExitStack()
    with es:
        p = Prog(nc, es)
        ins = {
            "x": p.dram("x", [T, D], kind="ExternalInput"),
            "w1": p.dram("w1", [4096, 1024], kind="ExternalInput"),
            "b1": p.dram("b1", [4096], kind="ExternalInput"),
            "w2": p.dram("w2", [1024, 4096], kind="ExternalInput"),
            "b2": p.dram("b2", [1024], kind="ExternalInput"),
            "c_iden": p.dram("c_iden", [128, 128], BF16, kind="ExternalInput"),
        }
        y = p.dram("y", [T, D], kind="ExternalOutput")
        scr = {
            "bw1": p.dram("bw1", [4096, 1024], BF16),
            "bw2": p.dram("bw2", [1024, 4096], BF16),
            "bw1T": p.dram("bw1T", [1024, 4096], BF16),
            "bw2T": p.dram("bw2T", [4096, 1024], BF16),
        }
        r = Res(p, resident_x=False)
        r.load_consts(p, ins["c_iden"].ap())
        pr = prep_l1ffn(p, r, ins, scr)
        prep_transpose(p, r, [(scr["bw1"], 4096, 1024, scr["bw1T"]),
                              (scr["bw2"], 1024, 4096, scr["bw2T"])])
        bufs = mk_l1ffn_bufs(p)
        xio = XStream(p, r, ins["x"], y)
        emit_l1ffn(p, r, bufs, scr, pr, xio)
        p.wait("y", *xio.final_toks())
        p.build()
    return nc, p.ninstr



def prep_moe_alloc(p: Prog):
    return {"multT": p.sb("multT", [128, 15 * 8 + 16]),
            "fc1b": p.sb("fc1b", [128, 15 * 16 + 32]),
            "gwT": p.sb("gwT", [128, 8 * 15], BF16),
            "bmat": p.sb("bmat", [16, 1024], BF16)}


def prep_moe(p: Prog, r: Res, ins, scr, pr=None, standalone=True):
    lsem = [p.sem(f"mld{i}") for i in range(2)]
    stsem = p.sem("mst")
    csem = p.sem("mcs")
    if pr is None:
        pr = prep_moe_alloc(p)
    multT, fc1b, gwT, bmat = pr["multT"], pr["fc1b"], pr["gwT"], pr["bmat"]
    es2 = ExitStack() if standalone else None
    wtmp = [p.sb("mpw", [128, 1024], es=es2) for _ in range(2)]
    wob = p.sb("mpo", [128, 1024], BF16, es=es2)
    mrow = p.sb("mrow", [16, 128], es=es2)
    grow = p.sb("grow", [15, 1024], es=es2)
    buf_users = [None, None]
    out_free = None
    last_store = None
    tposed = None
    for e in range(16):
        if e < 15:
            src = AP(ins["geglu_mult"], e * 1024, [[128, 8], [1, 128]])
            ncol, dst0 = 8, e * 8
        else:
            src = AP(ins["s_geglu_mult"], 0, [[128, 16], [1, 128]])
            ncol, dst0 = 16, 15 * 8
        p.wait("y", tposed)
        ld = p.dma("y", mrow[0:ncol, :], src, csem)
        p.wait("t", ld, r.idf_tok, r.ptf_free)
        tk = p.op("t", lambda e2, n=ncol: e2.transpose(
            r.PTF[:, 0:n], mrow[0:n, :], r.idf[0:n, 0:n]), inc=r.st)
        p.wait("v", tk)
        tposed = p.op("v", lambda e2, d0=dst0, n=ncol: e2.tensor_copy(
            multT[:, d0:d0 + n], r.PTF[:, 0:n]), inc=r.sv)
        r.ptf_free = tposed

    for e in range(15):
        for j in range(8):
            bi = j % 2
            p.wait("y", buf_users[bi])
            ld = p.dma("y", wtmp[bi][:, :],
                       AP(ins["fc1_w"], (e * 2048 + j * 128) * 1024,
                          [[1024, 128], [1, 1024]]), lsem[bi])
            p.wait("v", ld, tposed, out_free)
            t3 = p.op("v", lambda e2, b=bi, e_=e, j_=j: e2.tensor_scalar(
                wob[:, :], wtmp[b][:, :], multT[:, e_ * 8 + j_:e_ * 8 + j_ + 1], None,
                op0=ALU.mult), inc=r.sv)
            buf_users[bi] = t3
            p.wait("y", t3)
            last_store = p.dma("y", AP(scr["fc1"], (e * 2048 + j * 128) * 1024,
                                       [[1024, 128], [1, 1024]]), wob[:, :], stsem)
            out_free = last_store
        p.dma("g", AP(scr["fc1"], (e * 2048 + 1024) * 1024, [[1024, 1024], [1, 1024]]),
              AP(ins["fc1_w"], (e * 2048 + 1024) * 1024, [[1024, 1024], [1, 1024]]),
              r.dmag)
    jtok = {"fc1": (last_store, r.dmag.tok)}
    for j in range(16):
        bi = j % 2
        p.wait("y", buf_users[bi])
        ld = p.dma("y", wtmp[bi][:, :],
                   AP(ins["s_fc1_w"], (j * 128) * 1024, [[1024, 128], [1, 1024]]),
                   lsem[bi])
        p.wait("v", ld, tposed, out_free)
        t3 = p.op("v", lambda e2, b=bi, j_=j: e2.tensor_scalar(
            wob[:, :], wtmp[b][:, :], multT[:, 120 + j_:120 + j_ + 1], None,
            op0=ALU.mult), inc=r.sv)
        buf_users[bi] = t3
        p.wait("y", t3)
        last_store = p.dma("y", AP(scr["s_fc1"], (j * 128) * 1024,
                                   [[1024, 128], [1, 1024]]), wob[:, :], stsem)
        out_free = last_store
    p.dma("g", AP(scr["s_fc1"], 2048 * 1024, [[1024, 2048], [1, 1024]]),
          AP(ins["s_fc1_w"], 2048 * 1024, [[1024, 2048], [1, 1024]]), r.dmag)
    jtok["s_fc1"] = (last_store, r.dmag.tok)
    p.dma("g", AP(scr["fc2"], 0, [[1024, 15 * 1024], [1, 1024]]),
          AP(ins["fc2_w"], 0, [[1024, 15 * 1024], [1, 1024]]), r.dmag)
    jtok["fc2"] = (r.dmag.tok,)
    p.dma("g", AP(scr["s_fc2"], 0, [[2048, 1024], [1, 2048]]),
          AP(ins["s_fc2_w"], 0, [[2048, 1024], [1, 2048]]), r.dmag)
    jtok["s_fc2"] = (r.dmag.tok,)

    tp2 = tposed

    def thin_cols(src_ap, dst0, n):
        nonlocal tp2
        p.wait("y", tp2)
        ld = p.dma("y", mrow[0:n, :], src_ap, csem)
        p.wait("t", ld, r.ptf_free)
        tk = p.op("t", lambda e2, n=n: e2.transpose(r.PTF[:, 0:n], mrow[0:n, :],
                                                    r.idf[0:n, 0:n]), inc=r.st)
        p.wait("v", tk)
        tp2 = p.op("v", lambda e2, d0=dst0, n=n: e2.tensor_copy(
            fc1b[:, d0:d0 + n], r.PTF[:, 0:n]), inc=r.sv)
        r.ptf_free = tp2
        return tp2

    for e in range(15):
        thin_cols(AP(ins["fc1_b"], e * 2048, [[128, 16], [1, 128]]), e * 16, 16)
    thin_cols(AP(ins["s_fc1_b"], 0, [[128, 16], [1, 128]]), 240, 16)
    thin_cols(AP(ins["s_fc1_b"], 2048, [[128, 16], [1, 128]]), 256, 16)
    bs = tp2
    for e in range(15):
        p.wait("v", bs)
        bs = p.op("v", lambda e2, e_=e: e2.tensor_tensor(
            fc1b[:, e_ * 16:e_ * 16 + 8], fc1b[:, e_ * 16:e_ * 16 + 8],
            multT[:, e_ * 8:e_ * 8 + 8], op=ALU.mult), inc=r.sv)
    p.wait("v", bs)
    bs = p.op("v", lambda e2: e2.tensor_tensor(
        fc1b[:, 240:256], fc1b[:, 240:256], multT[:, 120:136], op=ALU.mult), inc=r.sv)

    p.wait("y", bs, tp2)
    ldg = p.dma("y", grow[:, :], AP(ins["gate_w"], 0, [[1024, 15], [1, 1024]]), csem)
    gt = None
    for dc in range(8):
        p.wait("t", ldg, r.ptf_free)
        tk = p.op("t", lambda e2, dc=dc: e2.transpose(
            r.PTF[:, 0:15], grow[0:15, dc * 128:(dc + 1) * 128], r.idf[0:15, 0:15]),
            inc=r.st)
        p.wait("v", tk, gt)
        gt = p.op("v", lambda e2, dc=dc: e2.tensor_copy(
            gwT[:, dc * 15:(dc + 1) * 15], r.PTF[:, 0:15]), inc=r.sv)
        r.ptf_free = gt

    p.dma("g", bmat[0:15, :], AP(ins["fc2_b"], 0, [[1024, 15], [1, 1024]]), r.dmag)
    p.dma("g", bmat[15:16, :], AP(ins["s_fc2_b"], 0, [[1024, 1], [1, 1024]]), r.dmag)

    if standalone:
        es2.close()
        p.barrier([r.sv.tok, r.ss.tok, r.st.tok, r.sg.tok, lsem[0].tok,
                   lsem[1].tok, stsem.tok, csem.tok, r.dmag.tok, r.dmac.tok])
    pr.update({"wdone": last_store, "castdone": r.dmag.tok, "biasdone": bs,
               "gatedone": gt, "jtok": jtok})
    return pr


def prep_moe_pool(p: Prog, r: Res, ins, scr, pr, pacer=()):
    multT, fc1b, gwT, bmat = pr["multT"], pr["fc1b"], pr["gwT"], pr["bmat"]
    gsem = p.sem("mpg")
    ldsem = p.sem("mpl")
    stsem = p.sem("mps")
    csem = p.sem("mpc")
    wtmp = [p.sb(f"mpw{i}", [128, 1024]) for i in range(2)]
    wob = [p.sb(f"mpo{i}", [128, 1024], BF16) for i in range(2)]
    TOTAL = 648 + 90 + 136 * 8
    state = {"step": 0, "next": 0}

    def pace(w):
        state["step"] += w
        if pacer and state["step"] >= state["next"]:
            j = min(len(pacer) - 1, state["step"] * len(pacer) // TOTAL)
            p.wait("g", pacer[j])
            state["next"] = (j + 1) * TOTAL // len(pacer)
    for e in range(15):
        pace(5)
        p.dma("g", AP(scr["fc1"], (e * 2048 + 1024) * 1024,
                      [[1024, 1024], [1, 1024]]),
              AP(ins["fc1_w"], (e * 2048 + 1024) * 1024,
                 [[1024, 1024], [1, 1024]]), r.dmag)
    pace(5)
    p.dma("g", AP(scr["s_fc1"], 2048 * 1024, [[1024, 2048], [1, 1024]]),
          AP(ins["s_fc1_w"], 2048 * 1024, [[1024, 2048], [1, 1024]]), r.dmag)
    pace(5)
    p.dma("g", AP(scr["fc2"], 0, [[1024, 15 * 1024], [1, 1024]]),
          AP(ins["fc2_w"], 0, [[1024, 15 * 1024], [1, 1024]]), r.dmag)
    pace(5)
    p.dma("g", AP(scr["s_fc2"], 0, [[2048, 1024], [1, 2048]]),
          AP(ins["s_fc2_w"], 0, [[2048, 1024], [1, 2048]]), r.dmag)
    tk = None
    for e in range(15):
        for j in range(8):
            pace(1)
            tk = p.dma("g", multT[:, e * 8 + j:e * 8 + j + 1],
                       AP(ins["geglu_mult"], e * 1024 + j * 128,
                          [[1, 128], [1, 1]]), csem)
    for j in range(16):
        pace(1)
        tk = p.dma("g", multT[:, 120 + j:121 + j],
                   AP(ins["s_geglu_mult"], j * 128, [[1, 128], [1, 1]]), csem)
    multT_tok = tk
    for e in range(15):
        for j in range(16):
            pace(1)
            f0 = j * 128 if j < 8 else 1024 + (j - 8) * 128
            tk = p.dma("g", fc1b[:, e * 16 + j:e * 16 + j + 1],
                       AP(ins["fc1_b"], e * 2048 + f0, [[1, 128], [1, 1]]), csem)
    for j in range(16):
        pace(1)
        tk = p.dma("g", fc1b[:, 240 + j:241 + j],
                   AP(ins["s_fc1_b"], j * 128, [[1, 128], [1, 1]]), csem)
    for j in range(16):
        pace(1)
        tk = p.dma("g", fc1b[:, 256 + j:257 + j],
                   AP(ins["s_fc1_b"], 2048 + j * 128, [[1, 128], [1, 1]]), csem)
    p.wait("g", tk, multT_tok)
    bs = None
    for e in range(15):
        bs = p.op("g", lambda e2, e_=e: e2.tensor_tensor(
            fc1b[:, e_ * 16:e_ * 16 + 8], fc1b[:, e_ * 16:e_ * 16 + 8],
            multT[:, e_ * 8:e_ * 8 + 8], op=ALU.mult), inc=gsem)
    bs = p.op("g", lambda e2: e2.tensor_tensor(
        fc1b[:, 240:256], fc1b[:, 240:256], multT[:, 120:136], op=ALU.mult),
        inc=gsem)
    gt = None
    for dc in range(8):
        for e in range(15):
            pace(1)
            gt = p.dma("g", gwT[:, dc * 15 + e:dc * 15 + e + 1],
                       AP(ins["gate_w"], e * 1024 + dc * 128, [[1, 128], [1, 1]]),
                       csem)
    p.dma("g", bmat[0:15, :], AP(ins["fc2_b"], 0, [[1024, 15], [1, 1024]]), r.dmag)
    p.dma("g", bmat[15:16, :], AP(ins["s_fc2_b"], 0, [[1024, 1], [1, 1024]]),
          r.dmag)
    chunks = [(ins["fc1_w"], (e * 2048 + j * 128) * 1024, e * 8 + j,
               scr["fc1"], (e * 2048 + j * 128) * 1024)
              for e in range(15) for j in range(8)]
    chunks += [(ins["s_fc1_w"], (j * 128) * 1024, 120 + j,
                scr["s_fc1"], (j * 128) * 1024) for j in range(16)]
    loads = {}

    def post_load(k):
        if k >= len(chunks):
            return
        src_t, src_off = chunks[k][0], chunks[k][1]
        loads[k] = p.dma("g", wtmp[k % 2][:, :],
                         AP(src_t, src_off, [[1024, 128], [1, 1024]]), ldsem)
    post_load(0)
    last_store = None
    wob_store = [None, None]
    for k in range(len(chunks)):
        pace(8)
        post_load(k + 1)
        bi = k % 2
        _, _, mcol, dst_t, dst_off = chunks[k]
        p.wait("g", loads[k], multT_tok, wob_store[bi])
        p.op("g", lambda e2, b=bi, c=mcol: e2.tensor_scalar(
            wob[b][:, :], wtmp[b][:, :], multT[:, c:c + 1], None,
            op0=ALU.mult), inc=gsem)
        last_store = p.dma("g", AP(dst_t, dst_off, [[1024, 128], [1, 1024]]),
                           wob[bi][:, :], stsem)
        wob_store[bi] = last_store
    pr.update({"wdone": last_store, "castdone": r.dmag.tok, "biasdone": bs,
               "gatedone": gt})
    return pr


def emit_moe(p: Prog, r: Res, scr, pr, xio, tiles):
    sv, ss, st = r.sv, r.ss, r.st
    nsub = len(tiles)
    ng = nsub // 4
    xscr = scr["xln"]
    carr = p.sb("carr", [128, nsub * 16])
    cT = p.sb("cT", [16, nsub * 128], BF16)
    work = p.sb("mwork", [128, 15])
    mtmp = p.sb("mmask", [128, 15])
    cbuf = p.sb("cbuf", [128, 15])
    ctmp2 = p.sb("ctmp2", [128, 16])
    cns = p.sb("cns", [128, 4])
    xlnf = p.sb("xlnf_moe", [128, 8 * 128], BF16)
    xsem = p.sem("xlw")
    mk = p.op("g", lambda e: e.memset(ctmp2[:, 15:16], 1.0), inc=r.sg)

    lgp = r.PS45
    lg_free = None
    ct_free = mk
    xst_prev = None
    for s_i, ti in enumerate(tiles):
        lntok = emit_ln(p, r, xio.tile(p, r, ti), s_i % 4, xlnf, 0,
                        after=(xio.ready(ti // 4), pr["wdone"], pr["castdone"],
                               pr["biasdone"], pr["gatedone"]),
                        war=(xst_prev, lg_free), cs=128)
        p.wait("y", lntok)
        xst_prev = p.dma("y", AP(xscr, s_i * 128,
                                 [[nsub * 128, 128], [128 * nsub * 128, 8], [1, 128]]),
                         xlnf[:, :], xsem)
        p.wait("t", lntok, lg_free)
        tk = None
        for dc in range(8):
            tk = p.op("t", lambda e, dc=dc: e.matmul(
                lgp[:, 0:15], xlnf[:, dc * 128:(dc + 1) * 128],
                pr["gwT"][:, dc * 15:(dc + 1) * 15],
                start=(dc == 0), stop=(dc == 7)), inc=(st if dc == 7 else None))
        p.wait("v", tk)
        q = [None]

        def vstep(fn):
            p.wait("v", q[0])
            q[0] = p.op("v", fn, inc=sv)
            return q[0]

        vstep(lambda e: e.reduce_max(cns[:, 0:1], lgp[:, 0:15], axis=AX.X))
        vstep(lambda e: e.tensor_scalar(cns[:, 1:2], cns[:, 0:1], -1.0, None,
                                        op0=ALU.mult))
        p.wait("s", q[0])
        t3 = p.op("s", lambda e: e.activation(cbuf[:, 0:15], lgp[:, 0:15], AF.Exp,
                                              bias=cns[:, 1:2]), inc=ss)
        lg_free = t3
        p.wait("v", t3)
        q[0] = t3
        vstep(lambda e: e.tensor_copy(work[:, :], cbuf[:, 0:15]))
        vstep(lambda e: e.reduce_max(cns[:, 0:1], work[:, :], axis=AX.X))
        vstep(lambda e: e.tensor_scalar(mtmp[:, :], work[:, :], cns[:, 0:1], None,
                                        op0=ALU.is_ge))
        vstep(lambda e: e.scalar_tensor_tensor(work[:, :], mtmp[:, :], -2.0,
                                               work[:, :], op0=ALU.mult, op1=ALU.add))
        vstep(lambda e: e.reduce_max(cns[:, 0:1], work[:, :], axis=AX.X))
        vstep(lambda e: e.tensor_scalar(mtmp[:, :], work[:, :], cns[:, 0:1], None,
                                        op0=ALU.is_ge))
        vstep(lambda e: e.scalar_tensor_tensor(work[:, :], mtmp[:, :], -2.0,
                                               work[:, :], op0=ALU.mult, op1=ALU.add))
        vstep(lambda e: e.reduce_max(cns[:, 2:3], work[:, :], axis=AX.X))
        vstep(lambda e: e.tensor_scalar(mtmp[:, :], cbuf[:, 0:15], cns[:, 2:3], None,
                                        op0=ALU.is_ge))
        vstep(lambda e: e.tensor_tensor(cbuf[:, 0:15], cbuf[:, 0:15], mtmp[:, :],
                                        op=ALU.mult))
        vstep(lambda e: e.reduce_sum(cns[:, 3:4], cbuf[:, 0:15], axis=AX.X))
        vstep(lambda e: e.reciprocal(cns[:, 3:4], cns[:, 3:4]))
        vstep(lambda e, s_=s_i: e.tensor_scalar(carr[:, s_ * 16:s_ * 16 + 15],
                                                cbuf[:, 0:15], cns[:, 3:4], None,
                                                op0=ALU.mult))
        p.wait("v", ct_free)
        t16 = vstep(lambda e, s_=s_i: e.tensor_copy(
            ctmp2[:, 0:15], carr[:, s_ * 16:s_ * 16 + 15]))
        p.wait("t", t16, r.ptf_free)
        t17 = p.op("t", lambda e: e.transpose(r.PTF[0:16, 0:128], ctmp2[:, 0:16],
                                              r.idf[:, :]), inc=st)
        p.wait("v", t17)
        ct_free = p.op("v", lambda e, s_=s_i: e.tensor_copy(
            cT[:, s_ * 128:(s_ + 1) * 128], r.PTF[0:16, 0:128]), inc=sv)
        r.ptf_free = ct_free

    xlt = [p.sb("xlt", [128, 8 * 512], BF16) for _ in range(2)]
    xlsem = [p.sem(f"xl2{i}") for i in range(2)]
    xl_used = [None, None]
    xl_loaded = {}

    def load_xln(tg):
        if tg in xl_loaded or tg >= ng:
            return
        bi = tg % 2
        p.wait("y", xl_used[bi], xst_prev)
        xl_loaded[tg] = p.dma(
            "y", xlt[bi][:, :],
            AP(xscr, tg * 512, [[nsub * 128, 128], [128 * nsub * 128, 8], [1, 512]]),
            xlsem[bi])

    gel = [p.sb(f"gel{i}", [128, 512], BF16) for i in range(2)]
    PHs = [r.PS45, r.PS23]
    PY = r.PS01
    ph_free = [(lg_free,), ()]
    prev_ak = [None, None]
    py_free = [None, None]

    def fc1_chunk(wbuf, ws, col_xh, col_g, bias_xh, bias_g, xbi, dst_a, waitw,
                  a_war=None, slot=0):
        PH = PHs[slot]
        p.wait("t", waitw, xl_loaded.get(xbi[0]), *ph_free[slot])
        tkx = tkg = None
        for dc in range(8):
            tkx = p.op("t", lambda e, dc=dc, w=wbuf, ws=ws, c=col_xh, b=xbi[1],
                       PH=PH: e.matmul(
                PH[:, 0:512], w[:, dc * ws + c:dc * ws + c + 128],
                xlt[b][:, dc * 512:(dc + 1) * 512],
                start=(dc == 0), stop=(dc == 7)), inc=(st if dc == 7 else None))
        for dc in range(8):
            tkg = p.op("t", lambda e, dc=dc, w=wbuf, ws=ws, c=col_g, b=xbi[1],
                       PH=PH: e.matmul(
                PH[:, 512:1024], w[:, dc * ws + c:dc * ws + c + 128],
                xlt[b][:, dc * 512:(dc + 1) * 512],
                start=(dc == 0), stop=(dc == 7)), inc=(st if dc == 7 else None))
        p.wait("s", tkg, prev_ak[slot])
        gk = p.op("s", lambda e, PH=PH, slot=slot: e.activation(
            gel[slot][:, :], PH[:, 512:1024], AF.Gelu, bias=bias_g), inc=ss)
        p.wait("v", tkx, gk, a_war)
        ak = p.op("v", lambda e, d=dst_a, PH=PH, slot=slot: e.scalar_tensor_tensor(
            d, PH[:, 0:512], bias_xh, gel[slot][:, :],
            op0=ALU.add, op1=ALU.mult), inc=sv)
        ph_free[slot] = (gk, ak)
        prev_ak[slot] = ak
        return ak

    def fc2_sub(tg, sub, nj, lhs_a, w2buf, scale_col, bias_lhs=None, bias_rhs=None):
        ti = tiles[tg * 4 + sub]
        py = PY
        for nh in range(2):
            p.wait("t", py_free[nh], pr["castdone"])
            first = True
            if bias_lhs is not None:
                p.op("t", lambda e, nh=nh, py=py: e.matmul(
                    py[:, nh * 512:(nh + 1) * 512], bias_lhs,
                    bias_rhs[:, nh * 512:(nh + 1) * 512], start=True, stop=False))
                first = False
            tk = None
            for j in range(nj):
                tk = p.op("t", lambda e, j=j, nh=nh, py=py, fs=(first and j == 0):
                          e.matmul(
                    py[:, nh * 512:(nh + 1) * 512],
                    lhs_a(j, sub),
                    w2buf[:, j * 1024 + nh * 512: j * 1024 + (nh + 1) * 512],
                    start=fs, stop=(j == nj - 1)),
                    inc=(st if j == nj - 1 else None))
            py_free[nh] = xio.add_half(p, r, ti, nh, py[:, nh * 512:(nh + 1) * 512],
                                       wait=(tk,), scale_col=scale_col)
        return py_free[1]

    asb = p.sb("asb", [128, 16 * 512], BF16)
    es_sh = ExitStack()
    s1b = [p.sb("s1b", [128, 8 * 1024], BF16, es=es_sh) for _ in range(2)]
    s2b = p.sb("s2b", [128, 8 * 1024], BF16, es=es_sh)
    s1sem = [p.sem(f"s1d{i}") for i in range(2)]
    s2sem = p.sem("s2d")
    s1_used = [None, None]
    s2_used = None
    s1_loaded = [None, None]
    s2_loaded = None

    def load_s1(k):
        nonlocal s1_loaded
        bi = k % 2
        p.wait("y", s1_used[bi], pr["wdone"], pr["castdone"])
        tk = None
        for dc in range(8):
            p.dma("y", s1b[bi][:, dc * 1024:dc * 1024 + 512],
                  scr["s_fc1T"].ap()[dc * 128:(dc + 1) * 128,
                                     k * 512:(k + 1) * 512], s1sem[bi])
            tk = p.dma("y", s1b[bi][:, dc * 1024 + 512:dc * 1024 + 1024],
                       scr["s_fc1T"].ap()[dc * 128:(dc + 1) * 128,
                                          2048 + k * 512:2048 + (k + 1) * 512],
                       s1sem[bi])
        s1_loaded[bi] = tk

    def load_s2(h):
        nonlocal s2_loaded
        p.wait("y", s2_used, pr["castdone"])
        tk = None
        for fc in range(8):
            tk = p.dma("y", s2b[:, fc * 1024:(fc + 1) * 1024],
                       scr["s_fc2T"].ap()[(h * 8 + fc) * 128:(h * 8 + fc + 1) * 128, :],
                       s2sem)
        s2_loaded = tk

    load_s1(0)
    load_xln(0)
    a_war = None
    for tg in range(ng):
        load_xln(tg + 1)
        for k in range(4):
            if k < 3:
                load_s1(k + 1)
            elif tg < ng - 1:
                load_s1(0)
            bi = k % 2
            for j4 in range(4):
                j = k * 4 + j4
                fc1_chunk(s1b[bi], 1024, j4 * 128, 512 + j4 * 128,
                          pr["fc1b"][:, 240 + j:241 + j], pr["fc1b"][:, 256 + j:257 + j],
                          (tg, tg % 2), asb[:, j * 512:(j + 1) * 512],
                          s1_loaded[bi], a_war=a_war, slot=j % 2)
            s1_used[bi] = st.tok
        for h in range(2):
            load_s2(h)
            p.wait("t", s2_loaded, sv.tok)
            for sub in range(4):
                bl = cT[0:16, (tg * 4 + sub) * 128:(tg * 4 + sub + 1) * 128] \
                    if h == 0 else None
                fc2_sub(tg, sub, 8,
                        lambda j, sub_=sub, h_=h: asb[:, (h_ * 8 + j) * 512 + sub_ * 128:
                                                      (h_ * 8 + j) * 512 + (sub_ + 1) * 128],
                        s2b, None, bias_lhs=bl, bias_rhs=pr["bmat"])
            s2_used = st.tok
        xl_used[tg % 2] = st.tok
        a_war = st.tok
    xl_loaded.clear()
    shared_done = st.tok
    p.barrier(p.all_toks())
    es_sh.close()

    f1b = p.sb("ef1", [128, 8 * 2048], BF16)
    f2b = p.sb("ef2", [128, 8 * 1024], BF16)
    abuf = asb
    f1sem = p.sem("f1d")
    f2sem = p.sem("f2d")
    f1_used = None
    f2_used = None

    for e in range(15):
        p.wait("y", f1_used, pr["wdone"], pr["castdone"])
        tk1 = None
        for dc in range(8):
            tk1 = p.dma("y", f1b[:, dc * 2048:(dc + 1) * 2048],
                        scr["fc1T"].ap()[dc * 128:(dc + 1) * 128,
                                         e * 2048:(e + 1) * 2048], f1sem)
        p.wait("y", f2_used)
        tk2 = None
        for fc in range(8):
            tk2 = p.dma("y", f2b[:, fc * 1024:(fc + 1) * 1024],
                        scr["fc2T"].ap()[fc * 128:(fc + 1) * 128,
                                         e * 1024:(e + 1) * 1024], f2sem)
        load_xln(0)
        for tg in range(ng):
            load_xln(tg + 1)
            for j in range(8):
                fc1_chunk(f1b, 2048, j * 128, 1024 + j * 128,
                          pr["fc1b"][:, e * 16 + j:e * 16 + j + 1],
                          pr["fc1b"][:, e * 16 + 8 + j:e * 16 + 8 + j + 1],
                          (tg, tg % 2), abuf[:, j * 512:(j + 1) * 512], tk1,
                          a_war=a_war, slot=j % 2)
            p.wait("t", tk2, sv.tok)
            for sub in range(4):
                fc2_sub(tg, sub, 8,
                        lambda j, sub_=sub: abuf[:, j * 512 + sub_ * 128:
                                                 j * 512 + (sub_ + 1) * 128],
                        f2b, carr[:, (tg * 4 + sub) * 16 + e:
                                  (tg * 4 + sub) * 16 + e + 1])
            xl_used[tg % 2] = st.tok
            a_war = st.tok
        xl_loaded.clear()
        f1_used = st.tok
        f2_used = st.tok


def build_moe_program(half):
    nc = bass.Bass("TRN2", target_bir_lowering=False)
    es = ExitStack()
    names = ["gate_w", "fc1_w", "fc1_b", "geglu_mult", "fc2_w", "fc2_b",
             "s_fc1_w", "s_fc1_b", "s_geglu_mult", "s_fc2_w", "s_fc2_b"]
    shapes = {"gate_w": [15, 1024], "fc1_w": [15 * 2048, 1024], "fc1_b": [15 * 2048],
              "geglu_mult": [15 * 1024], "fc2_w": [15 * 1024, 1024],
              "fc2_b": [15 * 1024], "s_fc1_w": [4096, 1024], "s_fc1_b": [4096],
              "s_geglu_mult": [2048], "s_fc2_w": [1024, 2048], "s_fc2_b": [1024]}
    with es:
        p = Prog(nc, es)
        ins = {n: p.dram(n, shapes[n], kind="ExternalInput") for n in names}
        ins["x"] = p.dram("x", [2048, D], kind="ExternalInput")
        ins["c_iden"] = p.dram("c_iden", [128, 128], BF16, kind="ExternalInput")
        y = p.dram("y", [2048, D], kind="ExternalOutput")
        scr = {
            "fc1": p.dram("s_fc1s", [15 * 2048, 1024], BF16),
            "fc2": p.dram("s_fc2s", [15 * 1024, 1024], BF16),
            "s_fc1": p.dram("s_sfc1s", [4096, 1024], BF16),
            "s_fc2": p.dram("s_sfc2s", [1024, 2048], BF16),
            "fc1T": p.dram("s_fc1Ts", [1024, 15 * 2048], BF16),
            "fc2T": p.dram("s_fc2Ts", [1024, 15 * 1024], BF16),
            "s_fc1T": p.dram("s_sfc1Ts", [1024, 4096], BF16),
            "s_fc2T": p.dram("s_sfc2Ts", [2048, 1024], BF16),
            "xln": p.dram("s_xln", [8 * 128 * 2048], BF16),
        }
        r = Res(p, resident_x=True, ntiles=16)
        r.load_consts(p, ins["c_iden"].ap())
        pr = prep_moe(p, r, ins, scr)
        prep_transpose(p, r, [
            (scr["fc1"], 15 * 2048, 1024, scr["fc1T"]),
            (scr["fc2"], 15 * 1024, 1024, scr["fc2T"]),
            (scr["s_fc1"], 4096, 1024, scr["s_fc1T"]),
            (scr["s_fc2"], 1024, 2048, scr["s_fc2T"]),
        ])
        xio = XResident(p, r, x_ap=ins["x"], y_ap=y, off=0, ntiles=16)
        xio.load_all(p, r)
        emit_moe(p, r, scr, pr, xio, list(range(16)))
        xio.store_all(p, r, waits=(r.sv.tok, r.ss.tok, r.st.tok))
        p.wait("y", *xio.final_toks())
        p.build()
    return nc, p.ninstr



def prep_attn_w(p: Prog, r: Res, ins, scr, has_bias):
    p.dma("g", AP(scr["wqkv"], 0, [[3072, 1024], [1, 3072]]),
          AP(ins["wqkv"], 0, [[3072, 1024], [1, 3072]]), r.dmag)
    cd = p.dma("g", AP(scr["wout"], 0, [[1024, 1024], [1, 1024]]),
               AP(ins["wout"], 0, [[1024, 1024], [1, 1024]]), r.dmag)
    pr = {"castdone": cd}
    if has_bias:
        brow = p.sb("brow", [1, 1024], BF16)
        pr["bout"] = brow
        pr["btok"] = p.dma("g", brow[:, :], AP(ins["bout"], 0, [[1024, 1], [1, 1024]]),
                           r.dmag)
        pr["castdone"] = pr["btok"]
    return pr


def emit_attn_window(p: Prog, r: Res, scr, pr, xio, kind, consts):
    sv, ss, st = r.sv, r.ss, r.st
    rope = (kind == "l0aa")
    lag = 1 if kind == "l0aa" else 0
    wqkv = p.sb("wqkv", [128, 8 * 3072], BF16)
    wout = p.sb("wout", [128, 8 * 1024], BF16)
    qsem = p.sem("aw_wd")
    wtok = None
    p.wait("y", pr["castdone"])
    for dc in range(8):
        p.dma("y", wqkv[:, dc * 3072:(dc + 1) * 3072],
              scr["wqkv"].ap()[dc * 128:(dc + 1) * 128, :], qsem)
        wtok = p.dma("y", wout[:, dc * 1024:(dc + 1) * 1024],
                     scr["wout"].ap()[dc * 128:(dc + 1) * 128, :], qsem)
    masks = p.sb("masks", [128, 3 * 384], BF16)
    msem = p.sem("aw_mk")
    mtok = None
    for mv in range(3):
        mtok = p.dma("y", masks[:, mv * 384:(mv + 1) * 384],
                     AP(consts["masks"], mv * 128 * 384, [[384, 128], [1, 384]]), msem)
    kring = p.sb("kring", [128, 4 * 1024], BF16)
    vring = p.sb("vring", [128, 4 * 1040], BF16)
    qfm = [p.sb("qfm", [128, 8 * 128], BF16) for _ in range(2)]
    qtm = p.sb("qtm", [128, 1024], BF16)
    ktm = p.sb("ktm", [128, 1024], BF16)
    aotm = p.sb("aotm", [128, 1024], BF16)
    aofm = p.sb("aofm", [128, 8 * 128], BF16)
    Pm = p.sb("Pm", [128, 384], BF16)
    rl = p.sb("rl", [128, 16])
    xlnf = p.sb("xlnf_aw", [128, 8 * 128], BF16)
    vmem = p.op("g", lambda e: e.memset(vring[:, :], 1.0), inc=r.sg)
    kmem = p.op("g", lambda e: e.memset(kring[:, :], 0.0), inc=r.sg)
    if rope:
        crep = [p.sb("crep", [128, 512], BF16) for _ in range(2)]
        srep = [p.sb("srep", [128, 512], BF16) for _ in range(2)]
        rsem = [p.sem(f"aw_cs{i}") for i in range(2)]
        rtmp = p.sb("rtmp", [128, 512])
        rtmp2 = p.sb("rtmp2", [128, 512])
        cs_used = [None, None]
        cs_tok = [None, None]

        def load_cs(i):
            if i >= NT:
                return
            bi = i % 2
            p.wait("y", cs_used[bi])
            p.dma("y", crep[bi][:, :],
                  AP(consts["crep"], i * 128 * 512, [[512, 128], [1, 512]]), rsem[bi])
            cs_tok[bi] = p.dma("y", srep[bi][:, :],
                               AP(consts["srep"], i * 128 * 512, [[512, 128], [1, 512]]),
                               rsem[bi])
        load_cs(0)

    PQ, PK, PV = r.PS01, r.PS23, r.PS45
    def av_slot(h):
        if h < 7:
            return PK[:, h * 65:h * 65 + 65]
        if h < 14:
            return PK[:, 512 + (h - 7) * 65:512 + (h - 7) * 65 + 65]
        return PV[:, (h - 14) * 65:(h - 14) * 65 + 65]

    X1 = [[64, 16], [1, 32]]
    RAW = {id(PQ): (r.PS0123, 0, 2048), id(PK): (r.PS0123, 1024, 2048)}

    def strided(tsr, base, pat):
        if id(tsr) in RAW:
            t, off, w = RAW[id(tsr)]
            return AP(t, off + base, [[w, 128]] + pat)
        return AP(tsr, base, [[tsr.shape[1] if hasattr(tsr, "shape") else 1024, 128]]
                  + pat)

    prev_attn = None
    prev_evac = None
    prev_add = None
    ln_tok = None

    def proj(i):
        nonlocal ln_tok, prev_evac
        ti = i
        lnt = emit_ln(p, r, xio.tile(p, r, ti), ti % 4, xlnf, 0,
                      after=(xio.ready(ti // 4),), war=(prev_attn,), cs=128)
        ln_tok = lnt
        if rope:
            load_cs(i + 1)
        p.wait("t", lnt, wtok, prev_evac, prev_attn, prev_add)
        toks = {}
        for which, ps, c0 in [("q", PQ, 0), ("k", PK, 1024), ("v", PV, 2048)]:
            for nh in range(2):
                for dc in range(8):
                    tk = p.op("t", lambda e, dc=dc, ps=ps, c0=c0, nh=nh: e.matmul(
                        ps[:, nh * 512:(nh + 1) * 512],
                        xlnf[:, dc * 128:(dc + 1) * 128],
                        wqkv[:, dc * 3072 + c0 + nh * 512:dc * 3072 + c0 + (nh + 1) * 512],
                        start=(dc == 0), stop=(dc == 7)),
                        inc=(st if dc == 7 else None))
            toks[which] = tk
        slot = i % 4
        p.wait("v", toks["v"], vmem, prev_attn)
        vdst = AP(vring, slot * 1040, [[4 * 1040, 128], [65, 16], [1, 64]])
        ev = p.op("v", lambda e, d=vdst: e.tensor_copy(d, PV[:, 0:1024]), inc=sv)
        if rope:
            bi = i % 2
            for src_ps, dst in [(PQ, qtm), (PK, ktm)]:
                p.wait("v", toks["q" if src_ps is PQ else "k"], cs_tok[bi], ev)
                q = [None]

                def vv(fn):
                    p.wait("v", q[0])
                    q[0] = p.op("v", fn, inc=sv)
                    return q[0]
                x1 = strided(src_ps, 0, X1)
                x2 = strided(src_ps, 32, X1)
                cc = AP(crep[bi], 0, [[512, 128], [32, 16], [1, 32]])
                ss2 = AP(srep[bi], 0, [[512, 128], [32, 16], [1, 32]])
                o1 = strided(dst, 0, X1)
                o2 = strided(dst, 32, X1)
                vv(lambda e, x1=x1, cc=cc: e.tensor_tensor(rtmp[:, :], x1, cc,
                                                           op=ALU.mult))
                vv(lambda e, x2=x2, ss_=ss2: e.tensor_tensor(rtmp2[:, :], x2, ss_,
                                                            op=ALU.mult))
                vv(lambda e, o1=o1: e.tensor_tensor(o1, rtmp[:, :], rtmp2[:, :],
                                                    op=ALU.subtract))
                vv(lambda e, x1=x1, ss_=ss2: e.tensor_tensor(rtmp[:, :], x1, ss_,
                                                            op=ALU.mult))
                vv(lambda e, x2=x2, cc=cc: e.tensor_tensor(rtmp2[:, :], x2, cc,
                                                           op=ALU.mult))
                ev = vv(lambda e, o2=o2: e.tensor_tensor(o2, rtmp[:, :], rtmp2[:, :],
                                                         op=ALU.add))
            cs_used[bi] = ev
        else:
            p.wait("s", toks["q"], prev_attn)
            e1 = p.op("s", lambda e: e.copy(qtm[:, :], PQ[:, 0:1024]), inc=ss)
            p.wait("s", toks["k"])
            ev2 = p.op("s", lambda e: e.copy(ktm[:, :], PK[:, 0:1024]), inc=ss)
            ev = ev2
        p.wait("t", ev, (ss.tok if not rope else None), r.pt_free)
        tk = None
        for dc in range(8):
            tk = p.op("t", lambda e, dc=dc: e.transpose(
                r.PT[:, dc * 128:(dc + 1) * 128], qtm[:, dc * 128:(dc + 1) * 128],
                r.idn[:, :]), inc=(st if dc == 7 else None))
        p.wait("v", tk, prev_attn)
        cp1 = p.op("v", lambda e, b=i % 2: e.tensor_copy(qfm[b][:, :], r.PT[:, 0:1024]),
                   inc=sv)
        r.pt_free = cp1
        p.wait("t", cp1)
        for dc in range(8):
            tk = p.op("t", lambda e, dc=dc: e.transpose(
                r.PT[:, dc * 128:(dc + 1) * 128], ktm[:, dc * 128:(dc + 1) * 128],
                r.idn[:, :]), inc=(st if dc == 7 else None))
        p.wait("v", tk, kmem, prev_attn)
        cp2 = p.op("v", lambda e, slot=slot: e.tensor_copy(
            kring[:, slot * 1024:(slot + 1) * 1024], r.PT[:, 0:1024]), inc=sv)
        r.pt_free = cp2
        prev_evac = cp2
        return cp1, cp2

    def attn(j, qready, kready):
        nonlocal prev_attn, prev_add
        qb = qfm[j % 2]
        if kind == "l0aa":
            chunks = [j - 1, j, j + 1]
            mv = 0 if j == 0 else (2 if j == NT - 1 else 1)
        else:
            chunks = [j - 2, j - 1, j]
            mv = 0 if j == 0 else (1 if j == 1 else 2)
        sc = PQ
        deltap = r.PS01
        outs = {}
        atok = None
        for h in range(16):
            hp, lo = divmod(h, 2)
            rbase = 64 * lo
            p.wait("t", qready, kready, mtok, kmem, vmem, prev_add)
            tk = None
            for c, ch in enumerate(chunks):
                slot = ch % 4
                tk = p.op("t", lambda e, c=c, ch=ch, slot=slot, hp=hp, rb=rbase:
                          e.matmul(
                    sc[:, c * 128:(c + 1) * 128],
                    kring[rb:rb + 64, slot * 1024 + hp * 128:slot * 1024 + (hp + 1) * 128],
                    qb[rb:rb + 64, hp * 128:(hp + 1) * 128],
                    start=True, stop=True), inc=(st if c == 2 else None))
            p.wait("s", tk)
            e1 = p.op("s", lambda e: e.activation(Pm[:, :], sc[:, 0:384], AF.Exp,
                                                  scale=0.125), inc=ss)
            p.wait("v", e1)
            e2 = p.op("v", lambda e, mv=mv: e.tensor_tensor(
                Pm[:, :], Pm[:, :], masks[:, mv * 384:(mv + 1) * 384], op=ALU.mult),
                inc=sv)
            p.wait("t", e2)
            for c, ch in enumerate(chunks):
                slot = ch % 4
                atok = p.op("t", lambda e, c=c, slot=slot, h=h: e.matmul(
                    av_slot(h),
                    Pm[:, c * 128:(c + 1) * 128],
                    vring[:, slot * 1040 + h * 65:slot * 1040 + (h + 1) * 65],
                    start=(c == 0), stop=(c == 2)), inc=(st if c == 2 else None))
            outs[h] = atok
        p.wait("v", atok)
        q = [None]
        for h in range(16):
            av = av_slot(h)
            p.wait("v", q[0])
            q[0] = p.op("v", lambda e, av=av, h=h: e.reciprocal(rl[:, h:h + 1],
                                                               av[:, 64:65]), inc=sv)
            p.wait("v", q[0])
            q[0] = p.op("v", lambda e, av=av, h=h: e.tensor_scalar(
                aotm[:, h * 64:(h + 1) * 64], av[:, 0:64], rl[:, h:h + 1], None,
                op0=ALU.mult), inc=sv)
        p.wait("t", q[0], r.pt_free)
        tk = None
        for dc in range(8):
            tk = p.op("t", lambda e, dc=dc: e.transpose(
                r.PT[:, dc * 128:(dc + 1) * 128], aotm[:, dc * 128:(dc + 1) * 128],
                r.idn[:, :]), inc=(st if dc == 7 else None))
        p.wait("v", tk)
        cp = p.op("v", lambda e: e.tensor_copy(aofm[:, :], r.PT[:, 0:1024]), inc=sv)
        r.pt_free = cp
        p.wait("t", cp, wtok)
        first = True
        tk = None
        if "bout" in pr:
            p.wait("t", pr["btok"], r.ones_tok)
            for nh in range(2):
                p.op("t", lambda e, nh=nh: e.matmul(
                    deltap[:, nh * 512:(nh + 1) * 512], r.ones_row[0:1, :],
                    pr["bout"][0:1, nh * 512:(nh + 1) * 512], start=True, stop=False))
            first = False
        for dc in range(8):
            for nh in range(2):
                tk = p.op("t", lambda e, dc=dc, nh=nh, fs=(first and dc == 0): e.matmul(
                    deltap[:, nh * 512:(nh + 1) * 512],
                    aofm[:, dc * 128:(dc + 1) * 128],
                    wout[:, dc * 1024 + nh * 512:dc * 1024 + (nh + 1) * 512],
                    start=fs, stop=(dc == 7)), inc=(st if (dc == 7 and nh == 1) else None))
        prev_add = xio.add(p, r, j, deltap[:, 0:1024], wait=(tk,))
        prev_attn = st.tok

    prev_add = None
    qtoks = {}
    xio.prefetch(p, r, 0)
    for i in range(NT):
        xio.prefetch(p, r, i // 4 + 1)
        cp1, cp2 = proj(i)
        qtoks[i % 2] = cp1
        j = i - lag
        if 0 <= j:
            attn(j, qtoks[j % 2], cp2)
            if j % 4 == 3:
                xio.flush(p, r, j // 4)
    if lag == 1:
        attn(NT - 1, qtoks[(NT - 1) % 2], cp2)
        xio.flush(p, r, NT // 4 - 1)
    return


def build_attnw_program(kind):
    nc = bass.Bass("TRN2", target_bir_lowering=False)
    es = ExitStack()
    rope = (kind == "l0aa")
    with es:
        p = Prog(nc, es)
        ins = {
            "x": p.dram("x", [T, D], kind="ExternalInput"),
            "wqkv": p.dram("wqkv", [1024, 3072], kind="ExternalInput"),
            "wout": p.dram("wout", [1024, 1024], kind="ExternalInput"),
            "c_iden": p.dram("c_iden", [128, 128], BF16, kind="ExternalInput"),
            "c_masks": p.dram("c_masks", [3 * 128, 384], BF16, kind="ExternalInput"),
        }
        if rope:
            ins["bout"] = p.dram("bout", [1024], kind="ExternalInput")
            ins["c_crep"] = p.dram("c_crep", [4096, 512], BF16, kind="ExternalInput")
            ins["c_srep"] = p.dram("c_srep", [4096, 512], BF16, kind="ExternalInput")
        y = p.dram("y", [T, D], kind="ExternalOutput")
        scr = {
            "wqkv": p.dram("s_wqkv", [1024, 3072], BF16),
            "wout": p.dram("s_wout", [1024, 1024], BF16),
        }
        r = Res(p, resident_x=False)
        r.load_consts(p, ins["c_iden"].ap())
        pr = prep_attn_w(p, r, ins, scr, has_bias=rope)
        xio = XStream(p, r, ins["x"], y, nbuf=3)
        consts = {"masks": ins["c_masks"]}
        if rope:
            consts["crep"] = ins["c_crep"]
            consts["srep"] = ins["c_srep"]
        emit_attn_window(p, r, scr, pr, xio, kind, consts)
        p.wait("y", *xio.final_toks())
        p.build()
    return nc, p.ninstr


def np_attn_consts(kind):
    import ml_dtypes
    pvec = np.arange(128)[:, None]
    qvec = np.arange(128)[None, :]
    masks = np.zeros((3, 128, 384), np.float32)
    for v in range(3):
        for c in range(3):
            if kind == "l0aa":
                diff = (c - 1) * 128 + pvec - qvec
                ok = (np.abs(diff) <= 128)
                if v == 0:
                    ok = ok & (c >= 1)
                if v == 2:
                    ok = ok & (c <= 1)
            else:
                diff = (c - 2) * 128 + pvec - qvec
                ok = (diff <= 0) & (diff >= -255)
                if v == 0:
                    ok = ok & (c >= 2)
                if v == 1:
                    ok = ok & (c >= 1)
            masks[v, :, c * 128:(c + 1) * 128] = ok
    out = {"c_masks": masks.reshape(3 * 128, 384).astype(ml_dtypes.bfloat16)}
    if kind == "l0aa":
        inv = 1.0 / 10000.0 ** (np.arange(0, 64, 2, dtype=np.float64) / 64)
        t = np.arange(4096, dtype=np.float64)
        fr = np.outer(t, inv)
        out["c_crep"] = np.tile(np.cos(fr).astype(np.float32), (1, 16)).astype(
            ml_dtypes.bfloat16)
        out["c_srep"] = np.tile(np.sin(fr).astype(np.float32), (1, 16)).astype(
            ml_dtypes.bfloat16)
    return out



def prep_attn_full_alloc(p: Prog, kind):
    if kind == "l0ab":
        return {"mrow": p.sb("mrowf", [1, 1024], BF16)}
    pr = {"negc": p.sb("negc", [128, 16])}
    for bn in ["bq", "bk", "bv", "bo"]:
        pr[bn] = p.sb("b_" + bn, [1, 1024], BF16)
    return pr


def prep_attn_full(p: Prog, r: Res, ins, scr, kind, pr=None, standalone=True):
    if pr is None:
        pr = prep_attn_full_alloc(p, kind)
    if kind == "l0ab":
        nw = p.sb("nwrow", [8, 128])
        nsem = p.sem("nws")
        ld = p.dma("y", nw[:, :], AP(ins["norm_w"], 0, [[128, 8], [1, 128]]), nsem)
        nwc = p.sb("nwc", [128, 8])
        p.wait("t", ld, r.idf_tok, r.ptf_free)
        tk = p.op("t", lambda e: e.transpose(r.PTF[:, 0:8], nw[0:8, :],
                                             r.idf[0:8, 0:8]), inc=r.st)
        p.wait("v", tk)
        tw = p.op("v", lambda e: e.tensor_copy(nwc[:, 0:8], r.PTF[:, 0:8]), inc=r.sv)
        r.ptf_free = tw
        lsem = [p.sem(f"fald{i}") for i in range(2)]
        stsem = p.sem("fast")
        es2 = ExitStack() if standalone else None
        wtmp = [p.sb("fpw", [128, 2048], es=es2) for _ in range(2)]
        wob = p.sb("fpo", [128, 2048], BF16, es=es2)
        busers = [None, None]
        out_free = None
        last = None
        for wname, scrn, width in [("Wq", "wq", 1024), ("Wkv", "wkv", 2048)]:
            for c in range(8):
                bi = c % 2
                p.wait("y", busers[bi])
                ld = p.dma("y", wtmp[bi][:, 0:width],
                           AP(ins[wname], c * 128 * width, [[width, 128], [1, width]]),
                           lsem[bi])
                p.wait("v", ld, tw, out_free)
                t3 = p.op("v", lambda e, b=bi, c=c, w=width: e.tensor_scalar(
                    wob[:, 0:w], wtmp[b][:, 0:w], nwc[:, c:c + 1], None,
                    op0=ALU.mult), inc=r.sv)
                busers[bi] = t3
                p.wait("y", t3)
                last = p.dma("y", AP(scr[scrn], c * 128 * width,
                                     [[width, 128], [1, width]]), wob[:, 0:w]
                             if False else wob[:, 0:width], stsem)
                out_free = last
        if standalone:
            es2.close()
            p.barrier([r.sv.tok, r.st.tok, lsem[0].tok, lsem[1].tok,
                       stsem.tok, nsem.tok])
        cd = p.dma("g", AP(scr["wo"], 0, [[1024, 1024], [1, 1024]]),
                   AP(ins["Wo"], 0, [[1024, 1024], [1, 1024]]), r.dmag)
        pr["wdone"] = last
        pr["castdone"] = cd
        mrow = pr["mrow"]
        es3 = ExitStack() if standalone else None
        gq = p.sb("gq", [1, 1024], es=es3)
        gk = p.sb("gk", [1, 1024], es=es3)
        gsem = p.sem("gsm")
        l1 = p.dma("y", gq[:, :], AP(ins["q_gamma"], 0, [[1024, 1], [1, 1024]]), gsem)
        l2 = p.dma("y", gk[:, :], AP(ins["k_gamma"], 0, [[1024, 1], [1, 1024]]), gsem)
        p.wait("v", l2)
        t1 = p.op("v", lambda e: e.tensor_scalar(gq[:, :], gq[:, :], 1.0, None,
                                                 op0=ALU.add), inc=r.sv)
        p.wait("v", t1)
        t2 = p.op("v", lambda e: e.tensor_scalar(gk[:, :], gk[:, :], 1.0, None,
                                                 op0=ALU.add), inc=r.sv)
        p.wait("v", t2)
        t3 = p.op("v", lambda e: e.tensor_tensor(gq[:, :], gq[:, :], gk[:, :],
                                                 op=ALU.mult), inc=r.sv)
        p.wait("v", t3)
        t4 = p.op("v", lambda e: e.tensor_scalar(mrow[:, :], gq[:, :], 64.0, None,
                                                 op0=ALU.mult), inc=r.sv)
        pr["mtok"] = t4
        if standalone:
            p.barrier(p.all_toks())
            es3.close()
    else:
        csem = p.sem("fa_cast")
        bsem = p.sem("fa_brow")
        for wname, scrn in [("Wq", "wq"), ("Wk", "wk"), ("Wv", "wv"), ("Wo", "wo")]:
            cd = p.dma("g", AP(scr[scrn], 0, [[1024, 1024], [1, 1024]]),
                       AP(ins[wname], 0, [[1024, 1024], [1, 1024]]), csem)
        pr["castdone"] = cd
        pr["wdone"] = cd
        for bn in ["bq", "bk", "bv", "bo"]:
            cd = p.dma("g", pr[bn][:, :], AP(ins[bn], 0, [[1024, 1], [1, 1024]]),
                       bsem)
        pr["btok"] = cd
        negc = pr["negc"]
        es3 = ExitStack() if standalone else None
        crow = p.sb("crow", [1, 16], BF16, es=es3)
        bqf = p.sb("bqf", [1, 1024], es=es3)
        bkf = p.sb("bkf", [1, 1024], es=es3)
        gsem = p.sem("gsm2")
        l1 = p.dma("y", bqf[:, :], AP(ins["bq"], 0, [[1024, 1], [1, 1024]]), gsem)
        l2 = p.dma("y", bkf[:, :], AP(ins["bk"], 0, [[1024, 1], [1, 1024]]), gsem)
        p.wait("v", l2)
        t1 = p.op("v", lambda e: e.tensor_tensor(bqf[:, :], bqf[:, :], bkf[:, :],
                                                 op=ALU.mult), inc=r.sv)
        p.wait("v", t1)
        t2 = p.op("v", lambda e: e.reduce_sum(
            bkf[:, 0:16], AP(bqf, 0, [[1024, 1], [64, 16], [1, 64]]), axis=AX.X),
            inc=r.sv)
        p.wait("v", t2)
        t3 = p.op("v", lambda e: e.tensor_scalar(crow[:, 0:16], bkf[:, 0:16],
                                                 -1.0 / 64.0, None, op0=ALU.mult),
                  inc=r.sv)
        p.wait("t", t3, r.ones_tok)
        tk = p.op("t", lambda e: e.matmul(r.PTF[:, 0:16], r.ones_row[0:1, :],
                                          crow[0:1, 0:16], start=True, stop=True),
                  inc=r.st)
        p.wait("v", tk, r.ptf_free)
        t5 = p.op("v", lambda e: e.tensor_copy(negc[:, 0:16], r.PTF[:, 0:16]), inc=r.sv)
        r.ptf_free = t5
        pr["negctok"] = t5
        if standalone:
            p.barrier(p.all_toks())
            es3.close()
    return pr


def emit_attn_full(p: Prog, r: Res, scr, pr, xio, kind):
    sv, ss, st = r.sv, r.ss, r.st
    l0 = (kind == "l0ab")
    xscr = scr["xln"]
    Kfm = p.sb("Kfm", [128, 4 * 4096], BF16)
    vaug = p.sb("vaug", [128, 32 * 520], BF16)
    Qfm = p.sb("Qfm", [128, 4 * 512], BF16)
    wq = p.sb("wq_s", [128, 8 * 512], BF16)
    wk = p.sb("wk_s", [128, 8 * 512], BF16)
    wv = p.sb("wv_s", [128, 8 * 512], BF16)
    wo = p.sb("wo_s", [128, 4 * 1024], BF16)
    xlnf = p.sb("xlnf_fa", [128, 8 * 128], BF16)
    rtmp = p.sb("rtmp_fa", [128, 512])
    rq8 = p.sb("rq8", [128, 8])
    Pbuf2 = p.sb("Pb2", [128, 2048], BF16)
    Pbuf = [Pbuf2[:, 0:1024], Pbuf2[:, 1024:2048]]
    tbuf2 = p.sb("tb2", [128, 2048]) if l0 else None
    kcast = p.sb("kcast", [128, 512], BF16)
    rlrow = p.sb("rlrow", [1, 512])
    fbc = p.sb("fbc", [64, 512])
    aofm = p.sb("aofm", [128, 4 * 512], BF16)
    mbc = p.sb("mbc", [128, 512], BF16) if l0 else None
    normsT = None if l0 else p.sb("normsT", [8, 4096])
    maskT = None if l0 else p.sb("maskT", [8, 4096], BF16)
    bis = None if l0 else p.sb("bis", [8, 8])
    wsem = p.sem("fa_w")
    xlsem = [p.sem(f"fa_xl{i}") for i in range(2)]
    xasem = [p.sem(f"fa_xas{i}") for i in range(2)]
    xbsem = [p.sem(f"fa_xbs{i}") for i in range(2)]
    xa = [p.sb("fa_xa", [128, 1024], BF16) for _ in range(2)]
    xb2 = [p.sb("fa_xb", [128, 8 * 512], BF16) for _ in range(2)]
    vmem_tok = None
    xscr_done = None

    def load_weights(sw):
        tk = None
        p.wait("y", pr["wdone"], pr["castdone"], st.tok)
        if l0:
            for dc in range(8):
                p.dma("y", wq[:, dc * 512:(dc + 1) * 512],
                      AP(scr["wq"], dc * 128 * 1024 + sw * 512,
                         [[1024, 128], [1, 512]]), wsem)
                p.dma("y", wk[:, dc * 512:(dc + 1) * 512],
                      AP(scr["wkv"], dc * 128 * 2048 + sw * 512,
                         [[2048, 128], [1, 512]]), wsem)
                tk = p.dma("y", wv[:, dc * 512:(dc + 1) * 512],
                           AP(scr["wkv"], dc * 128 * 2048 + 1024 + sw * 512,
                              [[2048, 128], [1, 512]]), wsem)
            for pc in range(4):
                tk = p.dma("y", wo[:, pc * 1024:(pc + 1) * 1024],
                           AP(scr["wo"], (sw * 512 + pc * 128) * 1024,
                              [[1024, 128], [1, 1024]]), wsem)
        else:
            for dc in range(8):
                p.dma("y", wq[:, dc * 512:(dc + 1) * 512],
                      scr["wqT"].ap()[dc * 128:(dc + 1) * 128,
                                      sw * 512:(sw + 1) * 512], wsem)
                p.dma("y", wk[:, dc * 512:(dc + 1) * 512],
                      scr["wkT"].ap()[dc * 128:(dc + 1) * 128,
                                      sw * 512:(sw + 1) * 512], wsem)
                tk = p.dma("y", wv[:, dc * 512:(dc + 1) * 512],
                           scr["wvT"].ap()[dc * 128:(dc + 1) * 128,
                                           sw * 512:(sw + 1) * 512], wsem)
            for pc in range(4):
                tk = p.dma("y", wo[:, pc * 1024:(pc + 1) * 1024],
                           scr["woT"].ap()[sw * 512 + pc * 128:
                                           sw * 512 + (pc + 1) * 128, :], wsem)
        return tk

    def proj_tm(ps, wbuf, xslice, xtok, bias_row, extra=()):
        p.wait("t", xtok, (wsem, wsem.n), *extra)
        first = True
        if bias_row is not None:
            p.wait("t", pr["btok"], r.ones_tok)
            p.op("t", lambda e, b=bias_row, ps=ps: e.matmul(
                ps[:, 0:512], r.ones_row[0:1, :], b, start=True, stop=False))
            first = False
        tk = None
        for dc in range(8):
            tk = p.op("t", lambda e, dc=dc, w=wbuf, xs=xslice(dc), ps=ps,
                      fs=(first and dc == 0): e.matmul(
                ps[:, 0:512], xs, w[:, dc * 512:(dc + 1) * 512],
                start=fs, stop=(dc == 7)), inc=(st if dc == 7 else None))
        return tk

    def rms_to_rq8(ps, war):
        p.wait("s", st.tok, war)
        t1 = p.op("s", lambda e, ps=ps: e.activation(rtmp[:, :], ps[:, 0:512],
                                                     AF.Square), inc=ss)
        p.wait("v", t1, war)
        t2 = p.op("v", lambda e: e.reduce_sum(
            rq8[:, 0:8], AP(rtmp, 0, [[512, 128], [64, 8], [1, 64]]), axis=AX.X),
            inc=sv)
        p.wait("v", t2)
        t3 = p.op("v", lambda e: e.tensor_scalar(rq8[:, 0:8], rq8[:, 0:8], 1e-24, None,
                                                 op0=ALU.max), inc=sv)
        p.wait("s", t3)
        t4 = p.op("s", lambda e: e.activation(rq8[:, 0:8], rq8[:, 0:8], AF.Ln), inc=ss)
        p.wait("s", t4)
        t5 = p.op("s", lambda e: e.activation(rq8[:, 0:8], rq8[:, 0:8], AF.Exp,
                                              scale=-0.5), inc=ss)
        return t5

    RQ8B = AP(rq8, 0, [[8, 128], [1, 8], [0, 64]])
    PQ_, PK_, PV_ = r.PS01, r.PS23, r.PS45

    def phase_a(sw):
        nonlocal vmem_tok, xscr_done
        wtok = load_weights(sw)
        p.wait("g", st.tok)
        vm = p.op("g", lambda e: e.memset(vaug[:, :], 1.0), inc=r.sg)
        vmem_tok = vm
        xa_used = [None, None]
        prev_kv = st.tok
        prev_sq = sv.tok
        nrm_prev = sv.tok
        prev_st = None
        for ti in range(NT):
            bi = ti % 2
            if sw == 0:
                xrd = xio.loadA(p, r, ti)
                lnt = emit_ln(p, r, xio.tileA(p, r, ti), ti % 4, xlnf, 0,
                              after=(xrd,), war=(prev_kv, prev_st), cs=128,
                              scale_mode=("rms2" if l0 else None))
                xio.consumedA(p, r, ti, lnt)
                xtok = lnt
                p.wait("y", lnt)
                prev_st = p.dma("y", AP(xscr, ti * 128,
                                        [[4096, 128], [128 * 4096, 8], [1, 128]]),
                                xlnf[:, 0:1024], xlsem[bi])
                xsl = lambda dc: xlnf[:, dc * 128:(dc + 1) * 128]
            else:
                p.wait("y", xa_used[bi], *xscr_done)
                ld = p.dma("y", xa[bi][:, :],
                           AP(xscr, ti * 128, [[4096, 128], [128 * 4096, 8], [1, 128]]),
                           xasem[bi])
                xtok = ld
                xsl = lambda dc, b=bi: xa[b][:, dc * 128:(dc + 1) * 128]
            ktk = proj_tm(PK_, wk, xsl, xtok,
                          None if l0 else pr["bk"][0:1, sw * 512:(sw + 1) * 512],
                          extra=(prev_kv,))
            vtk = proj_tm(PV_, wv, xsl, xtok,
                          None if l0 else pr["bv"][0:1, sw * 512:(sw + 1) * 512])
            if not l0:
                qtk = proj_tm(PQ_, wq, xsl, xtok,
                              pr["bq"][0:1, sw * 512:(sw + 1) * 512])
            if sw == 1:
                xa_used[bi] = vtk if l0 else qtk
            p.wait("v", vtk, vm, prev_kv)
            vdst = AP(vaug, ti * 520, [[32 * 520, 128], [65, 8], [1, 64]])
            ev = p.op("v", lambda e, d=vdst: e.tensor_copy(d, PV_[:, 0:512]), inc=sv)
            if l0:
                t5k = rms_to_rq8(PK_, nrm_prev)
                p.wait("v", t5k, ktk, nrm_prev, ev)
                kn = p.op("v", lambda e: e.tensor_tensor(
                    kcast[:, 0:512], PK_[:, 0:512], RQ8B, op=ALU.mult), inc=sv)
                nrm_prev = kn
                ksrctok = kn
            else:
                t1 = None
                p.wait("s", qtk, prev_sq, st.tok)
                t1 = p.op("s", lambda e: e.activation(rtmp[:, :], PQ_[:, 0:512],
                                                      AF.Square), inc=ss)
                p.wait("v", t1, prev_sq)
                t2 = p.op("v", lambda e: e.reduce_sum(
                    rq8[:, 0:8], AP(rtmp, 0, [[512, 128], [64, 8], [1, 64]]),
                    axis=AX.X), inc=sv)
                p.wait("t", t2, r.ptf_free)
                t3 = p.op("t", lambda e: e.transpose(r.PTF[0:8, 0:128], rq8[:, 0:8],
                                                     r.idf[:, :]), inc=st)
                p.wait("v", t3)
                t4 = p.op("v", lambda e, ti=ti: e.tensor_copy(
                    normsT[0:8, ti * 128:(ti + 1) * 128],
                    r.PTF[0:8, 0:128]), inc=sv)
                r.ptf_free = t4
                prev_sq = t4
                p.wait("v", ktk, nrm_prev, ev)
                kn = p.op("v", lambda e: e.tensor_copy(kcast[:, 0:512],
                                                       PK_[:, 0:512]), inc=sv)
                nrm_prev = kn
                ksrctok = kn
            p.wait("t", ksrctok, r.pt_free)
            tk = None
            for pc in range(4):
                tk = p.op("t", lambda e, pc=pc: e.transpose(
                    r.PT[:, pc * 128:(pc + 1) * 128],
                    kcast[:, pc * 128:(pc + 1) * 128], r.idn[:, :]),
                    inc=(st if pc == 3 else None))
            p.wait("v", tk)
            kdst = AP(Kfm, ti * 128, [[4 * 4096, 128], [4096, 4], [1, 128]])
            cpk = p.op("v", lambda e, d=kdst: e.tensor_copy(d, r.PT[:, 0:512]), inc=sv)
            r.pt_free = cpk
            prev_kv = cpk
        if sw == 0:
            xscr_done = (xlsem[0].tok, xlsem[1].tok)
        return prev_kv

    def bisect(sw):
        q = [None]

        def vstep(fn):
            p.wait("v", q[0])
            q[0] = p.op("v", fn, inc=sv)
            return q[0]
        rows = normsT[0:8, :]
        brow = bis[0:8, :]
        mrows = maskT[0:8, :]
        q[0] = sv.tok
        vstep(lambda e: e.reduce_max(brow[:, 1:2], rows[:, :], axis=AX.X))
        vstep(lambda e: e.tensor_scalar(brow[:, 1:2], brow[:, 1:2], 1.0, None,
                                        op0=ALU.add))
        vstep(lambda e: e.memset(brow[:, 0:1], 0.0))
        for it in range(40):
            vstep(lambda e: e.tensor_tensor(brow[:, 2:3], brow[:, 0:1], brow[:, 1:2],
                                            op=ALU.add))
            vstep(lambda e: e.tensor_scalar(brow[:, 2:3], brow[:, 2:3], 0.5, None,
                                            op0=ALU.mult))
            vstep(lambda e: e.tensor_scalar(mrows[:, :], rows[:, :], brow[:, 2:3],
                                            0.0, op0=ALU.is_ge, op1=ALU.add,
                                            accum_out=brow[:, 3:4]))
            vstep(lambda e: e.tensor_scalar(brow[:, 4:5], brow[:, 3:4], 819.0, None,
                                            op0=ALU.is_ge))
            vstep(lambda e: e.tensor_tensor(brow[:, 5:6], brow[:, 2:3], brow[:, 0:1],
                                            op=ALU.subtract))
            vstep(lambda e: e.tensor_tensor(brow[:, 5:6], brow[:, 5:6], brow[:, 4:5],
                                            op=ALU.mult))
            vstep(lambda e: e.tensor_tensor(brow[:, 0:1], brow[:, 0:1], brow[:, 5:6],
                                            op=ALU.add))
            vstep(lambda e: e.tensor_tensor(brow[:, 5:6], brow[:, 1:2], brow[:, 2:3],
                                            op=ALU.subtract))
            vstep(lambda e: e.tensor_tensor(brow[:, 5:6], brow[:, 5:6], brow[:, 4:5],
                                            op=ALU.mult))
            vstep(lambda e: e.tensor_tensor(brow[:, 1:2], brow[:, 2:3], brow[:, 5:6],
                                            op=ALU.add))
        return vstep(lambda e: e.tensor_scalar(mrows[:, :], rows[:, :], brow[:, 0:1],
                                               None, op0=ALU.is_ge))

    def phase_b(sw, kv_tok, mtok):
        SCe, SCo = r.PS01, r.PS23
        xb_used = [None, None]
        prev_add = None
        prev_dn = sv.tok
        for jq in range(8):
            bi = jq % 2
            p.wait("y", xb_used[bi], *xscr_done)
            ld = p.dma("y", xb2[bi][:, :],
                       AP(xscr, jq * 512, [[4096, 128], [128 * 4096, 8], [1, 512]]),
                       xbsem[bi])
            qtoks = []
            for s4 in range(4):
                qtk = proj_tm(SCe, wq,
                              lambda dc, b=bi, s4=s4: xb2[b][:, dc * 512 + s4 * 128:
                                                             dc * 512 + (s4 + 1) * 128],
                              ld,
                              None if l0 else pr["bq"][0:1, sw * 512:(sw + 1) * 512],
                              extra=(prev_add, prev_dn))
                if l0:
                    t5 = rms_to_rq8(SCe, prev_dn)
                    p.wait("v", t5, qtk, prev_dn, pr["mbctok"])
                    qn = p.op("v", lambda e: e.tensor_tensor(
                        rtmp[:, :], SCe[:, 0:512], RQ8B, op=ALU.mult), inc=sv)
                    p.wait("v", qn)
                    qn2 = p.op("v", lambda e: e.tensor_tensor(
                        Pbuf[1][:, 0:512], rtmp[:, :], mbc[:, :], op=ALU.mult), inc=sv)
                    qrdy = qn2
                else:
                    p.wait("v", qtk, prev_dn)
                    qrdy = p.op("v", lambda e: e.tensor_copy(Pbuf[1][:, 0:512],
                                                             SCe[:, 0:512]), inc=sv)
                p.wait("t", qrdy, r.pt_free)
                tk = None
                for pc in range(4):
                    tk = p.op("t", lambda e, pc=pc: e.transpose(
                        r.PT[:, pc * 128:(pc + 1) * 128],
                        Pbuf[1][:, pc * 128:(pc + 1) * 128], r.idn[:, :]),
                        inc=(st if pc == 3 else None))
                p.wait("v", tk)
                qdst = AP(Qfm, s4 * 128, [[4 * 512, 128], [512, 4], [1, 128]])
                cpq = p.op("v", lambda e, d=qdst: e.tensor_copy(d, r.PT[:, 0:512]),
                           inc=sv)
                r.pt_free = cpq
                prev_dn = cpq
                qtoks.append(cpq)
            qready = qtoks[-1]
            last_av = None
            ptok = [None, None]
            PARS = [(0, None, SCe, 0), (1, None, SCo, 64)]
            for hp in range(4):
                for cg in range(16):
                    sctok = [None, None]
                    a2s = [None, None]
                    for par, _, SC, rb in PARS:
                        p.wait("t", qready, kv_tok, mtok, prev_add, ptok[par])
                        tk = None
                        for cc in range(2):
                            c = cg * 2 + cc
                            tk = p.op("t", lambda e, c=c, cc=cc, SC=SC, rb=rb, hp=hp:
                                      e.matmul(
                                SC[:, cc * 512:(cc + 1) * 512],
                                Kfm[rb:rb + 64, hp * 4096 + c * 128:
                                    hp * 4096 + (c + 1) * 128],
                                Qfm[rb:rb + 64, hp * 512:(hp + 1) * 512],
                                start=True, stop=True), inc=(st if cc == 1 else None))
                        sctok[par] = tk
                    if l0:
                        a1s = [None, None]
                        for par, _, SC, rb in PARS:
                            p.wait("s", sctok[par], ptok[par])
                            a1s[par] = p.op("s", lambda e, SC=SC, par=par:
                                            e.activation(
                                tbuf2[:, par * 1024:(par + 1) * 1024],
                                SC[:, 0:1024], AF.Tanh, scale=0.02), inc=ss)
                        for par, _, SC, rb in PARS:
                            p.wait("s", a1s[par])
                            a2s[par] = p.op("s", lambda e, par=par: e.activation(
                                Pbuf[par][:, 0:1024],
                                tbuf2[:, par * 1024:(par + 1) * 1024], AF.Exp,
                                scale=6.25), inc=ss)
                    else:
                        for par, _, SC, rb in PARS:
                            h = 2 * hp + par
                            p.wait("s", sctok[par], ptok[par])
                            a2s[par] = p.op("s", lambda e, SC=SC, h=h, sw=sw, par=par:
                                            e.activation(
                                Pbuf[par][:, 0:1024], SC[:, 0:1024], AF.Exp,
                                scale=0.125,
                                bias=pr["negc"][:, sw * 8 + h:sw * 8 + h + 1]),
                                inc=ss)
                    for par, _, SC, rb in PARS:
                        h = 2 * hp + par
                        p.wait("t", a2s[par], vmem_tok)
                        cb = par * 512
                        for cc in range(2):
                            c = cg * 2 + cc
                            tk2 = p.op("t", lambda e, c=c, cc=cc, cb=cb, h=h, par=par:
                                       e.matmul(
                                r.PS45[0:65, cb:cb + 512],
                                vaug[:, c * 520 + h * 65:c * 520 + (h + 1) * 65],
                                Pbuf[par][:, cc * 512:(cc + 1) * 512],
                                start=(c == 0), stop=(c == 31)),
                                inc=(st if (c == 31) else None))
                        ptok[par] = tk2
                for par, h in [(0, 2 * hp), (1, 2 * hp + 1)]:
                    cb = par * 512
                    tk2 = ptok[par]
                    p.wait("v", tk2)
                    q1 = p.op("v", lambda e, cb=cb: e.reciprocal(
                        rlrow[0:1, :], r.PS45[64:65, cb:cb + 512]), inc=sv)
                    if not l0:
                        p.wait("t", mtok, r.ptf_free, r.consts_tok)
                        ms = p.op("t", lambda e, h=h, jq=jq: e.matmul(
                            r.PTF[0:1, 0:512], r.idn[0:8, h:h + 1],
                            maskT[0:8, jq * 512:(jq + 1) * 512],
                            start=True, stop=True), inc=st)
                        p.wait("v", q1, ms)
                        q1 = p.op("v", lambda e: e.tensor_tensor(
                            rlrow[0:1, :], rlrow[0:1, :], r.PTF[0:1, 0:512],
                            op=ALU.mult), inc=sv)
                        r.ptf_free = q1
                    p.wait("t", q1, r.ptf_free, r.ones_tok)
                    q2 = p.op("t", lambda e: e.matmul(r.PTF[0:64, 0:512],
                                                      r.ones_f[0:1, 0:64],
                                                      rlrow[0:1, 0:512],
                                                      start=True, stop=True), inc=st)
                    p.wait("v", q2)
                    q3 = p.op("v", lambda e: e.tensor_copy(fbc[:, :],
                                                           r.PTF[0:64, 0:512]),
                              inc=sv)
                    r.ptf_free = q3
                    p.wait("v", q3)
                    q4 = p.op("v", lambda e, cb=cb, hp=hp, par=par: e.tensor_tensor(
                        aofm[64 * par:64 * par + 64, hp * 512:(hp + 1) * 512],
                        r.PS45[0:64, cb:cb + 512], fbc[:, :], op=ALU.mult), inc=sv)
                    ptok[par] = q4
                    last_av = q4
            for s4 in range(4):
                deltap = SCe
                p.wait("t", last_av, (wsem, wsem.n), prev_add)
                first = True
                tk = None
                if (not l0) and sw == 0:
                    p.wait("t", pr["btok"], r.ones_tok)
                    for nh in range(2):
                        p.op("t", lambda e, nh=nh, deltap=deltap: e.matmul(
                            deltap[:, nh * 512:(nh + 1) * 512], r.ones_row[0:1, :],
                            pr["bo"][0:1, nh * 512:(nh + 1) * 512],
                            start=True, stop=False))
                    first = False
                for pc in range(4):
                    for nh in range(2):
                        tk = p.op("t", lambda e, pc=pc, nh=nh, s4=s4, deltap=deltap,
                                  fs=(first and pc == 0): e.matmul(
                            deltap[:, nh * 512:(nh + 1) * 512],
                            aofm[:, pc * 512 + s4 * 128:pc * 512 + (s4 + 1) * 128],
                            wo[:, pc * 1024 + nh * 512:pc * 1024 + (nh + 1) * 512],
                            start=fs, stop=(pc == 3)),
                            inc=(st if (pc == 3 and nh == 1) else None))
                prev_add = xio.add(p, r, jq * 4 + s4, deltap[:, 0:1024], wait=(tk,))
            xb_used[bi] = st.tok
        return prev_add

    mb2 = None
    for sw in range(2):
        if l0:
            p.wait("t", pr["mtok"], r.ptf_free, r.ones_tok)
            if mb2 is not None:
                p.wait("t", mb2)
            mb = p.op("t", lambda e, sw=sw: e.matmul(
                r.PTF[:, 0:512], r.ones_row[0:1, :],
                pr["mrow"][0:1, sw * 512:(sw + 1) * 512], start=True, stop=True),
                inc=st)
            p.wait("v", mb, sv.tok)
            mb2 = p.op("v", lambda e: e.tensor_copy(mbc[:, :], r.PTF[:, 0:512]),
                       inc=sv)
            r.ptf_free = mb2
            pr["mbctok"] = mb2
        kv = phase_a(sw)
        mtok = None
        if not l0:
            mtok = bisect(sw)
        phase_b(sw, kv, mtok)
        xio.new_pass()


def build_attnf_program(kind):
    nc = bass.Bass("TRN2", target_bir_lowering=False)
    es = ExitStack()
    l0 = (kind == "l0ab")
    with es:
        p = Prog(nc, es)
        ins = {"x": p.dram("x", [T, D], kind="ExternalInput"),
               "c_iden": p.dram("c_iden", [128, 128], BF16, kind="ExternalInput")}
        if l0:
            ins["norm_w"] = p.dram("norm_w", [1024], kind="ExternalInput")
            ins["Wq"] = p.dram("Wq", [1024, 1024], kind="ExternalInput")
            ins["Wkv"] = p.dram("Wkv", [1024, 2048], kind="ExternalInput")
            ins["Wo"] = p.dram("Wo", [1024, 1024], kind="ExternalInput")
            ins["q_gamma"] = p.dram("q_gamma", [1024], kind="ExternalInput")
            ins["k_gamma"] = p.dram("k_gamma", [1024], kind="ExternalInput")
            scr = {"wq": p.dram("s_wq", [1024, 1024], BF16),
                   "wkv": p.dram("s_wkv", [1024, 2048], BF16),
                   "wo": p.dram("s_wo", [1024, 1024], BF16)}
        else:
            for n in ["Wq", "Wk", "Wv", "Wo"]:
                ins[n] = p.dram(n, [1024, 1024], kind="ExternalInput")
            for n in ["bq", "bk", "bv", "bo"]:
                ins[n] = p.dram(n, [1024], kind="ExternalInput")
            scr = {w: p.dram("s_" + w, [1024, 1024], BF16)
                   for w in ["wq", "wk", "wv", "wo"]}
            for w in ["wq", "wk", "wv", "wo"]:
                scr[w + "T"] = p.dram("s_" + w + "T", [1024, 1024], BF16)
        scr["xln"] = p.dram("s_xlnf", [8 * 128 * 4096], BF16)
        y = p.dram("y", [T, D], kind="ExternalOutput")
        r = Res(p, resident_x=False)
        r.load_consts(p, ins["c_iden"].ap())
        pr = prep_attn_full(p, r, ins, scr, kind)
        if not l0:
            prep_transpose(p, r, [(scr[w], 1024, 1024, scr[w + "T"])
                                  for w in ["wq", "wk", "wv", "wo"]])
        xio = XPhased(p, r, ins["x"], y)
        emit_attn_full(p, r, scr, pr, xio, kind)
        p.wait("y", *xio.final_toks())
        p.build()
    return nc, p.ninstr



def build_fused_program():
    nc = bass.Bass("TRN2", target_bir_lowering=False)
    es = ExitStack()
    with es:
        p = Prog(nc, es)

        def EI(name, shape, dt=F32):
            return p.dram(name, shape, dt, kind="ExternalInput")

        I = {
            "x": EI("x", [T, D]),
            "aa_wqkv": EI("aa_wqkv", [1024, 3072]),
            "aa_wout": EI("aa_wout", [1024, 1024]),
            "aa_bout": EI("aa_bout", [1024]),
            "m_gate_w": EI("m_gate_w", [15, 1024]),
            "m_fc1_w": EI("m_fc1_w", [15 * 2048, 1024]),
            "m_fc1_b": EI("m_fc1_b", [15 * 2048]),
            "m_geglu": EI("m_geglu", [15 * 1024]),
            "m_fc2_w": EI("m_fc2_w", [15 * 1024, 1024]),
            "m_fc2_b": EI("m_fc2_b", [15 * 1024]),
            "m_sfc1_w": EI("m_sfc1_w", [4096, 1024]),
            "m_sfc1_b": EI("m_sfc1_b", [4096]),
            "m_sgeglu": EI("m_sgeglu", [2048]),
            "m_sfc2_w": EI("m_sfc2_w", [1024, 2048]),
            "m_sfc2_b": EI("m_sfc2_b", [1024]),
            "ab_norm_w": EI("ab_norm_w", [1024]),
            "ab_Wq": EI("ab_Wq", [1024, 1024]),
            "ab_Wkv": EI("ab_Wkv", [1024, 2048]),
            "ab_Wo": EI("ab_Wo", [1024, 1024]),
            "ab_qg": EI("ab_qg", [1024]),
            "ab_kg": EI("ab_kg", [1024]),
            "ba_Wq": EI("ba_Wq", [1024, 1024]),
            "ba_bq": EI("ba_bq", [1024]),
            "ba_Wk": EI("ba_Wk", [1024, 1024]),
            "ba_bk": EI("ba_bk", [1024]),
            "ba_Wv": EI("ba_Wv", [1024, 1024]),
            "ba_bv": EI("ba_bv", [1024]),
            "ba_Wo": EI("ba_Wo", [1024, 1024]),
            "ba_bo": EI("ba_bo", [1024]),
            "bb_wqkv": EI("bb_wqkv", [1024, 3072]),
            "bb_wout": EI("bb_wout", [1024, 1024]),
            "f_w1": EI("f_w1", [4096, 1024]),
            "f_b1": EI("f_b1", [4096]),
            "f_w2": EI("f_w2", [1024, 4096]),
            "f_b2": EI("f_b2", [1024]),
            "c_iden": EI("c_iden", [128, 128], BF16),
            "c_masks_aa": EI("c_masks_aa", [3 * 128, 384], BF16),
            "c_crep": EI("c_crep", [4096, 512], BF16),
            "c_srep": EI("c_srep", [4096, 512], BF16),
            "c_masks_ab": EI("c_masks_ab", [3 * 128, 384], BF16),
        }
        y = p.dram("y", [T, D], kind="ExternalOutput")
        xbuf = p.dram("xbuf", [T, D])
        scr_aa = {"wqkv": p.dram("s_aa_wqkv", [1024, 3072], BF16),
                  "wout": p.dram("s_aa_wout", [1024, 1024], BF16)}
        scr_moe = {"fc1": p.dram("s_m_fc1", [15 * 2048, 1024], BF16),
                   "fc2": p.dram("s_m_fc2", [15 * 1024, 1024], BF16),
                   "s_fc1": p.dram("s_m_sfc1", [4096, 1024], BF16),
                   "s_fc2": p.dram("s_m_sfc2", [1024, 2048], BF16),
                   "xln": p.dram("s_m_xln", [8 * 128 * 2048], BF16)}
        scr_ab = {"wq": p.dram("s_ab_wq", [1024, 1024], BF16),
                  "wkv": p.dram("s_ab_wkv", [1024, 2048], BF16),
                  "wo": p.dram("s_ab_wo", [1024, 1024], BF16),
                  "xln": p.dram("s_ab_xln", [8 * 128 * 4096], BF16)}
        scr_ba = {w: p.dram("s_ba_" + w, [1024, 1024], BF16)
                  for w in ["wq", "wk", "wv", "wo"]}
        scr_ba["xln"] = p.dram("s_ba_xln", [8 * 128 * 4096], BF16)
        scr_bb = {"wqkv": p.dram("s_bb_wqkv", [1024, 3072], BF16),
                  "wout": p.dram("s_bb_wout", [1024, 1024], BF16)}
        scr_f = {"bw1": p.dram("s_f_bw1", [4096, 1024], BF16),
                 "bw2": p.dram("s_f_bw2", [1024, 4096], BF16)}

        r = Res(p)
        r.load_consts(p, I["c_iden"].ap())

        def SCOPE(fn):
            es2 = ExitStack()
            old = p.es
            p.es = es2
            fn()
            p.barrier(p.all_toks(), strict=True)
            p.es = old
            es2.close()

        pr_moe = prep_moe_alloc(p)
        pr_ab = prep_attn_full_alloc(p, "l0ab")
        pr_ba = prep_attn_full_alloc(p, "l1aa")
        pr_f = prep_l1ffn_alloc(p)
        scr_moe["fc1T"] = p.dram("s_m_fc1T", [1024, 15 * 2048], BF16)
        scr_moe["fc2T"] = p.dram("s_m_fc2T", [1024, 15 * 1024], BF16)
        scr_moe["s_fc1T"] = p.dram("s_m_sfc1T", [1024, 4096], BF16)
        scr_moe["s_fc2T"] = p.dram("s_m_sfc2T", [2048, 1024], BF16)
        for w in ["wq", "wk", "wv", "wo"]:
            scr_ba[w + "T"] = p.dram("s_ba_" + w + "T", [1024, 1024], BF16)
        scr_f["bw1T"] = p.dram("s_f_bw1T", [1024, 4096], BF16)
        scr_f["bw2T"] = p.dram("s_f_bw2T", [4096, 1024], BF16)
        moe_ins = {"gate_w": I["m_gate_w"], "fc1_w": I["m_fc1_w"],
                   "fc1_b": I["m_fc1_b"], "geglu_mult": I["m_geglu"],
                   "fc2_w": I["m_fc2_w"], "fc2_b": I["m_fc2_b"],
                   "s_fc1_w": I["m_sfc1_w"], "s_fc1_b": I["m_sfc1_b"],
                   "s_geglu_mult": I["m_sgeglu"], "s_fc2_w": I["m_sfc2_w"],
                   "s_fc2_b": I["m_sfc2_b"]}

        pr_aa = prep_attn_w(p, r, {"wqkv": I["aa_wqkv"], "wout": I["aa_wout"],
                                   "bout": I["aa_bout"]}, scr_aa, has_bias=True)

        def s_l0aa():
            xio = XStream(p, r, I["x"], xbuf, nbuf=3)
            emit_attn_window(p, r, scr_aa, pr_aa, xio, "l0aa",
                             {"masks": I["c_masks_aa"], "crep": I["c_crep"],
                              "srep": I["c_srep"]})
        SCOPE(s_l0aa)

        def s_prep_moe():
            prep_moe(p, r, moe_ins, scr_moe, pr=pr_moe, standalone=False)
            prep_transpose(p, r, [
                (scr_moe["fc1"], 15 * 2048, 1024, scr_moe["fc1T"],
                 pr_moe["jtok"]["fc1"]),
                (scr_moe["s_fc1"], 4096, 1024, scr_moe["s_fc1T"],
                 pr_moe["jtok"]["s_fc1"]),
                (scr_moe["fc2"], 15 * 1024, 1024, scr_moe["fc2T"],
                 pr_moe["jtok"]["fc2"]),
                (scr_moe["s_fc2"], 1024, 2048, scr_moe["s_fc2T"],
                 pr_moe["jtok"]["s_fc2"]),
            ], queues=("s",), standalone=False)
        SCOPE(s_prep_moe)

        def s_moe(half):
            def fn():
                xio = XResident(p, r, x_ap=xbuf, y_ap=xbuf,
                                off=half * 2048 * 1024, ntiles=16)
                xio.load_all(p, r)
                emit_moe(p, r, scr_moe, pr_moe, xio, list(range(16)))
                xio.store_all(p, r, waits=(r.sv.tok, r.ss.tok, r.st.tok))
            return fn

        SCOPE(s_moe(0))
        SCOPE(s_moe(1))

        prep_attn_full(p, r, {"norm_w": I["ab_norm_w"], "Wq": I["ab_Wq"],
                              "Wkv": I["ab_Wkv"], "Wo": I["ab_Wo"],
                              "q_gamma": I["ab_qg"], "k_gamma": I["ab_kg"]},
                       scr_ab, "l0ab", pr=pr_ab)

        def s_l0ab():
            xio = XPhased(p, r, xbuf, xbuf)
            emit_attn_full(p, r, scr_ab, pr_ab, xio, "l0ab")
        SCOPE(s_l0ab)

        SCOPE(s_moe(0))
        SCOPE(s_moe(1))

        prep_attn_full(p, r, {"Wq": I["ba_Wq"], "bq": I["ba_bq"],
                              "Wk": I["ba_Wk"], "bk": I["ba_bk"],
                              "Wv": I["ba_Wv"], "bv": I["ba_bv"],
                              "Wo": I["ba_Wo"], "bo": I["ba_bo"]},
                       scr_ba, "l1aa", pr=pr_ba)
        prep_transpose(p, r, [(scr_ba[w], 1024, 1024, scr_ba[w + "T"])
                              for w in ["wq", "wk", "wv", "wo"]])

        def s_l1aa():
            xio = XPhased(p, r, xbuf, xbuf)
            emit_attn_full(p, r, scr_ba, pr_ba, xio, "l1aa")
        SCOPE(s_l1aa)

        prep_l1ffn(p, r, {"w1": I["f_w1"], "b1": I["f_b1"],
                          "w2": I["f_w2"], "b2": I["f_b2"]}, scr_f, pr=pr_f)
        prep_transpose(p, r, [(scr_f["bw1"], 4096, 1024, scr_f["bw1T"]),
                              (scr_f["bw2"], 1024, 4096, scr_f["bw2T"])])

        def s_ffn(dst):
            def fn():
                xio = XStream(p, r, xbuf, dst, nbuf=2)
                bufs = mk_l1ffn_bufs(p)
                emit_l1ffn(p, r, bufs, scr_f, pr_f, xio)
            return fn
        SCOPE(s_ffn(xbuf))

        pr_bb = prep_attn_w(p, r, {"wqkv": I["bb_wqkv"], "wout": I["bb_wout"]},
                            scr_bb, has_bias=False)

        def s_l1ab():
            xio = XStream(p, r, xbuf, xbuf, nbuf=3)
            emit_attn_window(p, r, scr_bb, pr_bb, xio, "l1ab",
                             {"masks": I["c_masks_ab"]})
        SCOPE(s_l1ab)

        SCOPE(s_ffn(y))
        p.build()
    return nc, p.ninstr, len(p._sems)



_PROGS = {}


def _prog(kind):
    if kind not in _PROGS:
        if kind == "moe":
            _PROGS[kind] = build_moe_program(0)[0]
        elif kind in ("l0aa", "l1ab"):
            _PROGS[kind] = build_attnw_program(kind)[0]
        elif kind in ("l0ab", "l1aa"):
            _PROGS[kind] = build_attnf_program(kind)[0]
        elif kind == "l1ffn":
            _PROGS[kind] = build_l1ffn_program()[0]
    return _PROGS[kind]


def kernel(**ins):
    x = np.ascontiguousarray(ins["L0_attn_a_x"].astype(np.float32))
    B = x.shape[0]
    consts = np_consts()
    cw_aa = np_attn_consts("l0aa")
    cw_ab = np_attn_consts("l1ab")
    if "fused" not in _PROGS:
        _PROGS["fused"] = build_fused_program()[0]
    nc = _PROGS["fused"]
    base = {
        "aa_wqkv": ins["L0_attn_a_Wqkv"], "aa_wout": ins["L0_attn_a_Wout"],
        "aa_bout": ins["L0_attn_a_bout"],
        "m_gate_w": ins["L0_ffn_gate_w"],
        "m_fc1_w": np.ascontiguousarray(ins["L0_ffn_fc1_w"].reshape(-1, 1024)),
        "m_fc1_b": np.ascontiguousarray(ins["L0_ffn_fc1_b"].reshape(-1)),
        "m_geglu": np.ascontiguousarray(ins["L0_ffn_geglu_mult"].reshape(-1)),
        "m_fc2_w": np.ascontiguousarray(ins["L0_ffn_fc2_w"].reshape(-1, 1024)),
        "m_fc2_b": np.ascontiguousarray(ins["L0_ffn_fc2_b"].reshape(-1)),
        "m_sfc1_w": ins["L0_ffn_s_fc1_w"], "m_sfc1_b": ins["L0_ffn_s_fc1_b"],
        "m_sgeglu": ins["L0_ffn_s_geglu_mult"],
        "m_sfc2_w": ins["L0_ffn_s_fc2_w"], "m_sfc2_b": ins["L0_ffn_s_fc2_b"],
        "ab_norm_w": ins["L0_attn_b_norm_w"], "ab_Wq": ins["L0_attn_b_Wq"],
        "ab_Wkv": ins["L0_attn_b_Wkv"], "ab_Wo": ins["L0_attn_b_Wo"],
        "ab_qg": np.ascontiguousarray(ins["L0_attn_b_q_gamma"].reshape(-1)),
        "ab_kg": np.ascontiguousarray(ins["L0_attn_b_k_gamma"].reshape(-1)),
        "ba_Wq": ins["L1_attn_a_Wq"], "ba_bq": ins["L1_attn_a_bq"],
        "ba_Wk": ins["L1_attn_a_Wk"], "ba_bk": ins["L1_attn_a_bk"],
        "ba_Wv": ins["L1_attn_a_Wv"], "ba_bv": ins["L1_attn_a_bv"],
        "ba_Wo": ins["L1_attn_a_Wo"], "ba_bo": ins["L1_attn_a_bo"],
        "bb_wqkv": ins["L1_attn_b_w_qkv"], "bb_wout": ins["L1_attn_b_w_out"],
        "f_w1": ins["L1_ffn_w1"], "f_b1": ins["L1_ffn_b1"],
        "f_w2": ins["L1_ffn_w2"], "f_b2": ins["L1_ffn_b2"],
        "c_iden": consts["c_iden"],
        "c_masks_aa": cw_aa["c_masks"], "c_crep": cw_aa["c_crep"],
        "c_srep": cw_aa["c_srep"],
        "c_masks_ab": cw_ab["c_masks"],
    }
    maps = []
    for b in range(B):
        m = dict(base)
        m["x"] = np.ascontiguousarray(x[b])
        maps.append(m)
    res = run_bass_kernel_spmd(nc, maps, core_ids=list(range(B)))
    return np.stack([res.results[b]["y"] for b in range(B)], 0)


# version 16
# speedup vs baseline: 1.0303x; 1.0300x over previous
import numpy as np
from contextlib import ExitStack

import concourse.bass as bass
import concourse.mybir as mybir
from concourse.bass_utils import run_bass_kernel_spmd

F32 = mybir.dt.float32
BF16 = mybir.dt.bfloat16
AF = mybir.ActivationFunctionType
ALU = mybir.AluOpType
AX = mybir.AxisListType

T = 4096
D = 1024
NT = T // 128
LN_EPS = 1e-5


def AP(t, off, pat):
    return bass.AP(t, off, pat)


class Sem:
    _next = [0]

    def __init__(self, h):
        self.h = h
        self.n = 0
        Sem._next[0] += 1
        self.uid = Sem._next[0]

    def inc(self, k=1):
        self.n += k
        return self.n

    @property
    def tok(self):
        return (self, self.n)


class Prog:
    QMAP = {"t": "tensor", "v": "vector", "s": "scalar", "g": "gpsimd", "y": "sync"}

    def __init__(self, nc, es):
        self.nc = nc
        self.es = es
        self.es0 = es
        self.q = {k: [] for k in self.QMAP}
        self._waited = {k: {} for k in self.QMAP}
        self._sems = []
        self._sempool = {}
        self.last_inc = {k: True for k in self.QMAP}
        self.ninstr = 0
        self._uid = 0

    def un(self, base):
        self._uid += 1
        return f"{base}_{self._uid}"

    def sem(self, name):
        s = self._sempool.get(name)
        if s is None:
            s = Sem(self.es0.enter_context(self.nc.semaphore(name)))
            self._sempool[name] = s
            self._sems.append(s)
        return s

    def all_toks(self):
        return [s.tok for s in self._sems]

    def sb(self, name, shape, dt=F32, es=None):
        return (es or self.es).enter_context(
            self.nc.sbuf_tensor(self.un(name), shape, dt))

    def barrier(self, toks, strict=False):
        if strict:
            bad = [qn for qn in self.QMAP if self.q[qn] and not self.last_inc[qn]]
            if bad:
                raise RuntimeError(f"barrier: queues end without inc: {bad}")
        for qn in self.QMAP:
            self.wait(qn, *toks)

    def ps(self, name, shape, dt=F32):
        return self.es.enter_context(self.nc.psum_tensor(self.un(name), shape, dt))

    def dram(self, name, shape, dt=F32, kind="Internal"):
        return self.nc.dram_tensor(name, shape, dt, kind=kind)

    def wait(self, qn, *tokens):
        for tk in tokens:
            if tk is None:
                continue
            sem, val = tk
            if val <= 0:
                continue
            w = self._waited[qn]
            if w.get(sem.uid, -1) >= val:
                continue
            w[sem.uid] = val
            self.ninstr += 1
            self.q[qn].append(lambda e, h=sem.h, v=val: e.wait_ge(h, v))

    def op(self, qn, fn, inc=None, k=1):
        self.ninstr += 1
        self.last_inc[qn] = inc is not None
        if inc is None:
            self.q[qn].append(fn)
            return None
        inc.inc(k)
        self.q[qn].append(lambda e, f=fn, h=inc.h, kk=k: f(e).then_inc(h, kk))
        return inc.tok

    def dma(self, qn, out, in_, sem, **kw):
        self.ninstr += 1
        self.last_inc[qn] = True
        sem.inc(16)
        self.q[qn].append(
            lambda e, o=out, i=in_, h=sem.h, kw=kw: e.dma_start(o, i, **kw).then_inc(h, 16))
        return sem.tok

    def build(self):
        blk = self.es.enter_context(self.nc.Block())
        for qn, attr in self.QMAP.items():
            ops = self.q[qn]
            if not ops:
                continue

            def make(ops):
                def body(eng):
                    for f in ops:
                        f(eng)
                return body
            getattr(blk, attr)(make(ops))



class Res:

    def __init__(self, p: Prog, resident_x=False, ntiles=NT):
        self.p = p
        self.bnst = p.sb("bnst", [128, 12])
        self.mv = p.sb("mv", [128, 2])
        self.t0 = p.sb("lnt0", [128, 1])
        self.rstd = p.sb("rstd", [128, 4])
        self.lnb = p.sb("lnb", [128, 4])
        self.xln_tm = p.sb("xln_tm", [128, 1024], BF16)
        self.idn = p.sb("idn", [128, 128], BF16)
        self.idf = p.sb("idf", [128, 128])
        self.ones_row = p.sb("ones_row", [1, 128], BF16)
        self.ones_f = p.sb("ones_f", [1, 128])
        self.PS0123 = p.ps("PS0123", [128, 2048])
        self.PS01 = self.PS0123[:, 0:1024]
        self.PS23 = self.PS0123[:, 1024:2048]
        self.PS45 = p.ps("PS45", [128, 1024])
        self.PT = p.ps("PT", [128, 1024], BF16)
        self.PTF = p.ps("PTF", [128, 512])
        self.pt_free = None
        self.ptf_free = None
        self.sv = p.sem("sv")
        self.ss = p.sem("ss")
        self.st = p.sem("st")
        self.sg = p.sem("sg")
        self.dmac = p.sem("dmac")
        self.dmag = p.sem("dmag")
        self.consts_tok = None

    def load_consts(self, p, iden_ap):
        p.dma("y", self.idn[:, :], iden_ap, self.dmac)
        self.consts_tok = self.dmac.tok
        p.wait("s", self.consts_tok)
        p.op("s", lambda e: e.copy(self.idf[:, :], self.idn[:, :]), inc=self.ss)
        p.op("g", lambda e: e.memset(self.ones_row[:, :], 1.0), inc=self.sg)
        p.op("g", lambda e: e.memset(self.ones_f[:, :], 1.0), inc=self.sg)
        self.ones_tok = self.sg.tok
        self.idf_tok = self.ss.tok


def emit_ln(p: Prog, r: Res, x_tile_ap, sub, dst_fm, dst_off, after=(),
            scale_mode=None, war=(), cs=512):
    sv, ss, st = r.sv, r.ss, r.st
    p.wait("v", *[a for a in after if a])
    p.op("v", lambda e: e.bn_stats(r.bnst[:, 0:6], x_tile_ap[:, 0:512]))
    t1 = p.op("v", lambda e: e.bn_stats(r.bnst[:, 6:12], x_tile_ap[:, 512:1024]), inc=sv)
    p.wait("v", t1)
    t2 = p.op("v", lambda e: e.bn_aggr(r.mv[:, 0:2], r.bnst[:, 0:12]), inc=sv)
    p.wait("v", t2)
    t3 = p.op("v", lambda e: e.tensor_scalar(r.t0[:, 0:1], r.mv[:, 1:2], LN_EPS, None,
                                             op0=ALU.add), inc=sv)
    p.wait("s", t3)
    t4 = p.op("s", lambda e: e.activation(r.t0[:, 0:1], r.t0[:, 0:1], AF.Ln), inc=ss)
    p.wait("s", t4)
    t5 = p.op("s", lambda e: e.activation(r.rstd[:, sub:sub + 1], r.t0[:, 0:1], AF.Exp,
                                          scale=-0.5), inc=ss)
    if scale_mode == "rms2":
        EPS32 = float(np.finfo(np.float32).eps)
        p.wait("v", t5)
        t5a = p.op("v", lambda e: e.scalar_tensor_tensor(
            r.t0[:, 0:1], r.mv[:, 1:2], 1.0, r.rstd[:, sub:sub + 1],
            op0=ALU.mult, op1=ALU.mult), inc=sv)
        p.wait("v", t5a)
        t5b = p.op("v", lambda e: e.scalar_tensor_tensor(
            r.t0[:, 0:1], r.t0[:, 0:1], EPS32, r.rstd[:, sub:sub + 1],
            op0=ALU.bypass, op1=ALU.mult) if False else e.tensor_tensor(
            r.t0[:, 0:1], r.t0[:, 0:1], r.rstd[:, sub:sub + 1], op=ALU.mult),
            inc=sv)
        p.wait("v", t5b)
        t5c = p.op("v", lambda e: e.tensor_scalar(r.t0[:, 0:1], r.t0[:, 0:1], EPS32, None,
                                                  op0=ALU.add), inc=sv)
        p.wait("s", t5c)
        t5d = p.op("s", lambda e: e.activation(r.t0[:, 0:1], r.t0[:, 0:1], AF.Ln), inc=ss)
        p.wait("s", t5d)
        t5e = p.op("s", lambda e: e.activation(r.t0[:, 0:1], r.t0[:, 0:1], AF.Exp,
                                               scale=-0.5), inc=ss)
        p.wait("v", t5e)
        t5 = p.op("v", lambda e: e.tensor_tensor(
            r.rstd[:, sub:sub + 1], r.rstd[:, sub:sub + 1], r.t0[:, 0:1],
            op=ALU.mult), inc=sv)
    p.wait("v", t5)
    t6 = p.op("v", lambda e: e.scalar_tensor_tensor(
        r.lnb[:, sub:sub + 1], r.mv[:, 0:1], -1.0, r.rstd[:, sub:sub + 1],
        op0=ALU.mult, op1=ALU.mult), inc=sv)
    p.wait("s", t6)
    t7 = p.op("s", lambda e: e.activation(r.xln_tm[:, :], x_tile_ap[:, 0:1024], AF.Identity,
                                          bias=r.lnb[:, sub:sub + 1],
                                          scale=r.rstd[:, sub:sub + 1]), inc=ss)
    p.wait("t", t7, r.pt_free, r.consts_tok)
    tk = None
    for dc in range(8):
        tk = p.op("t", lambda e, dc=dc: e.transpose(
            r.PT[:, dc * 128:(dc + 1) * 128], r.xln_tm[:, dc * 128:(dc + 1) * 128],
            r.idn[:, :]), inc=(st if dc == 7 else None))
    p.wait("v", tk, *[w for w in war if w])
    dst = AP(dst_fm, dst_off, [[dst_fm.shape[1], 128], [cs, 8], [1, 128]])
    cp = p.op("v", lambda e, d=dst: e.tensor_copy(d, r.PT[:, 0:1024]), inc=sv)
    r.pt_free = cp
    return cp



class XResident:
    def __init__(self, p, r, x_ap=None, y_ap=None, off=0, ntiles=16):
        self.r = r
        self._ready = None
        self.x_ap = x_ap
        self.y_ap = y_ap
        self.off = off
        self.X = p.sb("Xres", [128, ntiles * 1024], F32)
        self.addtok = {}

    def load_all(self, p, r):
        self.lsem = p.sem("xrld")
        nt = self.X.shape[1] // 1024
        for h in range(0, nt, 8):
            n = min(8, nt - h)
            tk = p.dma("y", self.X[:, h * 1024:(h + n) * 1024],
                       AP(self.x_ap, self.off + h * 128 * 1024,
                          [[1024, 128], [128 * 1024, n], [1, 1024]]),
                       self.lsem)
        self._ready = tk

    def store_all(self, p, r, waits=()):
        self.ssem = p.sem("xrst")
        nt = self.X.shape[1] // 1024
        p.wait("y", *waits)
        for h in range(0, nt, 8):
            n = min(8, nt - h)
            tk = p.dma("y", AP(self.y_ap, self.off + h * 128 * 1024,
                               [[1024, 128], [128 * 1024, n], [1, 1024]]),
                       self.X[:, h * 1024:(h + n) * 1024], self.ssem)
        return tk

    def final_toks(self):
        return [self.ssem.tok]

    def prefetch(self, p, r, i):
        pass

    def ready(self, i):
        return self._ready

    def tile(self, p, r, ti):
        return self.X[:, ti * 1024:(ti + 1) * 1024]

    def tile4(self, p, r, i):
        return [self.tile(p, r, 4 * i + s) for s in range(4)]

    def add(self, p, r, ti, psum_ap, wait=(), scale_col=None):
        xt = self.X[:, ti * 1024:(ti + 1) * 1024]
        p.wait("v", *wait, self.addtok.get(ti))
        sc = scale_col if scale_col is not None else 1.0
        tk = p.op("v", lambda e, xt=xt, ps=psum_ap, sc=sc: e.scalar_tensor_tensor(
            xt, ps, sc, xt, op0=ALU.mult, op1=ALU.add), inc=r.sv)
        self.addtok[ti] = tk
        return tk

    def add_half(self, p, r, ti, half, psum_ap, wait=(), scale_col=None):
        xt = self.X[:, ti * 1024 + half * 512:ti * 1024 + (half + 1) * 512]
        p.wait("v", *wait, self.addtok.get((ti, half)))
        sc = scale_col if scale_col is not None else 1.0
        tk = p.op("v", lambda e, xt=xt, ps=psum_ap, sc=sc: e.scalar_tensor_tensor(
            xt, ps, sc, xt, op0=ALU.mult, op1=ALU.add), inc=r.sv)
        self.addtok[(ti, half)] = tk
        return tk

    def flush(self, p, r, i):
        pass

    def loadA(self, p, r, ti):
        return self._ready

    def tileA(self, p, r, ti):
        return self.tile(p, r, ti)

    def consumedA(self, p, r, ti, tok):
        pass

    def new_pass(self):
        pass


class XStream:

    def __init__(self, p, r, x_handle, y_handle, nbuf=2, ngroups=8):
        self.xh = x_handle
        self.yh = y_handle
        self.nbuf = nbuf
        self.ng = ngroups
        self.xb = [p.sb("xs", [128, 4 * 1024]) for i in range(nbuf)]
        self.lsem = [p.sem(f"xld{i}") for i in range(nbuf)]
        self.ssem = [p.sem(f"xst{i}") for i in range(nbuf)]
        self.loaded = {}
        self.stores = {}
        self.addtok = {}

    def _dram_ap(self, h, i):
        return AP(h, (4 * i) * 128 * 1024, [[1024, 128], [128 * 1024, 4], [1, 1024]])

    def prefetch(self, p, r, i):
        if i in self.loaded or i >= self.ng:
            return
        buf = i % self.nbuf
        p.wait("y", self.stores.get(i - self.nbuf))
        self.loaded[i] = p.dma("y", self.xb[buf][:, :], self._dram_ap(self.xh, i),
                               self.lsem[buf])

    def ready(self, i):
        return self.loaded.get(i)

    def tile(self, p, r, ti):
        i, sub = divmod(ti, 4)
        return self.xb[i % self.nbuf][:, sub * 1024:(sub + 1) * 1024]

    def tile4(self, p, r, i):
        return [self.tile(p, r, 4 * i + s) for s in range(4)]

    def add(self, p, r, ti, psum_ap, wait=(), scale_col=None):
        i, sub = divmod(ti, 4)
        dst = self.tile(p, r, ti)
        p.wait("v", *wait, self.addtok.get(ti), self.loaded.get(i))
        sc = scale_col if scale_col is not None else 1.0
        tk = p.op("v", lambda e, d=dst, ps=psum_ap, sc=sc: e.scalar_tensor_tensor(
            d, ps, sc, d, op0=ALU.mult, op1=ALU.add), inc=r.sv)
        self.addtok[ti] = tk
        return tk

    def flush(self, p, r, i):
        buf = i % self.nbuf
        p.wait("y", *[self.addtok.get(i * 4 + s) for s in range(4)])
        self.stores[i] = p.dma("y", self._dram_ap(self.yh, i), self.xb[buf][:, :],
                               self.ssem[buf])

    def final_toks(self):
        return [sm.tok for sm in self.ssem]




class XPhased:

    def __init__(self, p, r, x_handle, y_handle):
        self.xh = x_handle
        self.yh = y_handle
        self.ta = [p.sb("xpa", [128, 1024]) for _ in range(2)]
        self.tasem = [p.sem(f"xpas{i}") for i in range(2)]
        self.tacons = {}
        self.taload = {}
        self.tb = [p.sb("xpb", [128, 4 * 1024]) for _ in range(1)]
        self.tbsem = [p.sem(f"xpbs{i}") for i in range(1)]
        self.tbssem = [p.sem(f"xpbo{i}") for i in range(1)]
        self.tbload = {}
        self.stores = {}
        self.prev_stores = {}
        self.pass_finals = [None]
        self.addtok = {}

    def new_pass(self):
        self.tbload = {}
        self.prev_stores = self.stores
        self.pass_finals = [sm.tok for sm in self.tbssem]
        self.stores = {}
        self.addtok = {}
        self.xh = self.yh

    def loadA(self, p, r, ti):
        if ti in self.taload or ti >= NT:
            return self.taload.get(ti)
        bi = ti % 2
        p.wait("y", self.tacons.get(ti - 2))
        self.taload[ti] = p.dma("y", self.ta[bi][:, :],
                                AP(self.xh, ti * 128 * 1024, [[1024, 128], [1, 1024]]),
                                self.tasem[bi])
        return self.taload[ti]

    def tileA(self, p, r, ti):
        return self.ta[ti % 2]

    def consumedA(self, p, r, ti, tok):
        self.tacons[ti] = tok

    def _loadB(self, p, r, jq):
        if jq in self.tbload:
            return self.tbload[jq]
        bi = 0
        p.wait("y", self.stores.get(jq - 1), self.prev_stores.get(jq),
               self.pass_finals[bi])
        self.tbload[jq] = p.dma(
            "y", self.tb[bi][:, :],
            AP(self.xh, jq * 4 * 128 * 1024, [[1024, 128], [128 * 1024, 4], [1, 1024]]),
            self.tbsem[bi])
        return self.tbload[jq]

    def add(self, p, r, ti, psum_ap, wait=(), scale_col=None):
        jq, s4 = divmod(ti, 4)
        ld = self._loadB(p, r, jq)
        bi = 0
        dst = self.tb[bi][:, s4 * 1024:(s4 + 1) * 1024]
        p.wait("v", *wait, ld, self.addtok.get(ti))
        sc = scale_col if scale_col is not None else 1.0
        tk = p.op("v", lambda e, d=dst, ps=psum_ap, sc=sc: e.scalar_tensor_tensor(
            d, ps, sc, d, op0=ALU.mult, op1=ALU.add), inc=r.sv)
        self.addtok[ti] = tk
        if s4 == 3:
            p.wait("y", tk)
            self.stores[jq] = p.dma(
                "y", AP(self.yh, jq * 4 * 128 * 1024,
                        [[1024, 128], [128 * 1024, 4], [1, 1024]]),
                self.tb[bi][:, :], self.tbssem[bi])
        return tk

    def final_toks(self):
        return [s.tok for s in self.tbssem]



def prep_transpose(p: Prog, r: Res, jobs, waits=(), standalone=True,
                   queues=("y",)):
    es2 = ExitStack() if standalone else None
    nq = len(queues)
    tbuf = [p.sb(f"tts{i}", [128, 2048], BF16, es=es2) for i in range(2 * nq)]
    lsem = [p.sem(f"ttl{i}") for i in range(2 * nq)]
    ssem = p.sem("ttst")
    stored = [None] * (2 * nq)
    for q in queues:
        p.wait(q, *waits)
    i = 0
    last = None
    for job in jobs:
        (src, R, C, dst) = job[:4]
        if len(job) > 4:
            for q in queues:
                p.wait(q, *job[4])
        rchunk = 2048
        while R % rchunk:
            rchunk //= 2
        assert rchunk % 128 == 0
        for c0 in range(0, C, 128):
            for r0 in range(0, R, rchunk):
                bi = i % (2 * nq)
                q = queues[i % nq]
                i += 1
                p.wait(q, stored[bi])
                ld = p.dma(q, tbuf[bi][:, 0:rchunk],
                           src.ap()[r0:r0 + rchunk, c0:c0 + 128], lsem[bi],
                           transpose=True)
                p.wait("g", ld)
                stored[bi] = p.dma("g", dst.ap()[c0:c0 + 128, r0:r0 + rchunk],
                                   tbuf[bi][:, 0:rchunk], ssem)
                last = stored[bi]
    if standalone:
        p.barrier(p.all_toks())
        es2.close()
    return last



def prep_l1ffn_alloc(p: Prog):
    return {"b1sb": p.sb("b1sb", [128, 32]),
            "b2row": p.sb("b2row", [1, 1024], BF16)}


def prep_l1ffn(p: Prog, r: Res, ins, scr, pr=None, standalone=True):
    if pr is None:
        pr = prep_l1ffn_alloc(p)
    b1sb, b2row = pr["b1sb"], pr["b2row"]
    es2 = ExitStack() if standalone else None
    wtmp = [p.sb("pw", [128, 1024], es=es2) for _ in range(2)]
    wsg = p.sb("pwsg", [128, 1024], BF16, es=es2)
    wout = p.sb("pwo", [128, 1024], BF16, es=es2)
    asum = p.sb("pasum", [128, 5], es=es2)
    lsem = [p.sem(f"pld{i}") for i in range(2)]
    stsem = p.sem("pst")
    out_free = None
    sg_free = None
    jobs = [("w1", scr["bw1"], 32, 1), ("w2", scr["bw2"], 8, 4)]
    last_store = None
    buf_users = [None, None]
    for wname, scr_t, nchunk, npiece in jobs:
        w_ap = ins[wname]
        for c in range(nchunk):
            red = []
            for pc in range(npiece):
                buf = wtmp[pc % 2]
                p.wait("y", buf_users[pc % 2])
                ld = p.dma("y", buf[:, :],
                           AP(w_ap, c * 128 * (1024 * npiece) + pc * 1024,
                              [[1024 * npiece, 128], [1, 1024]]), lsem[pc % 2])
                p.wait("v", ld)
                red.append(p.op("v", lambda e, b=buf, pc=pc: e.tensor_reduce(
                    asum[:, pc + 1:pc + 2], b[:, :], axis=AX.X, op=ALU.add,
                    apply_absolute_value=True), inc=r.sv))
                buf_users[pc % 2] = red[-1]
            p.wait("v", red[-1])
            if npiece == 1:
                t = p.op("v", lambda e: e.tensor_copy(asum[:, 0:1], asum[:, 1:2]),
                         inc=r.sv)
            else:
                t = p.op("v", lambda e, np_=npiece: e.tensor_reduce(
                    asum[:, 0:1], asum[:, 1:1 + np_], axis=AX.X, op=ALU.add), inc=r.sv)
            for pc in range(npiece):
                buf = wtmp[pc % 2]
                p.wait("y", buf_users[pc % 2])
                ld = p.dma("y", buf[:, :],
                           AP(w_ap, c * 128 * (1024 * npiece) + pc * 1024,
                              [[1024 * npiece, 128], [1, 1024]]), lsem[pc % 2])
                p.wait("s", ld, sg_free)
                t2 = p.op("s", lambda e, b=buf: e.activation(wsg[:, :], b[:, :], AF.Sign),
                          inc=r.ss)
                buf_users[pc % 2] = t2
                p.wait("v", t2, t, out_free)
                t3 = p.op("v", lambda e, w=1024 * npiece: e.tensor_scalar(
                    wout[:, :], wsg[:, :], asum[:, 0:1], 1.0 / w,
                    op0=ALU.mult, op1=ALU.mult), inc=r.sv)
                sg_free = t3
                p.wait("y", t3)
                last_store = p.dma("y", AP(scr_t, c * 128 * (1024 * npiece) + pc * 1024,
                                           [[1024 * npiece, 128], [1, 1024]]),
                                   wout[:, :], stsem)
                out_free = last_store
    b1t = p.sb("pb1t", [32, 128], es=es2)
    bsem = p.sem("pb1")
    ldb = p.dma("y", b1t[:, :], AP(ins["b1"], 0, [[128, 32], [1, 128]]), bsem)
    p.wait("t", ldb, r.idf_tok, r.ptf_free)
    tk = p.op("t", lambda e: e.transpose(r.PTF[:, 0:32], b1t[:, :], r.idf[0:32, 0:32]),
              inc=r.st)
    p.wait("v", tk)
    tk2 = p.op("v", lambda e: e.tensor_copy(b1sb[:, 0:32], r.PTF[:, 0:32]), inc=r.sv)
    r.ptf_free = tk2
    b2tok = p.dma("g", b2row[:, :], AP(ins["b2"], 0, [[1024, 1], [1, 1024]]), r.dmag)
    if standalone:
        p.barrier(p.all_toks())
        es2.close()
    pr.update({"wdone": last_store, "consts": b2tok, "b1tok": tk2})
    return pr


def mk_l1ffn_bufs(p: Prog):
    return {
        "w1s": [p.sb("w1s", [128, 8 * 512], BF16) for _ in range(2)],
        "w2h": p.sb("w2h", [128, 16 * 1024], BF16),
        "abuf": p.sb("abuf", [128, 32 * 512], BF16),
        "xlnf": p.sb("xlnf", [128, 8 * 512], BF16),
    }


def emit_l1ffn(p: Prog, r: Res, bufs, scr, pr, xio):
    sv, ss, st = r.sv, r.ss, r.st
    w1s, w2h, abuf, xlnf = bufs["w1s"], bufs["w2h"], bufs["abuf"], bufs["xlnf"]
    dma1 = [p.sem(f"w1d{i}") for i in range(2)]
    dma2 = p.sem("w2d")
    w1_loaded = [None, None]
    w1_used = [None, None]
    w2_loaded = None
    w2_used = None

    def load_w1_slice(sl):
        nonlocal w1_loaded
        buf = sl % 2
        p.wait("y", w1_used[buf], pr["wdone"])
        tk = None
        for dc in range(8):
            tk = p.dma("y", w1s[buf][:, dc * 512:(dc + 1) * 512],
                       scr["bw1T"].ap()[dc * 128:(dc + 1) * 128,
                                        sl * 512:(sl + 1) * 512], dma1[buf])
        w1_loaded[buf] = tk

    def load_w2_half(h):
        nonlocal w2_loaded
        p.wait("y", w2_used, pr["wdone"])
        tk = None
        for fc in range(16):
            j = h * 16 + fc
            tk = p.dma("y", w2h[:, fc * 1024:(fc + 1) * 1024],
                       scr["bw2T"].ap()[j * 128:(j + 1) * 128, :], dma2)
        w2_loaded = tk

    PH = r.PS45
    PY = [r.PS01, r.PS23]
    ph_free = [None, None]
    py_free = [None, None]

    load_w1_slice(0)
    xio.prefetch(p, r, 0)
    abuf_free = None
    xlnf_free = None
    for i in range(8):
        xio.prefetch(p, r, i + 1)
        xt = xio.tile4(p, r, i)
        lntok = None
        for sub in range(4):
            lntok = emit_ln(p, r, xt[sub], sub, xlnf, sub * 128,
                            after=(xio.ready(i),), war=(xlnf_free,))
        for sl in range(8):
            buf = sl % 2
            if sl < 7 or i < 7:
                load_w1_slice((sl + 1) % 8)
            for j4 in range(4):
                j = sl * 4 + j4
                par = j % 2
                ph = PH[:, par * 512:(par + 1) * 512]
                p.wait("t", w1_loaded[buf], lntok, ph_free[par])
                tk = None
                for dc in range(8):
                    tk = p.op("t", lambda e, b=buf, dc=dc, j4=j4, ph=ph: e.matmul(
                        ph, w1s[b][:, dc * 512 + j4 * 128: dc * 512 + (j4 + 1) * 128],
                        xlnf[:, dc * 512:(dc + 1) * 512],
                        start=(dc == 0), stop=(dc == 7)),
                        inc=(st if dc == 7 else None))
                p.wait("s", tk, abuf_free)
                gk = p.op("s", lambda e, j=j, ph=ph: e.activation(
                    abuf[:, j * 512:(j + 1) * 512], ph, AF.Gelu,
                    bias=pr["b1sb"][:, j:j + 1]), inc=ss)
                ph_free[par] = gk
            w1_used[buf] = st.tok
        xlnf_free = st.tok
        for h in range(2):
            load_w2_half(h)
            for sub in range(4):
                par = sub % 2
                py = PY[par]
                p.wait("t", w2_loaded, ss.tok, py_free[par])
                started = [False, False]
                if h == 0:
                    p.wait("t", pr["consts"], r.ones_tok)
                    for nh in range(2):
                        p.op("t", lambda e, nh=nh, py=py: e.matmul(
                            py[:, nh * 512:(nh + 1) * 512], r.ones_row[0:1, :],
                            pr["b2row"][0:1, nh * 512:(nh + 1) * 512],
                            start=True, stop=False))
                    started = [True, True]
                tk = None
                for fc in range(16):
                    j = h * 16 + fc
                    for nh in range(2):
                        tk = p.op("t", lambda e, fc=fc, j=j, sub=sub, nh=nh, py=py,
                                  st_=(not started[nh] and fc == 0): e.matmul(
                            py[:, nh * 512:(nh + 1) * 512],
                            abuf[:, j * 512 + sub * 128: j * 512 + (sub + 1) * 128],
                            w2h[:, fc * 1024 + nh * 512: fc * 1024 + (nh + 1) * 512],
                            start=st_, stop=(fc == 15)),
                            inc=(st if (fc == 15 and nh == 1) else None))
                py_free[par] = xio.add(p, r, i * 4 + sub, py[:, 0:1024], wait=(tk,))
            w2_used = st.tok
        abuf_free = st.tok
        xio.flush(p, r, i)
    return



def np_consts():
    import ml_dtypes
    return {"c_iden": np.eye(128, dtype=ml_dtypes.bfloat16)}


def build_l1ffn_program():
    nc = bass.Bass("TRN2", target_bir_lowering=False)
    es = ExitStack()
    with es:
        p = Prog(nc, es)
        ins = {
            "x": p.dram("x", [T, D], kind="ExternalInput"),
            "w1": p.dram("w1", [4096, 1024], kind="ExternalInput"),
            "b1": p.dram("b1", [4096], kind="ExternalInput"),
            "w2": p.dram("w2", [1024, 4096], kind="ExternalInput"),
            "b2": p.dram("b2", [1024], kind="ExternalInput"),
            "c_iden": p.dram("c_iden", [128, 128], BF16, kind="ExternalInput"),
        }
        y = p.dram("y", [T, D], kind="ExternalOutput")
        scr = {
            "bw1": p.dram("bw1", [4096, 1024], BF16),
            "bw2": p.dram("bw2", [1024, 4096], BF16),
            "bw1T": p.dram("bw1T", [1024, 4096], BF16),
            "bw2T": p.dram("bw2T", [4096, 1024], BF16),
        }
        r = Res(p, resident_x=False)
        r.load_consts(p, ins["c_iden"].ap())
        pr = prep_l1ffn(p, r, ins, scr)
        prep_transpose(p, r, [(scr["bw1"], 4096, 1024, scr["bw1T"]),
                              (scr["bw2"], 1024, 4096, scr["bw2T"])])
        bufs = mk_l1ffn_bufs(p)
        xio = XStream(p, r, ins["x"], y)
        emit_l1ffn(p, r, bufs, scr, pr, xio)
        p.wait("y", *xio.final_toks())
        p.build()
    return nc, p.ninstr



def prep_moe_alloc(p: Prog):
    return {"multT": p.sb("multT", [128, 15 * 8 + 16]),
            "fc1b": p.sb("fc1b", [128, 15 * 16 + 32]),
            "gwT": p.sb("gwT", [128, 8 * 15], BF16),
            "bmat": p.sb("bmat", [16, 1024], BF16)}


def prep_moe(p: Prog, r: Res, ins, scr, pr=None, standalone=True):
    lsem = [p.sem(f"mld{i}") for i in range(2)]
    stsem = p.sem("mst")
    csem = p.sem("mcs")
    if pr is None:
        pr = prep_moe_alloc(p)
    multT, fc1b, gwT, bmat = pr["multT"], pr["fc1b"], pr["gwT"], pr["bmat"]
    es2 = ExitStack() if standalone else None
    wtmp = [p.sb("mpw", [128, 1024], es=es2) for _ in range(2)]
    wob = p.sb("mpo", [128, 1024], BF16, es=es2)
    mrow = p.sb("mrow", [16, 128], es=es2)
    grow = p.sb("grow", [15, 1024], es=es2)
    buf_users = [None, None]
    out_free = None
    last_store = None
    tposed = None
    for e in range(16):
        if e < 15:
            src = AP(ins["geglu_mult"], e * 1024, [[128, 8], [1, 128]])
            ncol, dst0 = 8, e * 8
        else:
            src = AP(ins["s_geglu_mult"], 0, [[128, 16], [1, 128]])
            ncol, dst0 = 16, 15 * 8
        p.wait("y", tposed)
        ld = p.dma("y", mrow[0:ncol, :], src, csem)
        p.wait("t", ld, r.idf_tok, r.ptf_free)
        tk = p.op("t", lambda e2, n=ncol: e2.transpose(
            r.PTF[:, 0:n], mrow[0:n, :], r.idf[0:n, 0:n]), inc=r.st)
        p.wait("v", tk)
        tposed = p.op("v", lambda e2, d0=dst0, n=ncol: e2.tensor_copy(
            multT[:, d0:d0 + n], r.PTF[:, 0:n]), inc=r.sv)
        r.ptf_free = tposed

    for e in range(15):
        for j in range(8):
            bi = j % 2
            p.wait("y", buf_users[bi])
            ld = p.dma("y", wtmp[bi][:, :],
                       AP(ins["fc1_w"], (e * 2048 + j * 128) * 1024,
                          [[1024, 128], [1, 1024]]), lsem[bi])
            p.wait("v", ld, tposed, out_free)
            t3 = p.op("v", lambda e2, b=bi, e_=e, j_=j: e2.tensor_scalar(
                wob[:, :], wtmp[b][:, :], multT[:, e_ * 8 + j_:e_ * 8 + j_ + 1], None,
                op0=ALU.mult), inc=r.sv)
            buf_users[bi] = t3
            p.wait("y", t3)
            last_store = p.dma("y", AP(scr["fc1"], (e * 2048 + j * 128) * 1024,
                                       [[1024, 128], [1, 1024]]), wob[:, :], stsem)
            out_free = last_store
        p.dma("g", AP(scr["fc1"], (e * 2048 + 1024) * 1024, [[1024, 1024], [1, 1024]]),
              AP(ins["fc1_w"], (e * 2048 + 1024) * 1024, [[1024, 1024], [1, 1024]]),
              r.dmag)
    jtok = {"fc1": (last_store, r.dmag.tok)}
    for j in range(16):
        bi = j % 2
        p.wait("y", buf_users[bi])
        ld = p.dma("y", wtmp[bi][:, :],
                   AP(ins["s_fc1_w"], (j * 128) * 1024, [[1024, 128], [1, 1024]]),
                   lsem[bi])
        p.wait("v", ld, tposed, out_free)
        t3 = p.op("v", lambda e2, b=bi, j_=j: e2.tensor_scalar(
            wob[:, :], wtmp[b][:, :], multT[:, 120 + j_:120 + j_ + 1], None,
            op0=ALU.mult), inc=r.sv)
        buf_users[bi] = t3
        p.wait("y", t3)
        last_store = p.dma("y", AP(scr["s_fc1"], (j * 128) * 1024,
                                   [[1024, 128], [1, 1024]]), wob[:, :], stsem)
        out_free = last_store
    p.dma("g", AP(scr["s_fc1"], 2048 * 1024, [[1024, 2048], [1, 1024]]),
          AP(ins["s_fc1_w"], 2048 * 1024, [[1024, 2048], [1, 1024]]), r.dmag)
    jtok["s_fc1"] = (last_store, r.dmag.tok)
    p.dma("g", AP(scr["fc2"], 0, [[1024, 15 * 1024], [1, 1024]]),
          AP(ins["fc2_w"], 0, [[1024, 15 * 1024], [1, 1024]]), r.dmag)
    jtok["fc2"] = (r.dmag.tok,)
    p.dma("g", AP(scr["s_fc2"], 0, [[2048, 1024], [1, 2048]]),
          AP(ins["s_fc2_w"], 0, [[2048, 1024], [1, 2048]]), r.dmag)
    jtok["s_fc2"] = (r.dmag.tok,)

    tp2 = tposed

    def thin_cols(src_ap, dst0, n):
        nonlocal tp2
        p.wait("y", tp2)
        ld = p.dma("y", mrow[0:n, :], src_ap, csem)
        p.wait("t", ld, r.ptf_free)
        tk = p.op("t", lambda e2, n=n: e2.transpose(r.PTF[:, 0:n], mrow[0:n, :],
                                                    r.idf[0:n, 0:n]), inc=r.st)
        p.wait("v", tk)
        tp2 = p.op("v", lambda e2, d0=dst0, n=n: e2.tensor_copy(
            fc1b[:, d0:d0 + n], r.PTF[:, 0:n]), inc=r.sv)
        r.ptf_free = tp2
        return tp2

    for e in range(15):
        thin_cols(AP(ins["fc1_b"], e * 2048, [[128, 16], [1, 128]]), e * 16, 16)
    thin_cols(AP(ins["s_fc1_b"], 0, [[128, 16], [1, 128]]), 240, 16)
    thin_cols(AP(ins["s_fc1_b"], 2048, [[128, 16], [1, 128]]), 256, 16)
    bs = tp2
    for e in range(15):
        p.wait("v", bs)
        bs = p.op("v", lambda e2, e_=e: e2.tensor_tensor(
            fc1b[:, e_ * 16:e_ * 16 + 8], fc1b[:, e_ * 16:e_ * 16 + 8],
            multT[:, e_ * 8:e_ * 8 + 8], op=ALU.mult), inc=r.sv)
    p.wait("v", bs)
    bs = p.op("v", lambda e2: e2.tensor_tensor(
        fc1b[:, 240:256], fc1b[:, 240:256], multT[:, 120:136], op=ALU.mult), inc=r.sv)

    p.wait("y", bs, tp2)
    ldg = p.dma("y", grow[:, :], AP(ins["gate_w"], 0, [[1024, 15], [1, 1024]]), csem)
    gt = None
    for dc in range(8):
        p.wait("t", ldg, r.ptf_free)
        tk = p.op("t", lambda e2, dc=dc: e2.transpose(
            r.PTF[:, 0:15], grow[0:15, dc * 128:(dc + 1) * 128], r.idf[0:15, 0:15]),
            inc=r.st)
        p.wait("v", tk, gt)
        gt = p.op("v", lambda e2, dc=dc: e2.tensor_copy(
            gwT[:, dc * 15:(dc + 1) * 15], r.PTF[:, 0:15]), inc=r.sv)
        r.ptf_free = gt

    p.dma("g", bmat[0:15, :], AP(ins["fc2_b"], 0, [[1024, 15], [1, 1024]]), r.dmag)
    p.dma("g", bmat[15:16, :], AP(ins["s_fc2_b"], 0, [[1024, 1], [1, 1024]]), r.dmag)

    if standalone:
        es2.close()
        p.barrier([r.sv.tok, r.ss.tok, r.st.tok, r.sg.tok, lsem[0].tok,
                   lsem[1].tok, stsem.tok, csem.tok, r.dmag.tok, r.dmac.tok])
    pr.update({"wdone": last_store, "castdone": r.dmag.tok, "biasdone": bs,
               "gatedone": gt, "jtok": jtok})
    return pr


def prep_moe_pool(p: Prog, r: Res, ins, scr, pr, pacer=()):
    multT, fc1b, gwT, bmat = pr["multT"], pr["fc1b"], pr["gwT"], pr["bmat"]
    gsem = p.sem("mpg")
    ldsem = p.sem("mpl")
    stsem = p.sem("mps")
    csem = p.sem("mpc")
    wtmp = [p.sb(f"mpw{i}", [128, 1024]) for i in range(2)]
    wob = [p.sb(f"mpo{i}", [128, 1024], BF16) for i in range(2)]
    TOTAL = 648 + 90 + 136 * 8
    state = {"step": 0, "next": 0}

    def pace(w):
        state["step"] += w
        if pacer and state["step"] >= state["next"]:
            j = min(len(pacer) - 1, state["step"] * len(pacer) // TOTAL)
            p.wait("g", pacer[j])
            state["next"] = (j + 1) * TOTAL // len(pacer)
    for e in range(15):
        pace(5)
        p.dma("g", AP(scr["fc1"], (e * 2048 + 1024) * 1024,
                      [[1024, 1024], [1, 1024]]),
              AP(ins["fc1_w"], (e * 2048 + 1024) * 1024,
                 [[1024, 1024], [1, 1024]]), r.dmag)
    pace(5)
    p.dma("g", AP(scr["s_fc1"], 2048 * 1024, [[1024, 2048], [1, 1024]]),
          AP(ins["s_fc1_w"], 2048 * 1024, [[1024, 2048], [1, 1024]]), r.dmag)
    pace(5)
    p.dma("g", AP(scr["fc2"], 0, [[1024, 15 * 1024], [1, 1024]]),
          AP(ins["fc2_w"], 0, [[1024, 15 * 1024], [1, 1024]]), r.dmag)
    pace(5)
    p.dma("g", AP(scr["s_fc2"], 0, [[2048, 1024], [1, 2048]]),
          AP(ins["s_fc2_w"], 0, [[2048, 1024], [1, 2048]]), r.dmag)
    tk = None
    for e in range(15):
        for j in range(8):
            pace(1)
            tk = p.dma("g", multT[:, e * 8 + j:e * 8 + j + 1],
                       AP(ins["geglu_mult"], e * 1024 + j * 128,
                          [[1, 128], [1, 1]]), csem)
    for j in range(16):
        pace(1)
        tk = p.dma("g", multT[:, 120 + j:121 + j],
                   AP(ins["s_geglu_mult"], j * 128, [[1, 128], [1, 1]]), csem)
    multT_tok = tk
    for e in range(15):
        for j in range(16):
            pace(1)
            f0 = j * 128 if j < 8 else 1024 + (j - 8) * 128
            tk = p.dma("g", fc1b[:, e * 16 + j:e * 16 + j + 1],
                       AP(ins["fc1_b"], e * 2048 + f0, [[1, 128], [1, 1]]), csem)
    for j in range(16):
        pace(1)
        tk = p.dma("g", fc1b[:, 240 + j:241 + j],
                   AP(ins["s_fc1_b"], j * 128, [[1, 128], [1, 1]]), csem)
    for j in range(16):
        pace(1)
        tk = p.dma("g", fc1b[:, 256 + j:257 + j],
                   AP(ins["s_fc1_b"], 2048 + j * 128, [[1, 128], [1, 1]]), csem)
    p.wait("g", tk, multT_tok)
    bs = None
    for e in range(15):
        bs = p.op("g", lambda e2, e_=e: e2.tensor_tensor(
            fc1b[:, e_ * 16:e_ * 16 + 8], fc1b[:, e_ * 16:e_ * 16 + 8],
            multT[:, e_ * 8:e_ * 8 + 8], op=ALU.mult), inc=gsem)
    bs = p.op("g", lambda e2: e2.tensor_tensor(
        fc1b[:, 240:256], fc1b[:, 240:256], multT[:, 120:136], op=ALU.mult),
        inc=gsem)
    gt = None
    for dc in range(8):
        for e in range(15):
            pace(1)
            gt = p.dma("g", gwT[:, dc * 15 + e:dc * 15 + e + 1],
                       AP(ins["gate_w"], e * 1024 + dc * 128, [[1, 128], [1, 1]]),
                       csem)
    p.dma("g", bmat[0:15, :], AP(ins["fc2_b"], 0, [[1024, 15], [1, 1024]]), r.dmag)
    p.dma("g", bmat[15:16, :], AP(ins["s_fc2_b"], 0, [[1024, 1], [1, 1024]]),
          r.dmag)
    chunks = [(ins["fc1_w"], (e * 2048 + j * 128) * 1024, e * 8 + j,
               scr["fc1"], (e * 2048 + j * 128) * 1024)
              for e in range(15) for j in range(8)]
    chunks += [(ins["s_fc1_w"], (j * 128) * 1024, 120 + j,
                scr["s_fc1"], (j * 128) * 1024) for j in range(16)]
    loads = {}

    def post_load(k):
        if k >= len(chunks):
            return
        src_t, src_off = chunks[k][0], chunks[k][1]
        loads[k] = p.dma("g", wtmp[k % 2][:, :],
                         AP(src_t, src_off, [[1024, 128], [1, 1024]]), ldsem)
    post_load(0)
    last_store = None
    wob_store = [None, None]
    for k in range(len(chunks)):
        pace(8)
        post_load(k + 1)
        bi = k % 2
        _, _, mcol, dst_t, dst_off = chunks[k]
        p.wait("g", loads[k], multT_tok, wob_store[bi])
        p.op("g", lambda e2, b=bi, c=mcol: e2.tensor_scalar(
            wob[b][:, :], wtmp[b][:, :], multT[:, c:c + 1], None,
            op0=ALU.mult), inc=gsem)
        last_store = p.dma("g", AP(dst_t, dst_off, [[1024, 128], [1, 1024]]),
                           wob[bi][:, :], stsem)
        wob_store[bi] = last_store
    pr.update({"wdone": last_store, "castdone": r.dmag.tok, "biasdone": bs,
               "gatedone": gt})
    return pr


def emit_moe(p: Prog, r: Res, scr, pr, xio, tiles):
    sv, ss, st = r.sv, r.ss, r.st
    nsub = len(tiles)
    ng = nsub // 4
    xscr = scr["xln"]
    carr = p.sb("carr", [128, nsub * 16])
    cT = p.sb("cT", [16, nsub * 128], BF16)
    work = p.sb("mwork", [128, 15])
    mtmp = p.sb("mmask", [128, 15])
    cbuf = p.sb("cbuf", [128, 15])
    ctmp2 = p.sb("ctmp2", [128, 16])
    cns = p.sb("cns", [128, 4])
    xlnf = p.sb("xlnf_moe", [128, 8 * 128], BF16)
    xsem = p.sem("xlw")
    mk = p.op("g", lambda e: e.memset(ctmp2[:, 15:16], 1.0), inc=r.sg)

    lgp = r.PS45
    lg_free = None
    ct_free = mk
    xst_prev = None
    for s_i, ti in enumerate(tiles):
        lntok = emit_ln(p, r, xio.tile(p, r, ti), s_i % 4, xlnf, 0,
                        after=(xio.ready(ti // 4), pr["wdone"], pr["castdone"],
                               pr["biasdone"], pr["gatedone"]),
                        war=(xst_prev, lg_free), cs=128)
        p.wait("y", lntok)
        xst_prev = p.dma("y", AP(xscr, s_i * 128,
                                 [[nsub * 128, 128], [128 * nsub * 128, 8], [1, 128]]),
                         xlnf[:, :], xsem)
        p.wait("t", lntok, lg_free)
        tk = None
        for dc in range(8):
            tk = p.op("t", lambda e, dc=dc: e.matmul(
                lgp[:, 0:15], xlnf[:, dc * 128:(dc + 1) * 128],
                pr["gwT"][:, dc * 15:(dc + 1) * 15],
                start=(dc == 0), stop=(dc == 7)), inc=(st if dc == 7 else None))
        p.wait("v", tk)
        q = [None]

        def vstep(fn):
            p.wait("v", q[0])
            q[0] = p.op("v", fn, inc=sv)
            return q[0]

        vstep(lambda e: e.reduce_max(cns[:, 0:1], lgp[:, 0:15], axis=AX.X))
        vstep(lambda e: e.tensor_scalar(cns[:, 1:2], cns[:, 0:1], -1.0, None,
                                        op0=ALU.mult))
        p.wait("s", q[0])
        t3 = p.op("s", lambda e: e.activation(cbuf[:, 0:15], lgp[:, 0:15], AF.Exp,
                                              bias=cns[:, 1:2]), inc=ss)
        lg_free = t3
        p.wait("v", t3)
        q[0] = t3
        vstep(lambda e: e.tensor_copy(work[:, :], cbuf[:, 0:15]))
        vstep(lambda e: e.reduce_max(cns[:, 0:1], work[:, :], axis=AX.X))
        vstep(lambda e: e.tensor_scalar(mtmp[:, :], work[:, :], cns[:, 0:1], None,
                                        op0=ALU.is_ge))
        vstep(lambda e: e.scalar_tensor_tensor(work[:, :], mtmp[:, :], -2.0,
                                               work[:, :], op0=ALU.mult, op1=ALU.add))
        vstep(lambda e: e.reduce_max(cns[:, 0:1], work[:, :], axis=AX.X))
        vstep(lambda e: e.tensor_scalar(mtmp[:, :], work[:, :], cns[:, 0:1], None,
                                        op0=ALU.is_ge))
        vstep(lambda e: e.scalar_tensor_tensor(work[:, :], mtmp[:, :], -2.0,
                                               work[:, :], op0=ALU.mult, op1=ALU.add))
        vstep(lambda e: e.reduce_max(cns[:, 2:3], work[:, :], axis=AX.X))
        vstep(lambda e: e.tensor_scalar(mtmp[:, :], cbuf[:, 0:15], cns[:, 2:3], None,
                                        op0=ALU.is_ge))
        vstep(lambda e: e.tensor_tensor(cbuf[:, 0:15], cbuf[:, 0:15], mtmp[:, :],
                                        op=ALU.mult))
        vstep(lambda e: e.reduce_sum(cns[:, 3:4], cbuf[:, 0:15], axis=AX.X))
        vstep(lambda e: e.reciprocal(cns[:, 3:4], cns[:, 3:4]))
        vstep(lambda e, s_=s_i: e.tensor_scalar(carr[:, s_ * 16:s_ * 16 + 15],
                                                cbuf[:, 0:15], cns[:, 3:4], None,
                                                op0=ALU.mult))
        p.wait("v", ct_free)
        t16 = vstep(lambda e, s_=s_i: e.tensor_copy(
            ctmp2[:, 0:15], carr[:, s_ * 16:s_ * 16 + 15]))
        p.wait("t", t16, r.ptf_free)
        t17 = p.op("t", lambda e: e.transpose(r.PTF[0:16, 0:128], ctmp2[:, 0:16],
                                              r.idf[:, :]), inc=st)
        p.wait("v", t17)
        ct_free = p.op("v", lambda e, s_=s_i: e.tensor_copy(
            cT[:, s_ * 128:(s_ + 1) * 128], r.PTF[0:16, 0:128]), inc=sv)
        r.ptf_free = ct_free

    xlt = [p.sb("xlt", [128, 8 * 512], BF16) for _ in range(2)]
    xlsem = [p.sem(f"xl2{i}") for i in range(2)]
    xl_used = [None, None]
    xl_loaded = {}

    def load_xln(tg):
        if tg in xl_loaded or tg >= ng:
            return
        bi = tg % 2
        p.wait("y", xl_used[bi], xst_prev)
        xl_loaded[tg] = p.dma(
            "y", xlt[bi][:, :],
            AP(xscr, tg * 512, [[nsub * 128, 128], [128 * nsub * 128, 8], [1, 512]]),
            xlsem[bi])

    gel = [p.sb(f"gel{i}", [128, 512], BF16) for i in range(2)]
    PHs = [r.PS45, r.PS23]
    PY = r.PS01
    ph_free = [(lg_free,), ()]
    prev_ak = [None, None]
    py_free = [None, None]

    def fc1_chunk(wbuf, ws, col_xh, col_g, bias_xh, bias_g, xbi, dst_a, waitw,
                  a_war=None, slot=0):
        PH = PHs[slot]
        p.wait("t", waitw, xl_loaded.get(xbi[0]), *ph_free[slot])
        tkx = tkg = None
        for dc in range(8):
            tkx = p.op("t", lambda e, dc=dc, w=wbuf, ws=ws, c=col_xh, b=xbi[1],
                       PH=PH: e.matmul(
                PH[:, 0:512], w[:, dc * ws + c:dc * ws + c + 128],
                xlt[b][:, dc * 512:(dc + 1) * 512],
                start=(dc == 0), stop=(dc == 7)), inc=(st if dc == 7 else None))
        for dc in range(8):
            tkg = p.op("t", lambda e, dc=dc, w=wbuf, ws=ws, c=col_g, b=xbi[1],
                       PH=PH: e.matmul(
                PH[:, 512:1024], w[:, dc * ws + c:dc * ws + c + 128],
                xlt[b][:, dc * 512:(dc + 1) * 512],
                start=(dc == 0), stop=(dc == 7)), inc=(st if dc == 7 else None))
        p.wait("s", tkg, prev_ak[slot])
        gk = p.op("s", lambda e, PH=PH, slot=slot: e.activation(
            gel[slot][:, :], PH[:, 512:1024], AF.Gelu, bias=bias_g), inc=ss)
        p.wait("v", tkx, gk, a_war)
        ak = p.op("v", lambda e, d=dst_a, PH=PH, slot=slot: e.scalar_tensor_tensor(
            d, PH[:, 0:512], bias_xh, gel[slot][:, :],
            op0=ALU.add, op1=ALU.mult), inc=sv)
        ph_free[slot] = (gk, ak)
        prev_ak[slot] = ak
        return ak

    def fc2_sub(tg, sub, nj, lhs_a, w2buf, scale_col, bias_lhs=None, bias_rhs=None):
        ti = tiles[tg * 4 + sub]
        py = PY
        for nh in range(2):
            p.wait("t", py_free[nh], pr["castdone"])
            first = True
            if bias_lhs is not None:
                p.op("t", lambda e, nh=nh, py=py: e.matmul(
                    py[:, nh * 512:(nh + 1) * 512], bias_lhs,
                    bias_rhs[:, nh * 512:(nh + 1) * 512], start=True, stop=False))
                first = False
            tk = None
            for j in range(nj):
                tk = p.op("t", lambda e, j=j, nh=nh, py=py, fs=(first and j == 0):
                          e.matmul(
                    py[:, nh * 512:(nh + 1) * 512],
                    lhs_a(j, sub),
                    w2buf[:, j * 1024 + nh * 512: j * 1024 + (nh + 1) * 512],
                    start=fs, stop=(j == nj - 1)),
                    inc=(st if j == nj - 1 else None))
            py_free[nh] = xio.add_half(p, r, ti, nh, py[:, nh * 512:(nh + 1) * 512],
                                       wait=(tk,), scale_col=scale_col)
        return py_free[1]

    asb = p.sb("asb", [128, 16 * 512], BF16)
    es_sh = ExitStack()
    s1b = [p.sb("s1b", [128, 8 * 1024], BF16, es=es_sh) for _ in range(2)]
    s2b = p.sb("s2b", [128, 8 * 1024], BF16, es=es_sh)
    s1sem = [p.sem(f"s1d{i}") for i in range(2)]
    s2sem = p.sem("s2d")
    s1_used = [None, None]
    s2_used = None
    s1_loaded = [None, None]
    s2_loaded = None

    def load_s1(k):
        nonlocal s1_loaded
        bi = k % 2
        p.wait("y", s1_used[bi], pr["wdone"], pr["castdone"])
        tk = None
        for dc in range(8):
            p.dma("y", s1b[bi][:, dc * 1024:dc * 1024 + 512],
                  scr["s_fc1T"].ap()[dc * 128:(dc + 1) * 128,
                                     k * 512:(k + 1) * 512], s1sem[bi])
            tk = p.dma("y", s1b[bi][:, dc * 1024 + 512:dc * 1024 + 1024],
                       scr["s_fc1T"].ap()[dc * 128:(dc + 1) * 128,
                                          2048 + k * 512:2048 + (k + 1) * 512],
                       s1sem[bi])
        s1_loaded[bi] = tk

    def load_s2(h):
        nonlocal s2_loaded
        p.wait("y", s2_used, pr["castdone"])
        tk = None
        for fc in range(8):
            tk = p.dma("y", s2b[:, fc * 1024:(fc + 1) * 1024],
                       scr["s_fc2T"].ap()[(h * 8 + fc) * 128:(h * 8 + fc + 1) * 128, :],
                       s2sem)
        s2_loaded = tk

    load_s1(0)
    load_xln(0)
    a_war = None
    for tg in range(ng):
        load_xln(tg + 1)
        for k in range(4):
            if k < 3:
                load_s1(k + 1)
            elif tg < ng - 1:
                load_s1(0)
            bi = k % 2
            for j4 in range(4):
                j = k * 4 + j4
                fc1_chunk(s1b[bi], 1024, j4 * 128, 512 + j4 * 128,
                          pr["fc1b"][:, 240 + j:241 + j], pr["fc1b"][:, 256 + j:257 + j],
                          (tg, tg % 2), asb[:, j * 512:(j + 1) * 512],
                          s1_loaded[bi], a_war=a_war, slot=j % 2)
            s1_used[bi] = st.tok
        for h in range(2):
            load_s2(h)
            p.wait("t", s2_loaded, sv.tok)
            for sub in range(4):
                bl = cT[0:16, (tg * 4 + sub) * 128:(tg * 4 + sub + 1) * 128] \
                    if h == 0 else None
                fc2_sub(tg, sub, 8,
                        lambda j, sub_=sub, h_=h: asb[:, (h_ * 8 + j) * 512 + sub_ * 128:
                                                      (h_ * 8 + j) * 512 + (sub_ + 1) * 128],
                        s2b, None, bias_lhs=bl, bias_rhs=pr["bmat"])
            s2_used = st.tok
        xl_used[tg % 2] = st.tok
        a_war = st.tok
    xl_loaded.clear()
    shared_done = st.tok
    p.barrier(p.all_toks())
    es_sh.close()

    f1b = p.sb("ef1", [128, 8 * 2048], BF16)
    f2b = p.sb("ef2", [128, 8 * 1024], BF16)
    abuf = asb
    f1sem = p.sem("f1d")
    f2sem = p.sem("f2d")
    f1_used = None
    f2_used = None

    for e in range(15):
        p.wait("y", f1_used, pr["wdone"], pr["castdone"])
        tk1 = None
        for dc in range(8):
            tk1 = p.dma("y", f1b[:, dc * 2048:(dc + 1) * 2048],
                        scr["fc1T"].ap()[dc * 128:(dc + 1) * 128,
                                         e * 2048:(e + 1) * 2048], f1sem)
        p.wait("y", f2_used)
        tk2 = None
        for fc in range(8):
            tk2 = p.dma("y", f2b[:, fc * 1024:(fc + 1) * 1024],
                        scr["fc2T"].ap()[fc * 128:(fc + 1) * 128,
                                         e * 1024:(e + 1) * 1024], f2sem)
        load_xln(0)
        for tg in range(ng):
            load_xln(tg + 1)
            for j in range(8):
                fc1_chunk(f1b, 2048, j * 128, 1024 + j * 128,
                          pr["fc1b"][:, e * 16 + j:e * 16 + j + 1],
                          pr["fc1b"][:, e * 16 + 8 + j:e * 16 + 8 + j + 1],
                          (tg, tg % 2), abuf[:, j * 512:(j + 1) * 512], tk1,
                          a_war=a_war, slot=j % 2)
            p.wait("t", tk2, sv.tok)
            for sub in range(4):
                fc2_sub(tg, sub, 8,
                        lambda j, sub_=sub: abuf[:, j * 512 + sub_ * 128:
                                                 j * 512 + (sub_ + 1) * 128],
                        f2b, carr[:, (tg * 4 + sub) * 16 + e:
                                  (tg * 4 + sub) * 16 + e + 1])
            xl_used[tg % 2] = st.tok
            a_war = st.tok
        xl_loaded.clear()
        f1_used = st.tok
        f2_used = st.tok


def build_moe_program(half):
    nc = bass.Bass("TRN2", target_bir_lowering=False)
    es = ExitStack()
    names = ["gate_w", "fc1_w", "fc1_b", "geglu_mult", "fc2_w", "fc2_b",
             "s_fc1_w", "s_fc1_b", "s_geglu_mult", "s_fc2_w", "s_fc2_b"]
    shapes = {"gate_w": [15, 1024], "fc1_w": [15 * 2048, 1024], "fc1_b": [15 * 2048],
              "geglu_mult": [15 * 1024], "fc2_w": [15 * 1024, 1024],
              "fc2_b": [15 * 1024], "s_fc1_w": [4096, 1024], "s_fc1_b": [4096],
              "s_geglu_mult": [2048], "s_fc2_w": [1024, 2048], "s_fc2_b": [1024]}
    with es:
        p = Prog(nc, es)
        ins = {n: p.dram(n, shapes[n], kind="ExternalInput") for n in names}
        ins["x"] = p.dram("x", [2048, D], kind="ExternalInput")
        ins["c_iden"] = p.dram("c_iden", [128, 128], BF16, kind="ExternalInput")
        y = p.dram("y", [2048, D], kind="ExternalOutput")
        scr = {
            "fc1": p.dram("s_fc1s", [15 * 2048, 1024], BF16),
            "fc2": p.dram("s_fc2s", [15 * 1024, 1024], BF16),
            "s_fc1": p.dram("s_sfc1s", [4096, 1024], BF16),
            "s_fc2": p.dram("s_sfc2s", [1024, 2048], BF16),
            "fc1T": p.dram("s_fc1Ts", [1024, 15 * 2048], BF16),
            "fc2T": p.dram("s_fc2Ts", [1024, 15 * 1024], BF16),
            "s_fc1T": p.dram("s_sfc1Ts", [1024, 4096], BF16),
            "s_fc2T": p.dram("s_sfc2Ts", [2048, 1024], BF16),
            "xln": p.dram("s_xln", [8 * 128 * 2048], BF16),
        }
        r = Res(p, resident_x=True, ntiles=16)
        r.load_consts(p, ins["c_iden"].ap())
        pr = prep_moe(p, r, ins, scr)
        prep_transpose(p, r, [
            (scr["fc1"], 15 * 2048, 1024, scr["fc1T"]),
            (scr["fc2"], 15 * 1024, 1024, scr["fc2T"]),
            (scr["s_fc1"], 4096, 1024, scr["s_fc1T"]),
            (scr["s_fc2"], 1024, 2048, scr["s_fc2T"]),
        ])
        xio = XResident(p, r, x_ap=ins["x"], y_ap=y, off=0, ntiles=16)
        xio.load_all(p, r)
        emit_moe(p, r, scr, pr, xio, list(range(16)))
        xio.store_all(p, r, waits=(r.sv.tok, r.ss.tok, r.st.tok))
        p.wait("y", *xio.final_toks())
        p.build()
    return nc, p.ninstr



def prep_attn_w(p: Prog, r: Res, ins, scr, has_bias):
    p.dma("g", AP(scr["wqkv"], 0, [[3072, 1024], [1, 3072]]),
          AP(ins["wqkv"], 0, [[3072, 1024], [1, 3072]]), r.dmag)
    cd = p.dma("g", AP(scr["wout"], 0, [[1024, 1024], [1, 1024]]),
               AP(ins["wout"], 0, [[1024, 1024], [1, 1024]]), r.dmag)
    pr = {"castdone": cd}
    if has_bias:
        brow = p.sb("brow", [1, 1024], BF16)
        pr["bout"] = brow
        pr["btok"] = p.dma("g", brow[:, :], AP(ins["bout"], 0, [[1024, 1], [1, 1024]]),
                           r.dmag)
        pr["castdone"] = pr["btok"]
    return pr


def emit_attn_window(p: Prog, r: Res, scr, pr, xio, kind, consts):
    sv, ss, st = r.sv, r.ss, r.st
    rope = (kind == "l0aa")
    lag = 1 if kind == "l0aa" else 0
    wqkv = p.sb("wqkv", [128, 8 * 3072], BF16)
    wout = p.sb("wout", [128, 8 * 1024], BF16)
    qsem = p.sem("aw_wd")
    wtok = None
    p.wait("y", pr["castdone"])
    for dc in range(8):
        p.dma("y", wqkv[:, dc * 3072:(dc + 1) * 3072],
              scr["wqkv"].ap()[dc * 128:(dc + 1) * 128, :], qsem)
        wtok = p.dma("y", wout[:, dc * 1024:(dc + 1) * 1024],
                     scr["wout"].ap()[dc * 128:(dc + 1) * 128, :], qsem)
    masks = p.sb("masks", [128, 3 * 384], BF16)
    msem = p.sem("aw_mk")
    mtok = None
    for mv in range(3):
        mtok = p.dma("y", masks[:, mv * 384:(mv + 1) * 384],
                     AP(consts["masks"], mv * 128 * 384, [[384, 128], [1, 384]]), msem)
    kring = p.sb("kring", [128, 4 * 1024], BF16)
    vring = p.sb("vring", [128, 4 * 1040], BF16)
    qfm = [p.sb("qfm", [128, 8 * 128], BF16) for _ in range(2)]
    qtm = p.sb("qtm", [128, 1024], BF16)
    ktm = p.sb("ktm", [128, 1024], BF16)
    aotm = p.sb("aotm", [128, 1024], BF16)
    aofm = p.sb("aofm", [128, 8 * 128], BF16)
    Pm2 = [p.sb(f"Pm{i}", [128, 384], BF16) for i in range(2)]
    rl = p.sb("rl", [128, 16])
    xlnf = p.sb("xlnf_aw", [128, 8 * 128], BF16)
    vmem = p.op("g", lambda e: e.memset(vring[:, :], 1.0), inc=r.sg)
    kmem = p.op("g", lambda e: e.memset(kring[:, :], 0.0), inc=r.sg)
    if rope:
        crep = [p.sb("crep", [128, 512], BF16) for _ in range(2)]
        srep = [p.sb("srep", [128, 512], BF16) for _ in range(2)]
        rsem = [p.sem(f"aw_cs{i}") for i in range(2)]
        rtmp = p.sb("rtmp", [128, 512])
        rtmp2 = p.sb("rtmp2", [128, 512])
        cs_used = [None, None]
        cs_tok = [None, None]

        def load_cs(i):
            if i >= NT:
                return
            bi = i % 2
            p.wait("y", cs_used[bi])
            p.dma("y", crep[bi][:, :],
                  AP(consts["crep"], i * 128 * 512, [[512, 128], [1, 512]]), rsem[bi])
            cs_tok[bi] = p.dma("y", srep[bi][:, :],
                               AP(consts["srep"], i * 128 * 512, [[512, 128], [1, 512]]),
                               rsem[bi])
        load_cs(0)

    PQ, PK, PV = r.PS01, r.PS23, r.PS45
    def av_slot(h):
        if h < 7:
            return PK[:, h * 65:h * 65 + 65]
        if h < 14:
            return PK[:, 512 + (h - 7) * 65:512 + (h - 7) * 65 + 65]
        return PV[:, (h - 14) * 65:(h - 14) * 65 + 65]

    X1 = [[64, 16], [1, 32]]
    RAW = {id(PQ): (r.PS0123, 0, 2048), id(PK): (r.PS0123, 1024, 2048)}

    def strided(tsr, base, pat):
        if id(tsr) in RAW:
            t, off, w = RAW[id(tsr)]
            return AP(t, off + base, [[w, 128]] + pat)
        return AP(tsr, base, [[tsr.shape[1] if hasattr(tsr, "shape") else 1024, 128]]
                  + pat)

    prev_attn = None
    prev_evac = None
    prev_add = None
    ln_tok = None

    def proj(i):
        nonlocal ln_tok, prev_evac
        ti = i
        lnt = emit_ln(p, r, xio.tile(p, r, ti), ti % 4, xlnf, 0,
                      after=(xio.ready(ti // 4),), war=(prev_attn,), cs=128)
        ln_tok = lnt
        if rope:
            load_cs(i + 1)
        p.wait("t", lnt, wtok, prev_evac, prev_attn, prev_add)
        toks = {}
        for which, ps, c0 in [("q", PQ, 0), ("k", PK, 1024), ("v", PV, 2048)]:
            for nh in range(2):
                for dc in range(8):
                    tk = p.op("t", lambda e, dc=dc, ps=ps, c0=c0, nh=nh: e.matmul(
                        ps[:, nh * 512:(nh + 1) * 512],
                        xlnf[:, dc * 128:(dc + 1) * 128],
                        wqkv[:, dc * 3072 + c0 + nh * 512:dc * 3072 + c0 + (nh + 1) * 512],
                        start=(dc == 0), stop=(dc == 7)),
                        inc=(st if dc == 7 else None))
            toks[which] = tk
        slot = i % 4
        p.wait("v", toks["v"], vmem, prev_attn)
        vdst = AP(vring, slot * 1040, [[4 * 1040, 128], [65, 16], [1, 64]])
        ev = p.op("v", lambda e, d=vdst: e.tensor_copy(d, PV[:, 0:1024]), inc=sv)
        if rope:
            bi = i % 2
            for src_ps, dst in [(PQ, qtm), (PK, ktm)]:
                p.wait("v", toks["q" if src_ps is PQ else "k"], cs_tok[bi], ev)
                q = [None]

                def vv(fn):
                    p.wait("v", q[0])
                    q[0] = p.op("v", fn, inc=sv)
                    return q[0]
                x1 = strided(src_ps, 0, X1)
                x2 = strided(src_ps, 32, X1)
                cc = AP(crep[bi], 0, [[512, 128], [32, 16], [1, 32]])
                ss2 = AP(srep[bi], 0, [[512, 128], [32, 16], [1, 32]])
                o1 = strided(dst, 0, X1)
                o2 = strided(dst, 32, X1)
                vv(lambda e, x1=x1, cc=cc: e.tensor_tensor(rtmp[:, :], x1, cc,
                                                           op=ALU.mult))
                vv(lambda e, x2=x2, ss_=ss2: e.tensor_tensor(rtmp2[:, :], x2, ss_,
                                                            op=ALU.mult))
                vv(lambda e, o1=o1: e.tensor_tensor(o1, rtmp[:, :], rtmp2[:, :],
                                                    op=ALU.subtract))
                vv(lambda e, x1=x1, ss_=ss2: e.tensor_tensor(rtmp[:, :], x1, ss_,
                                                            op=ALU.mult))
                vv(lambda e, x2=x2, cc=cc: e.tensor_tensor(rtmp2[:, :], x2, cc,
                                                           op=ALU.mult))
                ev = vv(lambda e, o2=o2: e.tensor_tensor(o2, rtmp[:, :], rtmp2[:, :],
                                                         op=ALU.add))
            cs_used[bi] = ev
        else:
            p.wait("s", toks["q"], prev_attn)
            e1 = p.op("s", lambda e: e.copy(qtm[:, :], PQ[:, 0:1024]), inc=ss)
            p.wait("s", toks["k"])
            ev2 = p.op("s", lambda e: e.copy(ktm[:, :], PK[:, 0:1024]), inc=ss)
            ev = ev2
        p.wait("t", ev, (ss.tok if not rope else None), r.pt_free)
        tk = None
        for dc in range(8):
            tk = p.op("t", lambda e, dc=dc: e.transpose(
                r.PT[:, dc * 128:(dc + 1) * 128], qtm[:, dc * 128:(dc + 1) * 128],
                r.idn[:, :]), inc=(st if dc == 7 else None))
        p.wait("v", tk, prev_attn)
        cp1 = p.op("v", lambda e, b=i % 2: e.tensor_copy(qfm[b][:, :], r.PT[:, 0:1024]),
                   inc=sv)
        r.pt_free = cp1
        p.wait("t", cp1)
        for dc in range(8):
            tk = p.op("t", lambda e, dc=dc: e.transpose(
                r.PT[:, dc * 128:(dc + 1) * 128], ktm[:, dc * 128:(dc + 1) * 128],
                r.idn[:, :]), inc=(st if dc == 7 else None))
        p.wait("v", tk, kmem, prev_attn)
        cp2 = p.op("v", lambda e, slot=slot: e.tensor_copy(
            kring[:, slot * 1024:(slot + 1) * 1024], r.PT[:, 0:1024]), inc=sv)
        r.pt_free = cp2
        prev_evac = cp2
        return cp1, cp2

    def attn(j, qready, kready):
        nonlocal prev_attn, prev_add
        qb = qfm[j % 2]
        if kind == "l0aa":
            chunks = [j - 1, j, j + 1]
            mv = 0 if j == 0 else (2 if j == NT - 1 else 1)
        else:
            chunks = [j - 2, j - 1, j]
            mv = 0 if j == 0 else (1 if j == 1 else 2)
        SCs = [PQ[:, 0:384], PQ[:, 512:896]]
        deltap = r.PS01
        atok = None
        ptoks = [None, None]
        for k in range(8):
            sctoks = [None, None]
            e2s = [None, None]
            for par in range(2):
                h = 2 * k + par
                rbase = 64 * par
                p.wait("t", qready, kready, mtok, kmem, vmem, prev_add,
                       ptoks[par])
                tk = None
                for c, ch in enumerate(chunks):
                    slot = ch % 4
                    tk = p.op("t", lambda e, c=c, slot=slot, k=k, rb=rbase,
                              sc=SCs[par]: e.matmul(
                        sc[:, c * 128:(c + 1) * 128],
                        kring[rb:rb + 64,
                              slot * 1024 + k * 128:slot * 1024 + (k + 1) * 128],
                        qb[rb:rb + 64, k * 128:(k + 1) * 128],
                        start=True, stop=True), inc=(st if c == 2 else None))
                sctoks[par] = tk
            for par in range(2):
                p.wait("s", sctoks[par], ptoks[par])
                e1 = p.op("s", lambda e, par=par: e.activation(
                    Pm2[par][:, :], SCs[par][:, 0:384], AF.Exp,
                    scale=0.125), inc=ss)
                p.wait("v", e1)
                e2s[par] = p.op("v", lambda e, mv=mv, par=par: e.tensor_tensor(
                    Pm2[par][:, :], Pm2[par][:, :],
                    masks[:, mv * 384:(mv + 1) * 384], op=ALU.mult), inc=sv)
            for par in range(2):
                h = 2 * k + par
                p.wait("t", e2s[par])
                for c, ch in enumerate(chunks):
                    slot = ch % 4
                    atok = p.op("t", lambda e, c=c, slot=slot, h=h, par=par:
                                e.matmul(
                        av_slot(h),
                        Pm2[par][:, c * 128:(c + 1) * 128],
                        vring[:, slot * 1040 + h * 65:slot * 1040 + (h + 1) * 65],
                        start=(c == 0), stop=(c == 2)),
                        inc=(st if c == 2 else None))
                ptoks[par] = atok
        p.wait("v", ptoks[0], ptoks[1])
        q = [None]
        for h in range(16):
            av = av_slot(h)
            p.wait("v", q[0])
            q[0] = p.op("v", lambda e, av=av, h=h: e.reciprocal(rl[:, h:h + 1],
                                                               av[:, 64:65]), inc=sv)
            p.wait("v", q[0])
            q[0] = p.op("v", lambda e, av=av, h=h: e.tensor_scalar(
                aotm[:, h * 64:(h + 1) * 64], av[:, 0:64], rl[:, h:h + 1], None,
                op0=ALU.mult), inc=sv)
        p.wait("t", q[0], r.pt_free)
        tk = None
        for dc in range(8):
            tk = p.op("t", lambda e, dc=dc: e.transpose(
                r.PT[:, dc * 128:(dc + 1) * 128], aotm[:, dc * 128:(dc + 1) * 128],
                r.idn[:, :]), inc=(st if dc == 7 else None))
        p.wait("v", tk)
        cp = p.op("v", lambda e: e.tensor_copy(aofm[:, :], r.PT[:, 0:1024]), inc=sv)
        r.pt_free = cp
        p.wait("t", cp, wtok)
        first = True
        tk = None
        if "bout" in pr:
            p.wait("t", pr["btok"], r.ones_tok)
            for nh in range(2):
                p.op("t", lambda e, nh=nh: e.matmul(
                    deltap[:, nh * 512:(nh + 1) * 512], r.ones_row[0:1, :],
                    pr["bout"][0:1, nh * 512:(nh + 1) * 512], start=True, stop=False))
            first = False
        for dc in range(8):
            for nh in range(2):
                tk = p.op("t", lambda e, dc=dc, nh=nh, fs=(first and dc == 0): e.matmul(
                    deltap[:, nh * 512:(nh + 1) * 512],
                    aofm[:, dc * 128:(dc + 1) * 128],
                    wout[:, dc * 1024 + nh * 512:dc * 1024 + (nh + 1) * 512],
                    start=fs, stop=(dc == 7)), inc=(st if (dc == 7 and nh == 1) else None))
        prev_add = xio.add(p, r, j, deltap[:, 0:1024], wait=(tk,))
        prev_attn = st.tok

    prev_add = None
    qtoks = {}
    xio.prefetch(p, r, 0)
    for i in range(NT):
        xio.prefetch(p, r, i // 4 + 1)
        cp1, cp2 = proj(i)
        qtoks[i % 2] = cp1
        j = i - lag
        if 0 <= j:
            attn(j, qtoks[j % 2], cp2)
            if j % 4 == 3:
                xio.flush(p, r, j // 4)
    if lag == 1:
        attn(NT - 1, qtoks[(NT - 1) % 2], cp2)
        xio.flush(p, r, NT // 4 - 1)
    return


def build_attnw_program(kind):
    nc = bass.Bass("TRN2", target_bir_lowering=False)
    es = ExitStack()
    rope = (kind == "l0aa")
    with es:
        p = Prog(nc, es)
        ins = {
            "x": p.dram("x", [T, D], kind="ExternalInput"),
            "wqkv": p.dram("wqkv", [1024, 3072], kind="ExternalInput"),
            "wout": p.dram("wout", [1024, 1024], kind="ExternalInput"),
            "c_iden": p.dram("c_iden", [128, 128], BF16, kind="ExternalInput"),
            "c_masks": p.dram("c_masks", [3 * 128, 384], BF16, kind="ExternalInput"),
        }
        if rope:
            ins["bout"] = p.dram("bout", [1024], kind="ExternalInput")
            ins["c_crep"] = p.dram("c_crep", [4096, 512], BF16, kind="ExternalInput")
            ins["c_srep"] = p.dram("c_srep", [4096, 512], BF16, kind="ExternalInput")
        y = p.dram("y", [T, D], kind="ExternalOutput")
        scr = {
            "wqkv": p.dram("s_wqkv", [1024, 3072], BF16),
            "wout": p.dram("s_wout", [1024, 1024], BF16),
        }
        r = Res(p, resident_x=False)
        r.load_consts(p, ins["c_iden"].ap())
        pr = prep_attn_w(p, r, ins, scr, has_bias=rope)
        xio = XStream(p, r, ins["x"], y, nbuf=3)
        consts = {"masks": ins["c_masks"]}
        if rope:
            consts["crep"] = ins["c_crep"]
            consts["srep"] = ins["c_srep"]
        emit_attn_window(p, r, scr, pr, xio, kind, consts)
        p.wait("y", *xio.final_toks())
        p.build()
    return nc, p.ninstr


def np_attn_consts(kind):
    import ml_dtypes
    pvec = np.arange(128)[:, None]
    qvec = np.arange(128)[None, :]
    masks = np.zeros((3, 128, 384), np.float32)
    for v in range(3):
        for c in range(3):
            if kind == "l0aa":
                diff = (c - 1) * 128 + pvec - qvec
                ok = (np.abs(diff) <= 128)
                if v == 0:
                    ok = ok & (c >= 1)
                if v == 2:
                    ok = ok & (c <= 1)
            else:
                diff = (c - 2) * 128 + pvec - qvec
                ok = (diff <= 0) & (diff >= -255)
                if v == 0:
                    ok = ok & (c >= 2)
                if v == 1:
                    ok = ok & (c >= 1)
            masks[v, :, c * 128:(c + 1) * 128] = ok
    out = {"c_masks": masks.reshape(3 * 128, 384).astype(ml_dtypes.bfloat16)}
    if kind == "l0aa":
        inv = 1.0 / 10000.0 ** (np.arange(0, 64, 2, dtype=np.float64) / 64)
        t = np.arange(4096, dtype=np.float64)
        fr = np.outer(t, inv)
        out["c_crep"] = np.tile(np.cos(fr).astype(np.float32), (1, 16)).astype(
            ml_dtypes.bfloat16)
        out["c_srep"] = np.tile(np.sin(fr).astype(np.float32), (1, 16)).astype(
            ml_dtypes.bfloat16)
    return out



def prep_attn_full_alloc(p: Prog, kind):
    if kind == "l0ab":
        return {"mrow": p.sb("mrowf", [1, 1024], BF16)}
    pr = {"negc": p.sb("negc", [128, 16])}
    for bn in ["bq", "bk", "bv", "bo"]:
        pr[bn] = p.sb("b_" + bn, [1, 1024], BF16)
    return pr


def prep_attn_full(p: Prog, r: Res, ins, scr, kind, pr=None, standalone=True):
    if pr is None:
        pr = prep_attn_full_alloc(p, kind)
    if kind == "l0ab":
        nw = p.sb("nwrow", [8, 128])
        nsem = p.sem("nws")
        ld = p.dma("y", nw[:, :], AP(ins["norm_w"], 0, [[128, 8], [1, 128]]), nsem)
        nwc = p.sb("nwc", [128, 8])
        p.wait("t", ld, r.idf_tok, r.ptf_free)
        tk = p.op("t", lambda e: e.transpose(r.PTF[:, 0:8], nw[0:8, :],
                                             r.idf[0:8, 0:8]), inc=r.st)
        p.wait("v", tk)
        tw = p.op("v", lambda e: e.tensor_copy(nwc[:, 0:8], r.PTF[:, 0:8]), inc=r.sv)
        r.ptf_free = tw
        lsem = [p.sem(f"fald{i}") for i in range(2)]
        stsem = p.sem("fast")
        es2 = ExitStack() if standalone else None
        wtmp = [p.sb("fpw", [128, 2048], es=es2) for _ in range(2)]
        wob = p.sb("fpo", [128, 2048], BF16, es=es2)
        busers = [None, None]
        out_free = None
        last = None
        for wname, scrn, width in [("Wq", "wq", 1024), ("Wkv", "wkv", 2048)]:
            for c in range(8):
                bi = c % 2
                p.wait("y", busers[bi])
                ld = p.dma("y", wtmp[bi][:, 0:width],
                           AP(ins[wname], c * 128 * width, [[width, 128], [1, width]]),
                           lsem[bi])
                p.wait("v", ld, tw, out_free)
                t3 = p.op("v", lambda e, b=bi, c=c, w=width: e.tensor_scalar(
                    wob[:, 0:w], wtmp[b][:, 0:w], nwc[:, c:c + 1], None,
                    op0=ALU.mult), inc=r.sv)
                busers[bi] = t3
                p.wait("y", t3)
                last = p.dma("y", AP(scr[scrn], c * 128 * width,
                                     [[width, 128], [1, width]]), wob[:, 0:w]
                             if False else wob[:, 0:width], stsem)
                out_free = last
        if standalone:
            es2.close()
            p.barrier([r.sv.tok, r.st.tok, lsem[0].tok, lsem[1].tok,
                       stsem.tok, nsem.tok])
        cd = p.dma("g", AP(scr["wo"], 0, [[1024, 1024], [1, 1024]]),
                   AP(ins["Wo"], 0, [[1024, 1024], [1, 1024]]), r.dmag)
        pr["wdone"] = last
        pr["castdone"] = cd
        mrow = pr["mrow"]
        es3 = ExitStack() if standalone else None
        gq = p.sb("gq", [1, 1024], es=es3)
        gk = p.sb("gk", [1, 1024], es=es3)
        gsem = p.sem("gsm")
        l1 = p.dma("y", gq[:, :], AP(ins["q_gamma"], 0, [[1024, 1], [1, 1024]]), gsem)
        l2 = p.dma("y", gk[:, :], AP(ins["k_gamma"], 0, [[1024, 1], [1, 1024]]), gsem)
        p.wait("v", l2)
        t1 = p.op("v", lambda e: e.tensor_scalar(gq[:, :], gq[:, :], 1.0, None,
                                                 op0=ALU.add), inc=r.sv)
        p.wait("v", t1)
        t2 = p.op("v", lambda e: e.tensor_scalar(gk[:, :], gk[:, :], 1.0, None,
                                                 op0=ALU.add), inc=r.sv)
        p.wait("v", t2)
        t3 = p.op("v", lambda e: e.tensor_tensor(gq[:, :], gq[:, :], gk[:, :],
                                                 op=ALU.mult), inc=r.sv)
        p.wait("v", t3)
        t4 = p.op("v", lambda e: e.tensor_scalar(mrow[:, :], gq[:, :], 64.0, None,
                                                 op0=ALU.mult), inc=r.sv)
        pr["mtok"] = t4
        if standalone:
            p.barrier(p.all_toks())
            es3.close()
    else:
        csem = p.sem("fa_cast")
        bsem = p.sem("fa_brow")
        for wname, scrn in [("Wq", "wq"), ("Wk", "wk"), ("Wv", "wv"), ("Wo", "wo")]:
            cd = p.dma("g", AP(scr[scrn], 0, [[1024, 1024], [1, 1024]]),
                       AP(ins[wname], 0, [[1024, 1024], [1, 1024]]), csem)
        pr["castdone"] = cd
        pr["wdone"] = cd
        for bn in ["bq", "bk", "bv", "bo"]:
            cd = p.dma("g", pr[bn][:, :], AP(ins[bn], 0, [[1024, 1], [1, 1024]]),
                       bsem)
        pr["btok"] = cd
        negc = pr["negc"]
        es3 = ExitStack() if standalone else None
        crow = p.sb("crow", [1, 16], BF16, es=es3)
        bqf = p.sb("bqf", [1, 1024], es=es3)
        bkf = p.sb("bkf", [1, 1024], es=es3)
        gsem = p.sem("gsm2")
        l1 = p.dma("y", bqf[:, :], AP(ins["bq"], 0, [[1024, 1], [1, 1024]]), gsem)
        l2 = p.dma("y", bkf[:, :], AP(ins["bk"], 0, [[1024, 1], [1, 1024]]), gsem)
        p.wait("v", l2)
        t1 = p.op("v", lambda e: e.tensor_tensor(bqf[:, :], bqf[:, :], bkf[:, :],
                                                 op=ALU.mult), inc=r.sv)
        p.wait("v", t1)
        t2 = p.op("v", lambda e: e.reduce_sum(
            bkf[:, 0:16], AP(bqf, 0, [[1024, 1], [64, 16], [1, 64]]), axis=AX.X),
            inc=r.sv)
        p.wait("v", t2)
        t3 = p.op("v", lambda e: e.tensor_scalar(crow[:, 0:16], bkf[:, 0:16],
                                                 -1.0 / 64.0, None, op0=ALU.mult),
                  inc=r.sv)
        p.wait("t", t3, r.ones_tok)
        tk = p.op("t", lambda e: e.matmul(r.PTF[:, 0:16], r.ones_row[0:1, :],
                                          crow[0:1, 0:16], start=True, stop=True),
                  inc=r.st)
        p.wait("v", tk, r.ptf_free)
        t5 = p.op("v", lambda e: e.tensor_copy(negc[:, 0:16], r.PTF[:, 0:16]), inc=r.sv)
        r.ptf_free = t5
        pr["negctok"] = t5
        if standalone:
            p.barrier(p.all_toks())
            es3.close()
    return pr


def emit_attn_full(p: Prog, r: Res, scr, pr, xio, kind):
    sv, ss, st = r.sv, r.ss, r.st
    l0 = (kind == "l0ab")
    xscr = scr["xln"]
    Kfm = p.sb("Kfm", [128, 4 * 4096], BF16)
    vaug = p.sb("vaug", [128, 32 * 520], BF16)
    Qfm = p.sb("Qfm", [128, 4 * 512], BF16)
    wq = p.sb("wq_s", [128, 8 * 512], BF16)
    wk = p.sb("wk_s", [128, 8 * 512], BF16)
    wv = p.sb("wv_s", [128, 8 * 512], BF16)
    wo = p.sb("wo_s", [128, 4 * 1024], BF16)
    xlnf = p.sb("xlnf_fa", [128, 8 * 128], BF16)
    rtmp = p.sb("rtmp_fa", [128, 512])
    rq8 = p.sb("rq8", [128, 8])
    Pbuf2 = p.sb("Pb2", [128, 2048], BF16)
    Pbuf = [Pbuf2[:, 0:1024], Pbuf2[:, 1024:2048]]
    tbuf2 = p.sb("tb2", [128, 2048]) if l0 else None
    kcast = p.sb("kcast", [128, 512], BF16)
    rlrow = p.sb("rlrow", [1, 512])
    fbc = p.sb("fbc", [64, 512])
    aofm = p.sb("aofm", [128, 4 * 512], BF16)
    mbc = p.sb("mbc", [128, 512], BF16) if l0 else None
    normsT = None if l0 else p.sb("normsT", [8, 4096])
    maskT = None if l0 else p.sb("maskT", [8, 4096], BF16)
    bis = None if l0 else p.sb("bis", [8, 8])
    wsem = p.sem("fa_w")
    xlsem = [p.sem(f"fa_xl{i}") for i in range(2)]
    xasem = [p.sem(f"fa_xas{i}") for i in range(2)]
    xbsem = [p.sem(f"fa_xbs{i}") for i in range(2)]
    xa = [p.sb("fa_xa", [128, 1024], BF16) for _ in range(2)]
    xb2 = [p.sb("fa_xb", [128, 8 * 512], BF16) for _ in range(2)]
    vmem_tok = None
    xscr_done = None

    def load_weights(sw):
        tk = None
        p.wait("y", pr["wdone"], pr["castdone"], st.tok)
        if l0:
            for dc in range(8):
                p.dma("y", wq[:, dc * 512:(dc + 1) * 512],
                      AP(scr["wq"], dc * 128 * 1024 + sw * 512,
                         [[1024, 128], [1, 512]]), wsem)
                p.dma("y", wk[:, dc * 512:(dc + 1) * 512],
                      AP(scr["wkv"], dc * 128 * 2048 + sw * 512,
                         [[2048, 128], [1, 512]]), wsem)
                tk = p.dma("y", wv[:, dc * 512:(dc + 1) * 512],
                           AP(scr["wkv"], dc * 128 * 2048 + 1024 + sw * 512,
                              [[2048, 128], [1, 512]]), wsem)
            for pc in range(4):
                tk = p.dma("y", wo[:, pc * 1024:(pc + 1) * 1024],
                           AP(scr["wo"], (sw * 512 + pc * 128) * 1024,
                              [[1024, 128], [1, 1024]]), wsem)
        else:
            for dc in range(8):
                p.dma("y", wq[:, dc * 512:(dc + 1) * 512],
                      scr["wqT"].ap()[dc * 128:(dc + 1) * 128,
                                      sw * 512:(sw + 1) * 512], wsem)
                p.dma("y", wk[:, dc * 512:(dc + 1) * 512],
                      scr["wkT"].ap()[dc * 128:(dc + 1) * 128,
                                      sw * 512:(sw + 1) * 512], wsem)
                tk = p.dma("y", wv[:, dc * 512:(dc + 1) * 512],
                           scr["wvT"].ap()[dc * 128:(dc + 1) * 128,
                                           sw * 512:(sw + 1) * 512], wsem)
            for pc in range(4):
                tk = p.dma("y", wo[:, pc * 1024:(pc + 1) * 1024],
                           scr["woT"].ap()[sw * 512 + pc * 128:
                                           sw * 512 + (pc + 1) * 128, :], wsem)
        return tk

    def proj_tm(ps, wbuf, xslice, xtok, bias_row, extra=()):
        p.wait("t", xtok, (wsem, wsem.n), *extra)
        first = True
        if bias_row is not None:
            p.wait("t", pr["btok"], r.ones_tok)
            p.op("t", lambda e, b=bias_row, ps=ps: e.matmul(
                ps[:, 0:512], r.ones_row[0:1, :], b, start=True, stop=False))
            first = False
        tk = None
        for dc in range(8):
            tk = p.op("t", lambda e, dc=dc, w=wbuf, xs=xslice(dc), ps=ps,
                      fs=(first and dc == 0): e.matmul(
                ps[:, 0:512], xs, w[:, dc * 512:(dc + 1) * 512],
                start=fs, stop=(dc == 7)), inc=(st if dc == 7 else None))
        return tk

    def rms_to_rq8(ps, war):
        p.wait("s", st.tok, war)
        t1 = p.op("s", lambda e, ps=ps: e.activation(rtmp[:, :], ps[:, 0:512],
                                                     AF.Square), inc=ss)
        p.wait("v", t1, war)
        t2 = p.op("v", lambda e: e.reduce_sum(
            rq8[:, 0:8], AP(rtmp, 0, [[512, 128], [64, 8], [1, 64]]), axis=AX.X),
            inc=sv)
        p.wait("v", t2)
        t3 = p.op("v", lambda e: e.tensor_scalar(rq8[:, 0:8], rq8[:, 0:8], 1e-24, None,
                                                 op0=ALU.max), inc=sv)
        p.wait("s", t3)
        t4 = p.op("s", lambda e: e.activation(rq8[:, 0:8], rq8[:, 0:8], AF.Ln), inc=ss)
        p.wait("s", t4)
        t5 = p.op("s", lambda e: e.activation(rq8[:, 0:8], rq8[:, 0:8], AF.Exp,
                                              scale=-0.5), inc=ss)
        return t5

    RQ8B = AP(rq8, 0, [[8, 128], [1, 8], [0, 64]])
    PQ_, PK_, PV_ = r.PS01, r.PS23, r.PS45

    def phase_a(sw):
        nonlocal vmem_tok, xscr_done
        wtok = load_weights(sw)
        p.wait("g", st.tok)
        vm = p.op("g", lambda e: e.memset(vaug[:, :], 1.0), inc=r.sg)
        vmem_tok = vm
        xa_used = [None, None]
        prev_kv = st.tok
        prev_sq = sv.tok
        nrm_prev = sv.tok
        prev_st = None
        for ti in range(NT):
            bi = ti % 2
            if sw == 0:
                xrd = xio.loadA(p, r, ti)
                lnt = emit_ln(p, r, xio.tileA(p, r, ti), ti % 4, xlnf, 0,
                              after=(xrd,), war=(prev_kv, prev_st), cs=128,
                              scale_mode=("rms2" if l0 else None))
                xio.consumedA(p, r, ti, lnt)
                xtok = lnt
                p.wait("y", lnt)
                prev_st = p.dma("y", AP(xscr, ti * 128,
                                        [[4096, 128], [128 * 4096, 8], [1, 128]]),
                                xlnf[:, 0:1024], xlsem[bi])
                xsl = lambda dc: xlnf[:, dc * 128:(dc + 1) * 128]
            else:
                p.wait("y", xa_used[bi], *xscr_done)
                ld = p.dma("y", xa[bi][:, :],
                           AP(xscr, ti * 128, [[4096, 128], [128 * 4096, 8], [1, 128]]),
                           xasem[bi])
                xtok = ld
                xsl = lambda dc, b=bi: xa[b][:, dc * 128:(dc + 1) * 128]
            ktk = proj_tm(PK_, wk, xsl, xtok,
                          None if l0 else pr["bk"][0:1, sw * 512:(sw + 1) * 512],
                          extra=(prev_kv,))
            vtk = proj_tm(PV_, wv, xsl, xtok,
                          None if l0 else pr["bv"][0:1, sw * 512:(sw + 1) * 512])
            if not l0:
                qtk = proj_tm(PQ_, wq, xsl, xtok,
                              pr["bq"][0:1, sw * 512:(sw + 1) * 512])
            if sw == 1:
                xa_used[bi] = vtk if l0 else qtk
            p.wait("v", vtk, vm, prev_kv)
            vdst = AP(vaug, ti * 520, [[32 * 520, 128], [65, 8], [1, 64]])
            ev = p.op("v", lambda e, d=vdst: e.tensor_copy(d, PV_[:, 0:512]), inc=sv)
            if l0:
                t5k = rms_to_rq8(PK_, nrm_prev)
                p.wait("v", t5k, ktk, nrm_prev, ev)
                kn = p.op("v", lambda e: e.tensor_tensor(
                    kcast[:, 0:512], PK_[:, 0:512], RQ8B, op=ALU.mult), inc=sv)
                nrm_prev = kn
                ksrctok = kn
            else:
                t1 = None
                p.wait("s", qtk, prev_sq, st.tok)
                t1 = p.op("s", lambda e: e.activation(rtmp[:, :], PQ_[:, 0:512],
                                                      AF.Square), inc=ss)
                p.wait("v", t1, prev_sq)
                t2 = p.op("v", lambda e: e.reduce_sum(
                    rq8[:, 0:8], AP(rtmp, 0, [[512, 128], [64, 8], [1, 64]]),
                    axis=AX.X), inc=sv)
                p.wait("t", t2, r.ptf_free)
                t3 = p.op("t", lambda e: e.transpose(r.PTF[0:8, 0:128], rq8[:, 0:8],
                                                     r.idf[:, :]), inc=st)
                p.wait("v", t3)
                t4 = p.op("v", lambda e, ti=ti: e.tensor_copy(
                    normsT[0:8, ti * 128:(ti + 1) * 128],
                    r.PTF[0:8, 0:128]), inc=sv)
                r.ptf_free = t4
                prev_sq = t4
                p.wait("v", ktk, nrm_prev, ev)
                kn = p.op("v", lambda e: e.tensor_copy(kcast[:, 0:512],
                                                       PK_[:, 0:512]), inc=sv)
                nrm_prev = kn
                ksrctok = kn
            p.wait("t", ksrctok, r.pt_free)
            tk = None
            for pc in range(4):
                tk = p.op("t", lambda e, pc=pc: e.transpose(
                    r.PT[:, pc * 128:(pc + 1) * 128],
                    kcast[:, pc * 128:(pc + 1) * 128], r.idn[:, :]),
                    inc=(st if pc == 3 else None))
            p.wait("v", tk)
            kdst = AP(Kfm, ti * 128, [[4 * 4096, 128], [4096, 4], [1, 128]])
            cpk = p.op("v", lambda e, d=kdst: e.tensor_copy(d, r.PT[:, 0:512]), inc=sv)
            r.pt_free = cpk
            prev_kv = cpk
        if sw == 0:
            xscr_done = (xlsem[0].tok, xlsem[1].tok)
        return prev_kv

    def bisect(sw):
        q = [None]

        def vstep(fn):
            p.wait("v", q[0])
            q[0] = p.op("v", fn, inc=sv)
            return q[0]
        rows = normsT[0:8, :]
        brow = bis[0:8, :]
        mrows = maskT[0:8, :]
        q[0] = sv.tok
        vstep(lambda e: e.reduce_max(brow[:, 1:2], rows[:, :], axis=AX.X))
        vstep(lambda e: e.tensor_scalar(brow[:, 1:2], brow[:, 1:2], 1.0, None,
                                        op0=ALU.add))
        vstep(lambda e: e.memset(brow[:, 0:1], 0.0))
        for it in range(40):
            vstep(lambda e: e.tensor_tensor(brow[:, 2:3], brow[:, 0:1], brow[:, 1:2],
                                            op=ALU.add))
            vstep(lambda e: e.tensor_scalar(brow[:, 2:3], brow[:, 2:3], 0.5, None,
                                            op0=ALU.mult))
            vstep(lambda e: e.tensor_scalar(mrows[:, :], rows[:, :], brow[:, 2:3],
                                            0.0, op0=ALU.is_ge, op1=ALU.add,
                                            accum_out=brow[:, 3:4]))
            vstep(lambda e: e.tensor_scalar(brow[:, 4:5], brow[:, 3:4], 819.0, None,
                                            op0=ALU.is_ge))
            vstep(lambda e: e.tensor_tensor(brow[:, 5:6], brow[:, 2:3], brow[:, 0:1],
                                            op=ALU.subtract))
            vstep(lambda e: e.tensor_tensor(brow[:, 5:6], brow[:, 5:6], brow[:, 4:5],
                                            op=ALU.mult))
            vstep(lambda e: e.tensor_tensor(brow[:, 0:1], brow[:, 0:1], brow[:, 5:6],
                                            op=ALU.add))
            vstep(lambda e: e.tensor_tensor(brow[:, 5:6], brow[:, 1:2], brow[:, 2:3],
                                            op=ALU.subtract))
            vstep(lambda e: e.tensor_tensor(brow[:, 5:6], brow[:, 5:6], brow[:, 4:5],
                                            op=ALU.mult))
            vstep(lambda e: e.tensor_tensor(brow[:, 1:2], brow[:, 2:3], brow[:, 5:6],
                                            op=ALU.add))
        return vstep(lambda e: e.tensor_scalar(mrows[:, :], rows[:, :], brow[:, 0:1],
                                               None, op0=ALU.is_ge))

    def phase_b(sw, kv_tok, mtok):
        SCe, SCo = r.PS01, r.PS23
        xb_used = [None, None]
        prev_add = None
        prev_dn = sv.tok
        for jq in range(8):
            bi = jq % 2
            p.wait("y", xb_used[bi], *xscr_done)
            ld = p.dma("y", xb2[bi][:, :],
                       AP(xscr, jq * 512, [[4096, 128], [128 * 4096, 8], [1, 512]]),
                       xbsem[bi])
            qtoks = []
            for s4 in range(4):
                qtk = proj_tm(SCe, wq,
                              lambda dc, b=bi, s4=s4: xb2[b][:, dc * 512 + s4 * 128:
                                                             dc * 512 + (s4 + 1) * 128],
                              ld,
                              None if l0 else pr["bq"][0:1, sw * 512:(sw + 1) * 512],
                              extra=(prev_add, prev_dn))
                if l0:
                    t5 = rms_to_rq8(SCe, prev_dn)
                    p.wait("v", t5, qtk, prev_dn, pr["mbctok"])
                    qn = p.op("v", lambda e: e.tensor_tensor(
                        rtmp[:, :], SCe[:, 0:512], RQ8B, op=ALU.mult), inc=sv)
                    p.wait("v", qn)
                    qn2 = p.op("v", lambda e: e.tensor_tensor(
                        Pbuf[1][:, 0:512], rtmp[:, :], mbc[:, :], op=ALU.mult), inc=sv)
                    qrdy = qn2
                else:
                    p.wait("v", qtk, prev_dn)
                    qrdy = p.op("v", lambda e: e.tensor_copy(Pbuf[1][:, 0:512],
                                                             SCe[:, 0:512]), inc=sv)
                p.wait("t", qrdy, r.pt_free)
                tk = None
                for pc in range(4):
                    tk = p.op("t", lambda e, pc=pc: e.transpose(
                        r.PT[:, pc * 128:(pc + 1) * 128],
                        Pbuf[1][:, pc * 128:(pc + 1) * 128], r.idn[:, :]),
                        inc=(st if pc == 3 else None))
                p.wait("v", tk)
                qdst = AP(Qfm, s4 * 128, [[4 * 512, 128], [512, 4], [1, 128]])
                cpq = p.op("v", lambda e, d=qdst: e.tensor_copy(d, r.PT[:, 0:512]),
                           inc=sv)
                r.pt_free = cpq
                prev_dn = cpq
                qtoks.append(cpq)
            qready = qtoks[-1]
            last_av = None
            ptok = [None, None]
            PARS = [(0, None, SCe, 0), (1, None, SCo, 64)]
            for hp in range(4):
                for cg in range(16):
                    sctok = [None, None]
                    a2s = [None, None]
                    for par, _, SC, rb in PARS:
                        p.wait("t", qready, kv_tok, mtok, prev_add, ptok[par])
                        tk = None
                        for cc in range(2):
                            c = cg * 2 + cc
                            tk = p.op("t", lambda e, c=c, cc=cc, SC=SC, rb=rb, hp=hp:
                                      e.matmul(
                                SC[:, cc * 512:(cc + 1) * 512],
                                Kfm[rb:rb + 64, hp * 4096 + c * 128:
                                    hp * 4096 + (c + 1) * 128],
                                Qfm[rb:rb + 64, hp * 512:(hp + 1) * 512],
                                start=True, stop=True), inc=(st if cc == 1 else None))
                        sctok[par] = tk
                    if l0:
                        a1s = [None, None]
                        for par, _, SC, rb in PARS:
                            p.wait("s", sctok[par], ptok[par])
                            a1s[par] = p.op("s", lambda e, SC=SC, par=par:
                                            e.activation(
                                tbuf2[:, par * 1024:(par + 1) * 1024],
                                SC[:, 0:1024], AF.Tanh, scale=0.02), inc=ss)
                        for par, _, SC, rb in PARS:
                            p.wait("s", a1s[par])
                            a2s[par] = p.op("s", lambda e, par=par: e.activation(
                                Pbuf[par][:, 0:1024],
                                tbuf2[:, par * 1024:(par + 1) * 1024], AF.Exp,
                                scale=6.25), inc=ss)
                    else:
                        for par, _, SC, rb in PARS:
                            h = 2 * hp + par
                            p.wait("s", sctok[par], ptok[par])
                            a2s[par] = p.op("s", lambda e, SC=SC, h=h, sw=sw, par=par:
                                            e.activation(
                                Pbuf[par][:, 0:1024], SC[:, 0:1024], AF.Exp,
                                scale=0.125,
                                bias=pr["negc"][:, sw * 8 + h:sw * 8 + h + 1]),
                                inc=ss)
                    for par, _, SC, rb in PARS:
                        h = 2 * hp + par
                        p.wait("t", a2s[par], vmem_tok)
                        cb = par * 512
                        for cc in range(2):
                            c = cg * 2 + cc
                            tk2 = p.op("t", lambda e, c=c, cc=cc, cb=cb, h=h, par=par:
                                       e.matmul(
                                r.PS45[0:65, cb:cb + 512],
                                vaug[:, c * 520 + h * 65:c * 520 + (h + 1) * 65],
                                Pbuf[par][:, cc * 512:(cc + 1) * 512],
                                start=(c == 0), stop=(c == 31)),
                                inc=(st if (c == 31) else None))
                        ptok[par] = tk2
                for par, h in [(0, 2 * hp), (1, 2 * hp + 1)]:
                    cb = par * 512
                    tk2 = ptok[par]
                    p.wait("v", tk2)
                    q1 = p.op("v", lambda e, cb=cb: e.reciprocal(
                        rlrow[0:1, :], r.PS45[64:65, cb:cb + 512]), inc=sv)
                    if not l0:
                        p.wait("t", mtok, r.ptf_free, r.consts_tok)
                        ms = p.op("t", lambda e, h=h, jq=jq: e.matmul(
                            r.PTF[0:1, 0:512], r.idn[0:8, h:h + 1],
                            maskT[0:8, jq * 512:(jq + 1) * 512],
                            start=True, stop=True), inc=st)
                        p.wait("v", q1, ms)
                        q1 = p.op("v", lambda e: e.tensor_tensor(
                            rlrow[0:1, :], rlrow[0:1, :], r.PTF[0:1, 0:512],
                            op=ALU.mult), inc=sv)
                        r.ptf_free = q1
                    p.wait("t", q1, r.ptf_free, r.ones_tok)
                    q2 = p.op("t", lambda e: e.matmul(r.PTF[0:64, 0:512],
                                                      r.ones_f[0:1, 0:64],
                                                      rlrow[0:1, 0:512],
                                                      start=True, stop=True), inc=st)
                    p.wait("v", q2)
                    q3 = p.op("v", lambda e: e.tensor_copy(fbc[:, :],
                                                           r.PTF[0:64, 0:512]),
                              inc=sv)
                    r.ptf_free = q3
                    p.wait("v", q3)
                    q4 = p.op("v", lambda e, cb=cb, hp=hp, par=par: e.tensor_tensor(
                        aofm[64 * par:64 * par + 64, hp * 512:(hp + 1) * 512],
                        r.PS45[0:64, cb:cb + 512], fbc[:, :], op=ALU.mult), inc=sv)
                    ptok[par] = q4
                    last_av = q4
            for s4 in range(4):
                deltap = SCe
                p.wait("t", last_av, (wsem, wsem.n), prev_add)
                first = True
                tk = None
                if (not l0) and sw == 0:
                    p.wait("t", pr["btok"], r.ones_tok)
                    for nh in range(2):
                        p.op("t", lambda e, nh=nh, deltap=deltap: e.matmul(
                            deltap[:, nh * 512:(nh + 1) * 512], r.ones_row[0:1, :],
                            pr["bo"][0:1, nh * 512:(nh + 1) * 512],
                            start=True, stop=False))
                    first = False
                for pc in range(4):
                    for nh in range(2):
                        tk = p.op("t", lambda e, pc=pc, nh=nh, s4=s4, deltap=deltap,
                                  fs=(first and pc == 0): e.matmul(
                            deltap[:, nh * 512:(nh + 1) * 512],
                            aofm[:, pc * 512 + s4 * 128:pc * 512 + (s4 + 1) * 128],
                            wo[:, pc * 1024 + nh * 512:pc * 1024 + (nh + 1) * 512],
                            start=fs, stop=(pc == 3)),
                            inc=(st if (pc == 3 and nh == 1) else None))
                prev_add = xio.add(p, r, jq * 4 + s4, deltap[:, 0:1024], wait=(tk,))
            xb_used[bi] = st.tok
        return prev_add

    mb2 = None
    for sw in range(2):
        if l0:
            p.wait("t", pr["mtok"], r.ptf_free, r.ones_tok)
            if mb2 is not None:
                p.wait("t", mb2)
            mb = p.op("t", lambda e, sw=sw: e.matmul(
                r.PTF[:, 0:512], r.ones_row[0:1, :],
                pr["mrow"][0:1, sw * 512:(sw + 1) * 512], start=True, stop=True),
                inc=st)
            p.wait("v", mb, sv.tok)
            mb2 = p.op("v", lambda e: e.tensor_copy(mbc[:, :], r.PTF[:, 0:512]),
                       inc=sv)
            r.ptf_free = mb2
            pr["mbctok"] = mb2
        kv = phase_a(sw)
        mtok = None
        if not l0:
            mtok = bisect(sw)
        phase_b(sw, kv, mtok)
        xio.new_pass()


def build_attnf_program(kind):
    nc = bass.Bass("TRN2", target_bir_lowering=False)
    es = ExitStack()
    l0 = (kind == "l0ab")
    with es:
        p = Prog(nc, es)
        ins = {"x": p.dram("x", [T, D], kind="ExternalInput"),
               "c_iden": p.dram("c_iden", [128, 128], BF16, kind="ExternalInput")}
        if l0:
            ins["norm_w"] = p.dram("norm_w", [1024], kind="ExternalInput")
            ins["Wq"] = p.dram("Wq", [1024, 1024], kind="ExternalInput")
            ins["Wkv"] = p.dram("Wkv", [1024, 2048], kind="ExternalInput")
            ins["Wo"] = p.dram("Wo", [1024, 1024], kind="ExternalInput")
            ins["q_gamma"] = p.dram("q_gamma", [1024], kind="ExternalInput")
            ins["k_gamma"] = p.dram("k_gamma", [1024], kind="ExternalInput")
            scr = {"wq": p.dram("s_wq", [1024, 1024], BF16),
                   "wkv": p.dram("s_wkv", [1024, 2048], BF16),
                   "wo": p.dram("s_wo", [1024, 1024], BF16)}
        else:
            for n in ["Wq", "Wk", "Wv", "Wo"]:
                ins[n] = p.dram(n, [1024, 1024], kind="ExternalInput")
            for n in ["bq", "bk", "bv", "bo"]:
                ins[n] = p.dram(n, [1024], kind="ExternalInput")
            scr = {w: p.dram("s_" + w, [1024, 1024], BF16)
                   for w in ["wq", "wk", "wv", "wo"]}
            for w in ["wq", "wk", "wv", "wo"]:
                scr[w + "T"] = p.dram("s_" + w + "T", [1024, 1024], BF16)
        scr["xln"] = p.dram("s_xlnf", [8 * 128 * 4096], BF16)
        y = p.dram("y", [T, D], kind="ExternalOutput")
        r = Res(p, resident_x=False)
        r.load_consts(p, ins["c_iden"].ap())
        pr = prep_attn_full(p, r, ins, scr, kind)
        if not l0:
            prep_transpose(p, r, [(scr[w], 1024, 1024, scr[w + "T"])
                                  for w in ["wq", "wk", "wv", "wo"]])
        xio = XPhased(p, r, ins["x"], y)
        emit_attn_full(p, r, scr, pr, xio, kind)
        p.wait("y", *xio.final_toks())
        p.build()
    return nc, p.ninstr



def build_fused_program():
    nc = bass.Bass("TRN2", target_bir_lowering=False)
    es = ExitStack()
    with es:
        p = Prog(nc, es)

        def EI(name, shape, dt=F32):
            return p.dram(name, shape, dt, kind="ExternalInput")

        I = {
            "x": EI("x", [T, D]),
            "aa_wqkv": EI("aa_wqkv", [1024, 3072]),
            "aa_wout": EI("aa_wout", [1024, 1024]),
            "aa_bout": EI("aa_bout", [1024]),
            "m_gate_w": EI("m_gate_w", [15, 1024]),
            "m_fc1_w": EI("m_fc1_w", [15 * 2048, 1024]),
            "m_fc1_b": EI("m_fc1_b", [15 * 2048]),
            "m_geglu": EI("m_geglu", [15 * 1024]),
            "m_fc2_w": EI("m_fc2_w", [15 * 1024, 1024]),
            "m_fc2_b": EI("m_fc2_b", [15 * 1024]),
            "m_sfc1_w": EI("m_sfc1_w", [4096, 1024]),
            "m_sfc1_b": EI("m_sfc1_b", [4096]),
            "m_sgeglu": EI("m_sgeglu", [2048]),
            "m_sfc2_w": EI("m_sfc2_w", [1024, 2048]),
            "m_sfc2_b": EI("m_sfc2_b", [1024]),
            "ab_norm_w": EI("ab_norm_w", [1024]),
            "ab_Wq": EI("ab_Wq", [1024, 1024]),
            "ab_Wkv": EI("ab_Wkv", [1024, 2048]),
            "ab_Wo": EI("ab_Wo", [1024, 1024]),
            "ab_qg": EI("ab_qg", [1024]),
            "ab_kg": EI("ab_kg", [1024]),
            "ba_Wq": EI("ba_Wq", [1024, 1024]),
            "ba_bq": EI("ba_bq", [1024]),
            "ba_Wk": EI("ba_Wk", [1024, 1024]),
            "ba_bk": EI("ba_bk", [1024]),
            "ba_Wv": EI("ba_Wv", [1024, 1024]),
            "ba_bv": EI("ba_bv", [1024]),
            "ba_Wo": EI("ba_Wo", [1024, 1024]),
            "ba_bo": EI("ba_bo", [1024]),
            "bb_wqkv": EI("bb_wqkv", [1024, 3072]),
            "bb_wout": EI("bb_wout", [1024, 1024]),
            "f_w1": EI("f_w1", [4096, 1024]),
            "f_b1": EI("f_b1", [4096]),
            "f_w2": EI("f_w2", [1024, 4096]),
            "f_b2": EI("f_b2", [1024]),
            "c_iden": EI("c_iden", [128, 128], BF16),
            "c_masks_aa": EI("c_masks_aa", [3 * 128, 384], BF16),
            "c_crep": EI("c_crep", [4096, 512], BF16),
            "c_srep": EI("c_srep", [4096, 512], BF16),
            "c_masks_ab": EI("c_masks_ab", [3 * 128, 384], BF16),
        }
        y = p.dram("y", [T, D], kind="ExternalOutput")
        xbuf = p.dram("xbuf", [T, D])
        scr_aa = {"wqkv": p.dram("s_aa_wqkv", [1024, 3072], BF16),
                  "wout": p.dram("s_aa_wout", [1024, 1024], BF16)}
        scr_moe = {"fc1": p.dram("s_m_fc1", [15 * 2048, 1024], BF16),
                   "fc2": p.dram("s_m_fc2", [15 * 1024, 1024], BF16),
                   "s_fc1": p.dram("s_m_sfc1", [4096, 1024], BF16),
                   "s_fc2": p.dram("s_m_sfc2", [1024, 2048], BF16),
                   "xln": p.dram("s_m_xln", [8 * 128 * 2048], BF16)}
        scr_ab = {"wq": p.dram("s_ab_wq", [1024, 1024], BF16),
                  "wkv": p.dram("s_ab_wkv", [1024, 2048], BF16),
                  "wo": p.dram("s_ab_wo", [1024, 1024], BF16),
                  "xln": p.dram("s_ab_xln", [8 * 128 * 4096], BF16)}
        scr_ba = {w: p.dram("s_ba_" + w, [1024, 1024], BF16)
                  for w in ["wq", "wk", "wv", "wo"]}
        scr_ba["xln"] = p.dram("s_ba_xln", [8 * 128 * 4096], BF16)
        scr_bb = {"wqkv": p.dram("s_bb_wqkv", [1024, 3072], BF16),
                  "wout": p.dram("s_bb_wout", [1024, 1024], BF16)}
        scr_f = {"bw1": p.dram("s_f_bw1", [4096, 1024], BF16),
                 "bw2": p.dram("s_f_bw2", [1024, 4096], BF16)}

        r = Res(p)
        r.load_consts(p, I["c_iden"].ap())

        def SCOPE(fn):
            es2 = ExitStack()
            old = p.es
            p.es = es2
            fn()
            p.barrier(p.all_toks(), strict=True)
            p.es = old
            es2.close()

        pr_moe = prep_moe_alloc(p)
        pr_ab = prep_attn_full_alloc(p, "l0ab")
        pr_ba = prep_attn_full_alloc(p, "l1aa")
        pr_f = prep_l1ffn_alloc(p)
        scr_moe["fc1T"] = p.dram("s_m_fc1T", [1024, 15 * 2048], BF16)
        scr_moe["fc2T"] = p.dram("s_m_fc2T", [1024, 15 * 1024], BF16)
        scr_moe["s_fc1T"] = p.dram("s_m_sfc1T", [1024, 4096], BF16)
        scr_moe["s_fc2T"] = p.dram("s_m_sfc2T", [2048, 1024], BF16)
        for w in ["wq", "wk", "wv", "wo"]:
            scr_ba[w + "T"] = p.dram("s_ba_" + w + "T", [1024, 1024], BF16)
        scr_f["bw1T"] = p.dram("s_f_bw1T", [1024, 4096], BF16)
        scr_f["bw2T"] = p.dram("s_f_bw2T", [4096, 1024], BF16)
        moe_ins = {"gate_w": I["m_gate_w"], "fc1_w": I["m_fc1_w"],
                   "fc1_b": I["m_fc1_b"], "geglu_mult": I["m_geglu"],
                   "fc2_w": I["m_fc2_w"], "fc2_b": I["m_fc2_b"],
                   "s_fc1_w": I["m_sfc1_w"], "s_fc1_b": I["m_sfc1_b"],
                   "s_geglu_mult": I["m_sgeglu"], "s_fc2_w": I["m_sfc2_w"],
                   "s_fc2_b": I["m_sfc2_b"]}

        pr_aa = prep_attn_w(p, r, {"wqkv": I["aa_wqkv"], "wout": I["aa_wout"],
                                   "bout": I["aa_bout"]}, scr_aa, has_bias=True)

        def s_l0aa():
            xio = XStream(p, r, I["x"], xbuf, nbuf=3)
            emit_attn_window(p, r, scr_aa, pr_aa, xio, "l0aa",
                             {"masks": I["c_masks_aa"], "crep": I["c_crep"],
                              "srep": I["c_srep"]})
        SCOPE(s_l0aa)

        def s_prep_moe():
            prep_moe(p, r, moe_ins, scr_moe, pr=pr_moe, standalone=False)
            prep_transpose(p, r, [
                (scr_moe["fc1"], 15 * 2048, 1024, scr_moe["fc1T"],
                 pr_moe["jtok"]["fc1"]),
                (scr_moe["s_fc1"], 4096, 1024, scr_moe["s_fc1T"],
                 pr_moe["jtok"]["s_fc1"]),
                (scr_moe["fc2"], 15 * 1024, 1024, scr_moe["fc2T"],
                 pr_moe["jtok"]["fc2"]),
                (scr_moe["s_fc2"], 1024, 2048, scr_moe["s_fc2T"],
                 pr_moe["jtok"]["s_fc2"]),
            ], queues=("s",), standalone=False)
        SCOPE(s_prep_moe)

        def s_moe(half):
            def fn():
                xio = XResident(p, r, x_ap=xbuf, y_ap=xbuf,
                                off=half * 2048 * 1024, ntiles=16)
                xio.load_all(p, r)
                emit_moe(p, r, scr_moe, pr_moe, xio, list(range(16)))
                xio.store_all(p, r, waits=(r.sv.tok, r.ss.tok, r.st.tok))
            return fn

        SCOPE(s_moe(0))
        SCOPE(s_moe(1))

        prep_attn_full(p, r, {"norm_w": I["ab_norm_w"], "Wq": I["ab_Wq"],
                              "Wkv": I["ab_Wkv"], "Wo": I["ab_Wo"],
                              "q_gamma": I["ab_qg"], "k_gamma": I["ab_kg"]},
                       scr_ab, "l0ab", pr=pr_ab)

        def s_l0ab():
            xio = XPhased(p, r, xbuf, xbuf)
            emit_attn_full(p, r, scr_ab, pr_ab, xio, "l0ab")
        SCOPE(s_l0ab)

        SCOPE(s_moe(0))
        SCOPE(s_moe(1))

        prep_attn_full(p, r, {"Wq": I["ba_Wq"], "bq": I["ba_bq"],
                              "Wk": I["ba_Wk"], "bk": I["ba_bk"],
                              "Wv": I["ba_Wv"], "bv": I["ba_bv"],
                              "Wo": I["ba_Wo"], "bo": I["ba_bo"]},
                       scr_ba, "l1aa", pr=pr_ba)
        prep_transpose(p, r, [(scr_ba[w], 1024, 1024, scr_ba[w + "T"])
                              for w in ["wq", "wk", "wv", "wo"]])

        def s_l1aa():
            xio = XPhased(p, r, xbuf, xbuf)
            emit_attn_full(p, r, scr_ba, pr_ba, xio, "l1aa")
        SCOPE(s_l1aa)

        prep_l1ffn(p, r, {"w1": I["f_w1"], "b1": I["f_b1"],
                          "w2": I["f_w2"], "b2": I["f_b2"]}, scr_f, pr=pr_f)
        prep_transpose(p, r, [(scr_f["bw1"], 4096, 1024, scr_f["bw1T"]),
                              (scr_f["bw2"], 1024, 4096, scr_f["bw2T"])])

        def s_ffn(dst):
            def fn():
                xio = XStream(p, r, xbuf, dst, nbuf=2)
                bufs = mk_l1ffn_bufs(p)
                emit_l1ffn(p, r, bufs, scr_f, pr_f, xio)
            return fn
        SCOPE(s_ffn(xbuf))

        pr_bb = prep_attn_w(p, r, {"wqkv": I["bb_wqkv"], "wout": I["bb_wout"]},
                            scr_bb, has_bias=False)

        def s_l1ab():
            xio = XStream(p, r, xbuf, xbuf, nbuf=3)
            emit_attn_window(p, r, scr_bb, pr_bb, xio, "l1ab",
                             {"masks": I["c_masks_ab"]})
        SCOPE(s_l1ab)

        SCOPE(s_ffn(y))
        p.build()
    return nc, p.ninstr, len(p._sems)



_PROGS = {}


def _prog(kind):
    if kind not in _PROGS:
        if kind == "moe":
            _PROGS[kind] = build_moe_program(0)[0]
        elif kind in ("l0aa", "l1ab"):
            _PROGS[kind] = build_attnw_program(kind)[0]
        elif kind in ("l0ab", "l1aa"):
            _PROGS[kind] = build_attnf_program(kind)[0]
        elif kind == "l1ffn":
            _PROGS[kind] = build_l1ffn_program()[0]
    return _PROGS[kind]


def kernel(**ins):
    x = np.ascontiguousarray(ins["L0_attn_a_x"].astype(np.float32))
    B = x.shape[0]
    consts = np_consts()
    cw_aa = np_attn_consts("l0aa")
    cw_ab = np_attn_consts("l1ab")
    if "fused" not in _PROGS:
        _PROGS["fused"] = build_fused_program()[0]
    nc = _PROGS["fused"]
    base = {
        "aa_wqkv": ins["L0_attn_a_Wqkv"], "aa_wout": ins["L0_attn_a_Wout"],
        "aa_bout": ins["L0_attn_a_bout"],
        "m_gate_w": ins["L0_ffn_gate_w"],
        "m_fc1_w": np.ascontiguousarray(ins["L0_ffn_fc1_w"].reshape(-1, 1024)),
        "m_fc1_b": np.ascontiguousarray(ins["L0_ffn_fc1_b"].reshape(-1)),
        "m_geglu": np.ascontiguousarray(ins["L0_ffn_geglu_mult"].reshape(-1)),
        "m_fc2_w": np.ascontiguousarray(ins["L0_ffn_fc2_w"].reshape(-1, 1024)),
        "m_fc2_b": np.ascontiguousarray(ins["L0_ffn_fc2_b"].reshape(-1)),
        "m_sfc1_w": ins["L0_ffn_s_fc1_w"], "m_sfc1_b": ins["L0_ffn_s_fc1_b"],
        "m_sgeglu": ins["L0_ffn_s_geglu_mult"],
        "m_sfc2_w": ins["L0_ffn_s_fc2_w"], "m_sfc2_b": ins["L0_ffn_s_fc2_b"],
        "ab_norm_w": ins["L0_attn_b_norm_w"], "ab_Wq": ins["L0_attn_b_Wq"],
        "ab_Wkv": ins["L0_attn_b_Wkv"], "ab_Wo": ins["L0_attn_b_Wo"],
        "ab_qg": np.ascontiguousarray(ins["L0_attn_b_q_gamma"].reshape(-1)),
        "ab_kg": np.ascontiguousarray(ins["L0_attn_b_k_gamma"].reshape(-1)),
        "ba_Wq": ins["L1_attn_a_Wq"], "ba_bq": ins["L1_attn_a_bq"],
        "ba_Wk": ins["L1_attn_a_Wk"], "ba_bk": ins["L1_attn_a_bk"],
        "ba_Wv": ins["L1_attn_a_Wv"], "ba_bv": ins["L1_attn_a_bv"],
        "ba_Wo": ins["L1_attn_a_Wo"], "ba_bo": ins["L1_attn_a_bo"],
        "bb_wqkv": ins["L1_attn_b_w_qkv"], "bb_wout": ins["L1_attn_b_w_out"],
        "f_w1": ins["L1_ffn_w1"], "f_b1": ins["L1_ffn_b1"],
        "f_w2": ins["L1_ffn_w2"], "f_b2": ins["L1_ffn_b2"],
        "c_iden": consts["c_iden"],
        "c_masks_aa": cw_aa["c_masks"], "c_crep": cw_aa["c_crep"],
        "c_srep": cw_aa["c_srep"],
        "c_masks_ab": cw_ab["c_masks"],
    }
    maps = []
    for b in range(B):
        m = dict(base)
        m["x"] = np.ascontiguousarray(x[b])
        maps.append(m)
    res = run_bass_kernel_spmd(nc, maps, core_ids=list(range(B)))
    return np.stack([res.results[b]["y"] for b in range(B)], 0)
